# version 06
# baseline (speedup 1.0000x reference)
import numpy as np
import ml_dtypes
from contextlib import ExitStack

import concourse.bass as bass
import concourse.mybir as mybir
from concourse.bass_utils import run_bass_kernel_spmd

F32 = mybir.dt.float32
BF16 = mybir.dt.bfloat16
AF = mybir.ActivationFunctionType
ALU = mybir.AluOpType
AX = mybir.AxisListType

BF = ml_dtypes.bfloat16

D = 1024
NCORES = 8


ENGS = ("tensor", "scalar", "vector", "sync", "gpsimd")


class _Op:
    __slots__ = ("eng", "fn", "deps", "inc", "gseq", "eseq", "promoted", "val")

    def __init__(self, eng, fn, deps, inc):
        self.eng = eng
        self.fn = fn
        self.deps = deps
        self.inc = inc
        self.promoted = inc == 16
        self.val = None


class Plan:
    def __init__(self, nc, es):
        self.nc = nc
        self.ops = {e: [] for e in ENGS}
        self.all = []
        self.sems = {e: es.enter_context(nc.semaphore(f"sem_{e}")) for e in ENGS}

    def add(self, eng, fn, deps=(), dma=False):
        op = _Op(eng, fn, [d for d in deps if d is not None], 16 if dma else 1)
        if not dma:
            op.promoted = False
        op.gseq = len(self.all)
        op.eseq = len(self.ops[eng])
        self.all.append(op)
        self.ops[eng].append(op)
        for d in op.deps:
            assert d.gseq < op.gseq
            if d.eng == op.eng and op.eseq - d.eseq >= 2:
                continue
            d.promoted = True
        return op

    def t(self, fn, deps=()):
        return self.add("tensor", fn, deps)

    def s(self, fn, deps=()):
        return self.add("scalar", fn, deps)

    def v(self, fn, deps=()):
        return self.add("vector", fn, deps)

    def sy(self, fn, deps=(), dma=True):
        return self.add("sync", fn, deps, dma=dma)

    def g(self, fn, deps=(), dma=True):
        return self.add("gpsimd", fn, deps, dma=dma)

    def finalize_and_emit(self, block):
        for e in ENGS:
            cum = 0
            for op in self.ops[e]:
                if op.promoted:
                    cum += op.inc
                    op.val = cum
        plan = self

        def engine_body(eng):
            def body(handle):
                watermark = {e: 0 for e in ENGS}
                for op in plan.ops[eng]:
                    for d in op.deps:
                        if d.eng == eng and op.eseq - d.eseq >= 2:
                            continue
                        if d.val is None:
                            raise AssertionError("dep not promoted")
                        if d.val > watermark[d.eng]:
                            handle.wait_ge(plan.sems[d.eng], d.val)
                            watermark[d.eng] = d.val
                    ins = op.fn(handle)
                    if op.promoted:
                        ins.then_inc(plan.sems[eng], op.inc)
            return body

        block.tensor(engine_body("tensor"))
        block.scalar(engine_body("scalar"))
        block.vector(engine_body("vector"))
        block.sync(engine_body("sync"))
        block.gpsimd(engine_body("gpsimd"))



def emit_ln(p, ts_, NT, x_sb, xlnT, deps_x):
    nc = p.nc
    stats = ts_["stats"]
    sq = ts_["sq"]
    xb = ts_["xln_scr"]
    tr_ps = ts_["tr_ps"]
    ident = ts_["ident"]

    def st(t, j):
        return stats[:, t, j:j + 1]

    r_sum, r_sq = [], []
    for t in range(NT):
        r_sum.append(p.v(lambda v, t=t: v.reduce_sum(st(t, 0), x_sb[:, t, :], axis=AX.X, negate=True), deps_x))
        r_sq.append(p.s(lambda s, t=t: s.activation(sq[:], x_sb[:, t, :], AF.Square, accum_out=st(t, 1)), deps_x))
    mean_ops = [p.v(lambda v, t=t: v.tensor_scalar(st(t, 2), st(t, 0), 1.0 / D, None, ALU.mult), [r_sum[t]]) for t in range(NT)]
    m2_ops = [p.v(lambda v, t=t: v.tensor_tensor(st(t, 3), st(t, 2), st(t, 2), ALU.mult), [mean_ops[t]]) for t in range(NT)]
    var_ops = [p.v(lambda v, t=t: v.tensor_scalar(st(t, 4), st(t, 1), 1.0 / D, 1e-5, ALU.mult, ALU.add), [r_sq[t]]) for t in range(NT)]
    var2_ops = [p.v(lambda v, t=t: v.tensor_tensor(st(t, 4), st(t, 4), st(t, 3), ALU.subtract), [var_ops[t], m2_ops[t]]) for t in range(NT)]
    lnv_ops = [p.s(lambda s, t=t: s.activation(st(t, 5), st(t, 4), AF.Ln), [var2_ops[t]]) for t in range(NT)]
    rstd_ops = [p.s(lambda s, t=t: s.activation(st(t, 6), st(t, 5), AF.Exp, scale=-0.5), [lnv_ops[t]]) for t in range(NT)]

    evac_ops = []
    xln_ops = []
    tr_ops_prev = None
    evac_prev2 = [None, None]
    for t in range(NT):
        xop = p.v(lambda v, t=t: v.tensor_scalar(
            xb[t % 2][:], x_sb[:, t, :], st(t, 2), st(t, 6), ALU.add, ALU.mult),
            [rstd_ops[t], evac_prev2[t % 2]])
        xln_ops.append(xop)
        trs = []
        for dc in range(8):
            trs.append(p.t(lambda t_, t=t, dc=dc: t_.transpose(
                tr_ps[:, dc, :], xb[t % 2][:, dc * 128:(dc + 1) * 128], ident[:]),
                [xop] + ([tr_ops_prev] if dc == 0 else [])))
        ev = p.v(lambda v, t=t: v.tensor_copy(xlnT[:, :, t * 128:(t + 1) * 128], tr_ps[:, :, :]), [trs[-1]])
        evac_ops.append(ev)
        evac_prev2[t % 2] = ev
        tr_ops_prev = ev
    return evac_ops, xln_ops


def emit_router_l0(p, ts_, NT, xlnT, rwb, evac_ops, rw_dep):
    logits_ps = ts_["logits_ps"]
    stats = ts_["stats"]
    coefs = ts_["coefs"]
    exps_scr = ts_["exps_scr"]

    def st(t, j):
        return stats[:, t, j:j + 1]

    mm_ops = []
    eq_prev = None
    exps_prev = None
    for t in range(NT):
        mms = []
        for dc in range(8):
            mms.append(p.t(lambda t_, t=t, dc=dc: t_.matmul(
                logits_ps[:, 0:8], xlnT[:, dc, t * 128:(t + 1) * 128], rwb[:, dc, 0:8],
                start=(dc == 0), stop=(dc == 7)),
                ([evac_ops[t], rw_dep, eq_prev, exps_prev] if dc == 0 else [])))
        mm = mms[-1]
        nm1 = p.v(lambda v, t=t: v.reduce_max(st(t, 7), logits_ps[:, 0:8], axis=AX.X, negate=True), [mm])
        ex = p.s(lambda s, t=t: s.activation(exps_scr[:], logits_ps[:, 0:8], AF.Exp, bias=st(t, 7), accum_out=st(t, 0)), [nm1, mm])
        t1 = p.v(lambda v, t=t: v.reciprocal(st(t, 1), st(t, 0)), [ex])
        eq = p.v(lambda v, t=t: v.tensor_scalar(coefs[:, t, :], logits_ps[:, 0:8], st(t, 7), 0.0, ALU.add, ALU.is_equal), [nm1, mm])
        cf = p.v(lambda v, t=t: v.tensor_scalar(coefs[:, t, :], coefs[:, t, :], st(t, 1), None, ALU.mult), [t1, eq])
        eq_prev = cf
        exps_prev = ex
        mm_ops.append(cf)
    return mm_ops


def emit_moe(p, ts_, NT, T, xlnT, x_sb, coefs, wsrc, ln_done, router_done,
             H, n_routed, shared_coef, silu_on_gate):
    w_up = ts_["w_up"]; w_gate = ts_["w_gate"]; w_down = ts_["w_down"]
    hidT = ts_["hidT"]
    silu_sb = ts_["silu_sb"]
    g_ps = ts_["g_ps"]; u_ps = ts_["u_ps"]
    d_ps = ts_["d_ps"]
    tslices = [(s, min(512, T - s)) for s in range(0, T, 512)]
    n_exp = n_routed + (1 if shared_coef is not None else 0)

    wload = {}
    buf_last_reader = [None, None]
    seq = [(e, h8) for e in range(n_exp) for h8 in range(H // 512)]

    def load_w(i):
        e, h8 = seq[i]
        b = i % 2
        up_ap, gate_ap, down_ap = wsrc(e, h8)
        dep = [buf_last_reader[b]]
        o1 = p.g(lambda g, b=b, ap=up_ap: g.dma_start(w_up[b][:], ap), dep)
        o2 = p.g(lambda g, b=b, ap=gate_ap: g.dma_start(w_gate[b][:], ap), dep)
        o3 = p.g(lambda g, b=b, ap=down_ap: g.dma_start(w_down[b][:], ap), dep)
        wload[i] = (o1, o2, o3)

    load_w(0)
    load_w(1)

    silu_last = [None, None]
    g_last_reader = [None, None]
    u_last_reader = [None, None]
    d_last_reader = [None, None]
    last_x_op = None

    for i, (e, h8) in enumerate(seq):
        b = i % 2
        dW = list(wload[i])
        hid_ops = []
        tpairs = [tslices[i:i + 2] for i in range(0, len(tslices), 2)]
        for hc in range(4):
            for tp in tpairs:
                gmm = None
                for dc in range(8):
                    for ci, (s0, sl) in enumerate(tp):
                        gmm = p.t(lambda t_, b=b, hc=hc, s0=s0, sl=sl, dc=dc, ci=ci: t_.matmul(
                            g_ps[ci][:, 0:sl], w_gate[b][:, dc, hc * 128:(hc + 1) * 128],
                            xlnT[:, dc, s0:s0 + sl],
                            start=(dc == 0), stop=(dc == 7)),
                            (dW + ln_done + [g_last_reader[ci]]) if dc == 0 else [])
                sils = []
                for ci, (s0, sl) in enumerate(tp):
                    sil = p.s(lambda s, ci=ci, sl=sl: s.activation(
                        silu_sb[ci][:, 0:sl], g_ps[ci][:, 0:sl], AF.Silu),
                        [gmm, silu_last[ci]])
                    sils.append(sil)
                    g_last_reader[ci] = sil
                umm = None
                for dc in range(8):
                    for ci, (s0, sl) in enumerate(tp):
                        umm = p.t(lambda t_, b=b, hc=hc, s0=s0, sl=sl, dc=dc, ci=ci: t_.matmul(
                            u_ps[ci][:, 0:sl], w_up[b][:, dc, hc * 128:(hc + 1) * 128],
                            xlnT[:, dc, s0:s0 + sl],
                            start=(dc == 0), stop=(dc == 7)),
                            [u_last_reader[ci]] if dc == 0 else [])
                for ci, (s0, sl) in enumerate(tp):
                    mul = p.v(lambda v, hc=hc, s0=s0, sl=sl, ci=ci: v.tensor_tensor(
                        hidT[:, hc, s0:s0 + sl], u_ps[ci][:, 0:sl], silu_sb[ci][:, 0:sl], ALU.mult),
                        [umm, sils[ci]])
                    u_last_reader[ci] = mul
                    silu_last[ci] = mul
                    hid_ops.append(mul)
        hid_done = hid_ops[-1]
        b_last_mm = None
        for tt in range(NT):
            dmm = None
            for hc in range(4):
                for dcol in range(2):
                    dmm = p.t(lambda t_, b=b, tt=tt, dcol=dcol, hc=hc: t_.matmul(
                        d_ps[dcol][:, :], hidT[:, hc, tt * 128:(tt + 1) * 128],
                        w_down[b][:, hc, dcol * 512:(dcol + 1) * 512],
                        start=(hc == 0), stop=(hc == 3)),
                        ([hid_done, d_last_reader[0], d_last_reader[1]] + router_done)
                        if hc == 0 and dcol == 0 else [])
            for dcol in range(2):
                pd = dcol
                if e < n_routed:
                    stt = p.v(lambda v, tt=tt, dcol=dcol, pd=pd, e=e: v.scalar_tensor_tensor(
                        x_sb[:, tt, dcol * 512:(dcol + 1) * 512], d_ps[pd][:, :],
                        coefs[:, tt, e:e + 1], x_sb[:, tt, dcol * 512:(dcol + 1) * 512],
                        ALU.mult, ALU.add), [dmm])
                else:
                    stt = p.v(lambda v, tt=tt, dcol=dcol, pd=pd: v.scalar_tensor_tensor(
                        x_sb[:, tt, dcol * 512:(dcol + 1) * 512], d_ps[pd][:, :],
                        float(shared_coef), x_sb[:, tt, dcol * 512:(dcol + 1) * 512],
                        ALU.mult, ALU.add), [dmm])
                d_last_reader[pd] = stt
                last_x_op = stt
                b_last_mm = dmm
        buf_last_reader[b] = b_last_mm
        if i + 2 < len(seq):
            load_w(i + 2)
    return last_x_op


def emit_router_l1(p, ts_, NT, xlnT, rwb, evac_ops, rw_dep):
    logits_ps = ts_["logits_ps"]
    stats = ts_["stats"]
    coefs = ts_["coefs"]
    eq1 = ts_["eq1"]
    masked = ts_["masked"]
    eq2 = ts_["eq2"]

    def st(t, j):
        return stats[:, t, j:j + 1]

    out_ops = []
    last_ps_reader = None
    for t in range(NT):
        mms = []
        for dc in range(8):
            mms.append(p.t(lambda t_, t=t, dc=dc: t_.matmul(
                logits_ps[:, 0:8], xlnT[:, dc, t * 128:(t + 1) * 128], rwb[:, dc, 0:8],
                start=(dc == 0), stop=(dc == 7)),
                ([evac_ops[t], rw_dep, last_ps_reader] if dc == 0 else [])))
        mm = mms[-1]
        nm1 = p.v(lambda v, t=t: v.reduce_max(st(t, 0), logits_ps[:, 0:8], axis=AX.X, negate=True), [mm])
        e1 = p.v(lambda v, t=t: v.tensor_scalar(eq1[:], logits_ps[:, 0:8], st(t, 0), 0.0, ALU.add, ALU.is_equal), [nm1, mm])
        mk = p.v(lambda v, t=t: v.scalar_tensor_tensor(masked[:], eq1[:], -1e30, logits_ps[:, 0:8], ALU.mult, ALU.add), [e1, mm])
        nm2 = p.v(lambda v, t=t: v.reduce_max(st(t, 1), masked[:], axis=AX.X, negate=True), [mk])
        e2 = p.v(lambda v, t=t: v.tensor_scalar(eq2[:], logits_ps[:, 0:8], st(t, 1), 0.0, ALU.add, ALU.is_equal), [nm2, mm])
        last_ps_reader = e2
        dm = p.v(lambda v, t=t: v.tensor_tensor(st(t, 2), st(t, 0), st(t, 1), ALU.subtract), [nm2, nm1])
        z = p.s(lambda s, t=t: s.activation(st(t, 3), st(t, 2), AF.Exp), [dm])
        zp = p.v(lambda v, t=t: v.tensor_scalar(st(t, 4), st(t, 3), 1.0, None, ALU.add), [z])
        w1 = p.v(lambda v, t=t: v.reciprocal(st(t, 5), st(t, 4)), [zp])
        w1h = p.v(lambda v, t=t: v.tensor_scalar(st(t, 6), st(t, 5), 0.5, None, ALU.mult), [w1])
        w2h = p.v(lambda v, t=t: v.tensor_scalar(st(t, 7), st(t, 6), -1.0, 0.5, ALU.mult, ALU.add), [w1h])
        c1 = p.v(lambda v, t=t: v.tensor_scalar(coefs[:, t, :], eq1[:], st(t, 6), None, ALU.mult), [w1h, e1])
        c2 = p.v(lambda v, t=t: v.scalar_tensor_tensor(coefs[:, t, :], eq2[:], st(t, 7), coefs[:, t, :], ALU.mult, ALU.add), [w2h, e2, c1])
        out_ops.append(c2)
    return out_ops



def build_ffn(T, layer):
    NT = T // 128
    H = 4096 if layer == 0 else 2048
    nc = bass.Bass("TRN2", target_bir_lowering=False)
    xin = nc.dram_tensor("xin", [T, D], F32, kind="ExternalInput")
    rw = nc.dram_tensor("rw", [D, 8], F32, kind="ExternalInput")
    if layer == 0:
        up7 = nc.dram_tensor("up7", [7, D, H], F32, kind="ExternalInput")
        gate7 = nc.dram_tensor("gate7", [7, D, H], F32, kind="ExternalInput")
        down7 = nc.dram_tensor("down7", [7, H, D], F32, kind="ExternalInput")
        sup = nc.dram_tensor("sup", [D, H], F32, kind="ExternalInput")
        sgate = nc.dram_tensor("sgate", [D, H], F32, kind="ExternalInput")
        sdown = nc.dram_tensor("sdown", [H, D], F32, kind="ExternalInput")
    else:
        w1 = nc.dram_tensor("w1", [8, D, H], F32, kind="ExternalInput")
        w2 = nc.dram_tensor("w2", [8, D, H], F32, kind="ExternalInput")
        w3 = nc.dram_tensor("w3", [8, H, D], F32, kind="ExternalInput")
    identin = nc.dram_tensor("identin", [128, 128], BF16, kind="ExternalInput")
    xout = nc.dram_tensor("xout", [T, D], F32, kind="ExternalOutput")

    es = ExitStack()
    sb = lambda name, shape, dt: es.enter_context(nc.sbuf_tensor(name, shape, dt))
    ps = lambda name, shape, dt: es.enter_context(nc.psum_tensor(name, shape, dt))

    ts_ = {}
    x_sb = sb("x", [128, NT, D], F32)
    xlnT = sb("xlnT", [128, 8, T], BF16)
    ts_["stats"] = sb("stats", [128, NT, 8], F32)
    ts_["sq"] = sb("sq", [128, D], F32)
    ts_["xln_scr"] = [sb("xlns0", [128, D], BF16), sb("xlns1", [128, D], BF16)]
    ts_["ident"] = sb("ident", [128, 128], BF16)
    ts_["coefs"] = sb("coefs", [128, NT, 8], F32)
    ts_["exps_scr"] = sb("exps_scr", [128, 8], F32)
    ts_["eq1"] = sb("eq1", [128, 8], F32)
    ts_["eq2"] = sb("eq2", [128, 8], F32)
    ts_["masked"] = sb("masked", [128, 8], F32)
    rwb = sb("rwb", [128, 8, 8], BF16)
    ts_["w_up"] = [sb("wup0", [128, 8, 512], BF16), sb("wup1", [128, 8, 512], BF16)]
    ts_["w_gate"] = [sb("wg0", [128, 8, 512], BF16), sb("wg1", [128, 8, 512], BF16)]
    ts_["w_down"] = [sb("wd0", [128, 4, D], BF16), sb("wd1", [128, 4, D], BF16)]
    ts_["hidT"] = sb("hidT", [128, 4, T], BF16)
    ts_["silu_sb"] = [sb("sil0", [128, 512], F32), sb("sil1", [128, 512], F32)]
    ts_["tr_ps"] = ps("tr_ps", [128, 8, 128], BF16)
    ts_["logits_ps"] = ps("logits_ps", [128, 512], F32)
    ts_["g_ps"] = [ps("gps0", [128, 512], F32), ps("gps1", [128, 512], F32)]
    ts_["u_ps"] = [ps("ups0", [128, 512], F32), ps("ups1", [128, 512], F32)]
    ts_["d_ps"] = [ps("dps0", [128, 512], F32), ps("dps1", [128, 512], F32)]

    with nc.Block() as block:
        p = Plan(nc, es)
        ld_x = p.sy(lambda s: s.dma_start(x_sb[:, :, :], xin.rearrange("(t p) d -> p t d", p=128)))
        ld_id = p.sy(lambda s: s.dma_start(ts_["ident"][:], identin[:, :]))
        ld_rw = p.g(lambda g: g.dma_start(rwb[:, :, :], rw.rearrange("(c p) e -> p c e", p=128)))

        evac_ops, _ = emit_ln(p, ts_, NT, x_sb, xlnT, [ld_x, ld_id])
        if layer == 0:
            coef_ops = emit_router_l0(p, ts_, NT, xlnT, rwb, evac_ops, ld_rw)

            def wsrc(e, h8):
                hs = slice(h8 * 512, (h8 + 1) * 512)
                if e < 7:
                    return (up7[e, :, hs].rearrange("(c p) h -> p c h", p=128),
                            gate7[e, :, hs].rearrange("(c p) h -> p c h", p=128),
                            down7[e, hs, :].rearrange("(c p) d -> p c d", p=128))
                return (sup[:, hs].rearrange("(c p) h -> p c h", p=128),
                        sgate[:, hs].rearrange("(c p) h -> p c h", p=128),
                        sdown[hs, :].rearrange("(c p) d -> p c d", p=128))

            n_routed, shared_coef = 7, 0.5
        else:
            coef_ops = emit_router_l1(p, ts_, NT, xlnT, rwb, evac_ops, ld_rw)

            def wsrc(e, h8):
                hs = slice(h8 * 512, (h8 + 1) * 512)
                return (w2[e, :, hs].rearrange("(c p) h -> p c h", p=128),
                        w1[e, :, hs].rearrange("(c p) h -> p c h", p=128),
                        w3[e, hs, :].rearrange("(c p) d -> p c d", p=128))

            n_routed, shared_coef = 8, None

        last_x = emit_moe(p, ts_, NT, T, xlnT, x_sb, ts_["coefs"], wsrc,
                          ln_done=[evac_ops[-1], ld_rw], router_done=[coef_ops[-1]],
                          H=H, n_routed=n_routed, shared_coef=shared_coef,
                          silu_on_gate=True)

        st = p.sy(lambda s: s.dma_start(xout.rearrange("(t p) d -> p t d", p=128), x_sb[:, :, :]), [last_x])
        p.sy(lambda s: s.nop(), [st], dma=False)
        p.finalize_and_emit(block)
    es.close()
    return nc



def emit_ln_stream(p, ts_, NT, xin_tiled, xlnT, deps0, tr_war=None):
    stats = ts_["stats"]
    sq = ts_["sq"]
    xb = ts_["xln_scr"]
    xs = ts_["x_stream"]
    tr_ps = ts_["tr_ps"]
    ident = ts_["ident"]

    def st(t, j):
        return stats[:, t, j:j + 1]

    evac_ops = []
    prev_use = [None, None]
    tr_prev = tr_war
    evac_prev2 = [None, None]
    for t in range(NT):
        ld = p.sy(lambda s, t=t: s.dma_start(xs[t % 2][:], xin_tiled[t]), [prev_use[t % 2]] + deps0)
        rs = p.v(lambda v, t=t: v.reduce_sum(st(t, 0), xs[t % 2][:], axis=AX.X, negate=True), [ld])
        rq = p.s(lambda s, t=t: s.activation(sq[:], xs[t % 2][:], AF.Square, accum_out=st(t, 1)), [ld])
        mn = p.v(lambda v, t=t: v.tensor_scalar(st(t, 2), st(t, 0), 1.0 / D, None, ALU.mult), [rs])
        m2 = p.v(lambda v, t=t: v.tensor_tensor(st(t, 3), st(t, 2), st(t, 2), ALU.mult), [mn])
        va = p.v(lambda v, t=t: v.tensor_scalar(st(t, 4), st(t, 1), 1.0 / D, 1e-5, ALU.mult, ALU.add), [rq])
        vb = p.v(lambda v, t=t: v.tensor_tensor(st(t, 4), st(t, 4), st(t, 3), ALU.subtract), [va, m2])
        ln = p.s(lambda s, t=t: s.activation(st(t, 5), st(t, 4), AF.Ln), [vb])
        rstd = p.s(lambda s, t=t: s.activation(st(t, 6), st(t, 5), AF.Exp, scale=-0.5), [ln])
        xop = p.v(lambda v, t=t: v.tensor_scalar(xb[t % 2][:], xs[t % 2][:], st(t, 2), st(t, 6), ALU.add, ALU.mult),
                  [rstd, evac_prev2[t % 2]])
        prev_use[t % 2] = xop
        trs = []
        for dc in range(8):
            trs.append(p.t(lambda t_, t=t, dc=dc: t_.transpose(
                tr_ps[:, dc, :], xb[t % 2][:, dc * 128:(dc + 1) * 128], ident[:]),
                [xop] + ([tr_prev] if dc == 0 else [])))
        ev = p.v(lambda v, t=t: v.tensor_copy(xlnT[:, :, t * 128:(t + 1) * 128], tr_ps[:, :, :]), [trs[-1]])
        evac_ops.append(ev)
        evac_prev2[t % 2] = ev
        tr_prev = ev
    return evac_ops


def emit_w_transpose(p, ts_, w_dram, WT, deps0, war_dep):
    wstage = ts_["wstage"]
    tr_ps = ts_["tr_ps"]
    ident = ts_["ident"]
    w4 = w_dram.rearrange("(fc p) d -> fc p d", p=128)
    tr_prev = war_dep
    ld_first = None
    for hf in range(2):
        ld = p.g(lambda g, hf=hf: g.dma_start(wstage[:, :, :], w4[hf * 4:(hf + 1) * 4].rearrange("fc p d -> p fc d")),
                 deps0 + [tr_prev])
        if ld_first is None:
            ld_first = ld
        for dc in range(8):
            trs = []
            for fc in range(4):
                trs.append(p.t(lambda t_, dc=dc, fc=fc: t_.transpose(
                    tr_ps[:, fc, :], wstage[:, fc, dc * 128:(dc + 1) * 128], ident[:]),
                    [ld] + ([tr_prev] if fc == 0 else [])))
            ev = p.v(lambda v, dc=dc, hf=hf: v.tensor_copy(
                WT[:, dc, hf * 512:(hf + 1) * 512], tr_ps[:, 0:4, :]), [trs[-1]])
            tr_prev = ev
    return ld_first, tr_prev


def emit_proj_fm(p, ts_, WT, xlnT, OUT, Tn, wt_dep, xln_dep, rope, bank_war):
    pr_ps = ts_["pr_ps"]
    rt_ps = ts_.get("rt_ps")
    qraw = ts_.get("qraw")
    rsc = ts_.get("rope_scr")
    slices = [(s, min(512, Tn - s)) for s in range(0, Tn, 512)]
    out_ops = []
    bank_reader = [bank_war, bank_war]
    qraw_reader = [None, None]
    rt_reader = None
    rsc_reader = [None, None]
    k = 0
    for qc in range(8):
        for (s0, sl) in slices:
            pb = k % 2
            k += 1
            mm = None
            for dc in range(8):
                mm = p.t(lambda t_, qc=qc, s0=s0, sl=sl, dc=dc, pb=pb: t_.matmul(
                    pr_ps[pb][:, 0:sl], WT[:, dc, qc * 128:(qc + 1) * 128],
                    xlnT[:, dc, s0:s0 + sl],
                    start=(dc == 0), stop=(dc == 7)),
                    ([wt_dep, xln_dep, bank_reader[pb]] if dc == 0 else []))
            if rope is None:
                ev = p.v(lambda v, qc=qc, s0=s0, sl=sl, pb=pb: v.tensor_copy(
                    OUT[:, qc, s0:s0 + sl], pr_ps[pb][:, 0:sl]), [mm])
                bank_reader[pb] = ev
                out_ops.append(ev)
                continue
            permb, cosT, sinT, off = rope
            cs = cosT[:, off + s0: off + s0 + sl]
            sn = sinT[:, off + s0: off + s0 + sl]
            qr = p.v(lambda v, pb=pb, sl=sl: v.tensor_copy(qraw[pb][:, 0:sl], pr_ps[pb][:, 0:sl]), [mm, qraw_reader[pb]])
            bank_reader[pb] = qr
            rt = p.t(lambda t_, pb=pb, sl=sl: t_.matmul(rt_ps[:, 0:sl], permb[:], qraw[pb][:, 0:sl], start=True, stop=True),
                     [qr, rt_reader])
            t1 = p.v(lambda v, pb=pb, cs=cs, sl=sl: v.tensor_tensor(rsc[pb][:, 0:sl], qraw[pb][:, 0:sl], cs, ALU.mult),
                     [qr, rsc_reader[pb]])
            qraw_reader[pb] = t1
            t2 = p.v(lambda v, pb=pb, sn=sn, sl=sl: v.tensor_tensor(rt_ps[:, 0:sl], rt_ps[:, 0:sl], sn, ALU.mult), [rt])
            ev = p.v(lambda v, qc=qc, s0=s0, sl=sl, pb=pb: v.tensor_tensor(
                OUT[:, qc, s0:s0 + sl], rsc[pb][:, 0:sl], rt_ps[:, 0:sl], ALU.add), [t2, t1])
            rt_reader = ev
            rsc_reader[pb] = ev
            out_ops.append(ev)
    return out_ops


def emit_normbound(p, ts_, QT, KT, Tq, Tk, indm, ones1, q_deps, k_deps, scale):
    sqscr = ts_["sqscr"]
    nb_ps = ts_["nb_ps"]
    nstat = ts_["nstat"]
    ntr = ts_["ntr"]
    negmh = ts_["negmh"]

    state = {"ps_reader": None}

    def half(T_, XT, deps, col):
        slices = [(s, min(512, T_ - s)) for s in range(0, T_, 512)]
        run = None
        sq_reader = [None, None]
        for (s0, sl) in slices:
            mm = None
            for qc in range(8):
                pb = qc % 2
                sq = p.s(lambda s, qc=qc, s0=s0, sl=sl, pb=pb, XT=XT: s.activation(
                    sqscr[pb][:, 0:sl], XT[:, qc, s0:s0 + sl], AF.Square),
                    deps + [sq_reader[pb]])
                mm = p.t(lambda t_, qc=qc, pb=pb, sl=sl: t_.matmul(
                    nb_ps[0:16, 0:sl], indm[:, qc, :], sqscr[pb][:, 0:sl],
                    start=(qc == 0), stop=(qc == 7)),
                    [sq] + ([state["ps_reader"]] if qc == 0 else []))
                sq_reader[pb] = mm
            rm = p.v(lambda v, col=col, sl=sl: v.reduce_max(
                nstat[0:16, col + 2:col + 3], nb_ps[0:16, 0:sl], axis=AX.X), [mm])
            state["ps_reader"] = rm
            if run is None:
                run = p.v(lambda v, col=col: v.tensor_copy(
                    nstat[0:16, col:col + 1], nstat[0:16, col + 2:col + 3]), [rm])
            else:
                run = p.v(lambda v, col=col: v.tensor_tensor(
                    nstat[0:16, col:col + 1], nstat[0:16, col:col + 1],
                    nstat[0:16, col + 2:col + 3], ALU.max), [rm, run])
        return run

    rq = half(Tq, QT, q_deps, 0)
    rk = half(Tk, KT, k_deps, 1)
    pr = p.v(lambda v: v.tensor_tensor(nstat[0:16, 4:5], nstat[0:16, 0:1], nstat[0:16, 1:2], ALU.mult), [rq, rk])
    ln = p.s(lambda s: s.activation(nstat[0:16, 5:6], nstat[0:16, 4:5], AF.Ln), [pr])
    sq = p.s(lambda s: s.activation(nstat[0:16, 6:7], nstat[0:16, 5:6], AF.Exp, scale=0.5), [ln])
    ng = p.v(lambda v: v.tensor_scalar(nstat[0:16, 0:1], nstat[0:16, 6:7], -float(scale), None, ALU.mult), [sq, pr])
    tr = p.v(lambda v: v.transpose(ntr[:, :], nstat[:, 0:32]), [ng])
    bc = p.t(lambda t_: t_.matmul(nb_ps[:, 0:16], ones1[0:1, :], ntr[0:1, 0:16],
                                  start=True, stop=True), [tr, state["ps_reader"]])
    ev = p.v(lambda v: v.tensor_copy(negmh[:, :], nb_ps[:, 0:16]), [bc])
    return ev


def emit_proj_tok(p, ts_, WT, xlnT, V, NTk, wt_dep, xln_dep, bank_war):
    pr_ps = ts_["pr_ps"]
    out_ops = []
    bank_reader = [bank_war, bank_war]
    k = 0
    for tt in range(NTk):
        for fcol in range(2):
            pb = k % 2
            mm = None
            for dc in range(8):
                mm = p.t(lambda t_, tt=tt, fcol=fcol, dc=dc, pb=pb: t_.matmul(
                    pr_ps[pb][:, :], xlnT[:, dc, tt * 128:(tt + 1) * 128],
                    WT[:, dc, fcol * 512:(fcol + 1) * 512],
                    start=(dc == 0), stop=(dc == 7)),
                    ([wt_dep, xln_dep, bank_reader[pb]] if dc == 0 else []))
            ev = p.v(lambda v, tt=tt, fcol=fcol, pb=pb: v.tensor_copy(
                V[:, tt, fcol * 512:(fcol + 1) * 512], pr_ps[pb][:, :]), [mm])
            bank_reader[pb] = ev
            out_ops.append(ev)
            k += 1
    return out_ops


def build_attn_l0(debug_stage=0):
    TQ, TK = 2048, 2304
    NTQ, NTK = 16, 18
    nc = bass.Bass("TRN2", target_bir_lowering=False)
    xin = nc.dram_tensor("xin", [TK, D], F32, kind="ExternalInput")
    wq = nc.dram_tensor("wq", [D, D], F32, kind="ExternalInput")
    wk = nc.dram_tensor("wk", [D, D], F32, kind="ExternalInput")
    wv = nc.dram_tensor("wv", [D, D], F32, kind="ExternalInput")
    wo = nc.dram_tensor("wo", [D, D], F32, kind="ExternalInput")
    identin = nc.dram_tensor("identin", [128, 128], BF16, kind="ExternalInput")
    permin = nc.dram_tensor("permin", [128, 128], BF16, kind="ExternalInput")
    cosin = nc.dram_tensor("cosin", [128, TK], BF16, kind="ExternalInput")
    sinin = nc.dram_tensor("sinin", [128, TK], BF16, kind="ExternalInput")
    maskin = nc.dram_tensor("maskin", [128, 3, 384], F32, kind="ExternalInput")
    indmin = nc.dram_tensor("indmin", [128, 8, 16], BF16, kind="ExternalInput")
    onesin = nc.dram_tensor("onesin", [1, 128], F32, kind="ExternalInput")
    xout = nc.dram_tensor("xout", [TQ, D], F32, kind="ExternalOutput")

    es = ExitStack()
    es1 = ExitStack()
    sb = lambda name, shape, dt: es.enter_context(nc.sbuf_tensor(name, shape, dt))
    sb1 = lambda name, shape, dt: es1.enter_context(nc.sbuf_tensor(name, shape, dt))
    ps = lambda name, shape, dt: es.enter_context(nc.psum_tensor(name, shape, dt))

    ts_ = {}
    ts_["ident"] = sb("ident", [128, 128], BF16)
    WT = sb("WT", [128, 8, D], BF16)
    QT = sb("QT", [128, 8, TQ], BF16)
    KT = sb("KT", [128, 8, TK], BF16)
    V = sb("V", [128, NTK, D], BF16)
    indm = sb("indm", [128, 8, 16], BF16)
    ones1 = sb("ones1", [1, 128], F32)
    ts_["negmh"] = sb("negmh", [128, 16], F32)
    xlnT = sb1("xlnT", [128, 8, TK], BF16)
    ts_["stats"] = sb1("stats", [128, NTK, 8], F32)
    ts_["sq"] = sb1("sq", [128, D], F32)
    ts_["xln_scr"] = [sb1("xlns0", [128, D], BF16), sb1("xlns1", [128, D], BF16)]
    ts_["x_stream"] = [sb1("xs0", [128, D], F32), sb1("xs1", [128, D], F32)]
    ts_["wstage"] = sb1("wstage", [128, 4, D], BF16)
    perm = sb1("perm", [128, 128], BF16)
    cosT = sb1("cosT", [128, TK], BF16)
    sinT = sb1("sinT", [128, TK], BF16)
    ts_["qraw"] = [sb1("qraw0", [128, 512], BF16), sb1("qraw1", [128, 512], BF16)]
    ts_["rope_scr"] = [sb1("rsc0", [128, 512], F32), sb1("rsc1", [128, 512], F32)]
    ts_["sqscr"] = [sb1("sqs0", [128, 512], BF16), sb1("sqs1", [128, 512], BF16)]
    ts_["nstat"] = sb1("nstat", [32, 32], F32)
    ts_["ntr"] = sb1("ntr", [32, 32], F32)

    ts_["tr_ps"] = ps("tr_ps", [128, 8, 128], BF16)
    ts_["pr_ps"] = [ps("prps0", [128, 512], F32), ps("prps1", [128, 512], F32)]
    ts_["rt_ps"] = ps("rt_ps", [128, 512], F32)
    s_ps = [ps("sps0", [128, 512], F32), ps("sps1", [128, 512], F32)]
    y_ps = [ps("yps0", [128, 512], F32), ps("yps1", [128, 512], F32)]
    ts_["nb_ps"] = s_ps[0]
    ptr_ps = ts_["tr_ps"]
    ytr_ps = ts_["tr_ps"]
    wo_ps = ts_["pr_ps"]

    with nc.Block() as block:
        p = Plan(nc, es)
        ld_id = p.sy(lambda s: s.dma_start(ts_["ident"][:], identin[:, :]))
        ld_pm = p.sy(lambda s: s.dma_start(perm[:], permin[:, :]))
        ld_cs = p.sy(lambda s: s.dma_start(cosT[:], cosin[:, :]))
        ld_sn = p.sy(lambda s: s.dma_start(sinT[:], sinin[:, :]))
        ld_im = p.sy(lambda s: s.dma_start(indm[:, :, :], indmin[:, :, :]))
        ld_on = p.sy(lambda s: s.dma_start(ones1[:], onesin[:, :]))

        xin_tiled = xin.rearrange("(t p) d -> t p d", p=128)
        evac_ops = emit_ln_stream(p, ts_, NTK, xin_tiled, xlnT, [ld_id])
        xln_dep = evac_ops[-1]

        v_ops = q_ops = k_ops = [xln_dep]
        q_dep = k_dep = [xln_dep]
        nb_ev = woT = xln_dep
        if debug_stage == 0 or debug_stage >= 2 or debug_stage == 12:
            _, wvT = emit_w_transpose(p, ts_, wv[:, :], WT, [ld_id], xln_dep)
            v_ops = [wvT]
            if debug_stage != 12:
                v_ops = emit_proj_tok(p, ts_, WT, xlnT, V, NTK, wvT, xln_dep, None)
        if debug_stage == 0 or debug_stage >= 3:
            _, wqT = emit_w_transpose(p, ts_, wq[:, :], WT, [], v_ops[-1])
            q_ops = emit_proj_fm(p, ts_, WT, xlnT[:, :, 256:256 + TQ], QT, TQ, wqT, xln_dep,
                                 (perm, cosT, sinT, 256), v_ops[-1])
            q_dep = [q_ops[-1], ld_pm, ld_cs, ld_sn]
        if debug_stage == 0 or debug_stage >= 4:
            _, wkT = emit_w_transpose(p, ts_, wk[:, :], WT, [], q_ops[-1])
            k_ops = emit_proj_fm(p, ts_, WT, xlnT, KT, TK, wkT, xln_dep,
                                 (perm, cosT, sinT, 0), q_ops[-1])
            k_dep = [k_ops[-1], ld_pm, ld_cs, ld_sn]
        if debug_stage == 0 or debug_stage >= 5:
            _, woT = emit_w_transpose(p, ts_, wo[:, :], WT, [], k_ops[-1])
            nb_ev = emit_normbound(p, ts_, QT, KT, TQ, TK, indm, ones1,
                                   q_dep + [ld_im], k_dep, 0.125)

        if debug_stage:
            if debug_stage == 12:
                dw = nc.dram_tensor("dbg_wt", [128, 8, D], BF16, kind="ExternalOutput")
                dl = p.sy(lambda s: s.dma_start(dw[:, :, :], WT[:, :, :]), [wvT])
                p.sy(lambda s: s.nop(), [dl], dma=False)
                p.finalize_and_emit(block)
                es1.close(); es.close()
                return nc
            dq = nc.dram_tensor("dbg_qt", [128, 8, TQ], BF16, kind="ExternalOutput")
            dk = nc.dram_tensor("dbg_kt", [128, 8, TK], BF16, kind="ExternalOutput")
            dv = nc.dram_tensor("dbg_v", [128, NTK, D], BF16, kind="ExternalOutput")
            dn = nc.dram_tensor("dbg_nm", [128, 16], F32, kind="ExternalOutput")
            dls = []
            if debug_stage >= 2:
                dls.append(p.sy(lambda s: s.dma_start(dv[:, :, :], V[:, :, :]), [v_ops[-1]]))
            if debug_stage >= 3:
                dls.append(p.sy(lambda s: s.dma_start(dq[:, :, :], QT[:, :, :]), q_dep))
            if debug_stage >= 4:
                dls.append(p.sy(lambda s: s.dma_start(dk[:, :, :], KT[:, :, :]), k_dep))
            if debug_stage >= 5:
                dls.append(p.sy(lambda s: s.dma_start(dn[:, :], ts_["negmh"][:, :]), [nb_ev, woT]))
            p.sy(lambda s: s.nop(), [dls[-1] if dls else xln_dep], dma=False)
            p.finalize_and_emit(block)
            es1.close()
            es.close()
            return nc

        es1.close()
        masks = sb("masks", [128, 3, 384], F32)
        P_sb = [sb("psb0", [128, 384], BF16), sb("psb1", [128, 384], BF16)]
        PT_sb = [sb("ptsb0", [128, 3, 128], BF16), sb("ptsb1", [128, 3, 128], BF16)]
        lbuf = sb("lbuf", [128, 16], F32)
        linv = sb("linv", [128, 16], F32)
        y_tile = [sb("yt0", [128, D], BF16), sb("yt1", [128, D], BF16)]
        yT_sb = sb("yTsb", [128, 8, 128], BF16)
        xadd = [sb("xadd0", [128, D], F32), sb("xadd1", [128, D], F32)]
        ld_mk = p.sy(lambda s: s.dma_start(masks[:, :, :], maskin[:, :, :]),
                     [nb_ev, woT, xln_dep, q_ops[-1], k_ops[-1], v_ops[-1]])

        negmh = ts_["negmh"]
        s_reader = [nb_ev, nb_ev]
        y_reader = [None, None]
        pt_reader = {0: woT}
        psb_reader = [None, None]
        yT_reader = None
        ytile_reader = [None, None]
        wo_reader = [k_ops[-1], k_ops[-1]]
        xadd_reader = [None, None]
        store_ops = []
        last_store = None
        for t in range(NTQ):
            mi = min(t, 2)
            ytile_ops = []
            for h in range(16):
                hb = (h % 2) * 64
                qc = h // 2
                smm = p.t(lambda t_, t=t, h=h, hb=hb, qc=qc: t_.matmul(
                    s_ps[h % 2][:, 0:384],
                    QT[hb:hb + 64, qc, t * 128:(t + 1) * 128],
                    KT[hb:hb + 64, qc, t * 128:t * 128 + 384],
                    start=True, stop=True),
                    [s_reader[h % 2]] + (q_dep + k_dep if (t == 0 and h == 0) else []))
                madd = p.v(lambda v, t=t, h=h, mi=mi: v.tensor_tensor(
                    s_ps[h % 2][:, 0:384], s_ps[h % 2][:, 0:384], masks[:, mi, :], ALU.add),
                    [smm, ld_mk])
                expo = p.s(lambda s, t=t, h=h: s.activation(
                    P_sb[h % 2][:], s_ps[h % 2][:, 0:384], AF.Exp, scale=0.125,
                    bias=negmh[:, h:h + 1], accum_out=lbuf[:, h:h + 1]), [madd, nb_ev, psb_reader[h % 2]])
                s_reader[h % 2] = expo
                li = p.v(lambda v, h=h: v.reciprocal(linv[:, h:h + 1], lbuf[:, h:h + 1]), [expo])
                trs = []
                for jc in range(3):
                    trs.append(p.t(lambda t_, h=h, jc=jc: t_.transpose(
                        ptr_ps[:, jc, :], P_sb[h % 2][:, jc * 128:(jc + 1) * 128], ts_["ident"][:]),
                        [expo] + ([pt_reader[0]] if jc == 0 else [])))
                pte = p.v(lambda v, h=h: v.tensor_copy(PT_sb[h % 2][:, :, :], ptr_ps[:, 0:3, :]), [trs[-1]])
                psb_reader[h % 2] = trs[-1]
                pt_reader[0] = pte
                pvs = None
                for jc in range(3):
                    pvs = p.t(lambda t_, t=t, h=h, jc=jc: t_.matmul(
                        y_ps[h % 2][:, 0:64], PT_sb[h % 2][:, jc, :], V[:, t + jc, h * 64:(h + 1) * 64],
                        start=(jc == 0), stop=(jc == 2)),
                        [pte, v_ops[-1], y_reader[h % 2]] if jc == 0 else [])
                se = p.v(lambda v, t=t, h=h: v.tensor_scalar(
                    y_tile[t % 2][:, h * 64:(h + 1) * 64], y_ps[h % 2][:, 0:64],
                    linv[:, h:h + 1], None, ALU.mult), [pvs, li, ytile_reader[t % 2] if h == 0 else None])
                y_reader[h % 2] = se
                ytile_ops.append(se)
            ytrs = []
            for fc in range(8):
                ytrs.append(p.t(lambda t_, t=t, fc=fc: t_.transpose(
                    ytr_ps[:, fc, :], y_tile[t % 2][:, fc * 128:(fc + 1) * 128], ts_["ident"][:]),
                    [ytile_ops[-1]] + ([pt_reader[0]] if fc == 0 else [])))
            yte = p.v(lambda v, t=t: v.tensor_copy(yT_sb[:, :, :], ytr_ps[:, :, :]), [ytrs[-1], yT_reader])
            ytile_reader[t % 2] = ytrs[-1]
            pt_reader[0] = yte
            ldx = p.sy(lambda s, t=t: s.dma_start(
                xadd[t % 2][:], xin_tiled[t + 2]), [xadd_reader[t % 2]])
            adds = []
            for dcol in range(2):
                wmm = None
                for fc in range(8):
                    wmm = p.t(lambda t_, t=t, dcol=dcol, fc=fc: t_.matmul(
                        wo_ps[dcol][:, :], yT_sb[:, fc, :], WT[:, fc, dcol * 512:(dcol + 1) * 512],
                        start=(fc == 0), stop=(fc == 7)),
                        [yte, wo_reader[dcol]] if fc == 0 else [])
                ad = p.v(lambda v, t=t, dcol=dcol: v.tensor_tensor(
                    xadd[t % 2][:, dcol * 512:(dcol + 1) * 512], wo_ps[dcol][:, :],
                    xadd[t % 2][:, dcol * 512:(dcol + 1) * 512], ALU.add), [wmm, ldx])
                wo_reader[dcol] = ad
                adds.append(ad)
            yT_reader = wmm
            stv = p.sy(lambda s, t=t: s.dma_start(
                xout.rearrange("(t p) d -> t p d", p=128)[t], xadd[t % 2][:]), adds)
            xadd_reader[t % 2] = stv
            store_ops.append(stv)
            last_store = stv
        p.sy(lambda s: s.nop(), [last_store], dma=False)
        p.finalize_and_emit(block)
    es.close()
    return nc


def emit_normsq_half(p, ts_, XT, T_, indm, col, deps, run_prev):
    sqscr = ts_["sqscr"]
    nb_ps = ts_["nb_ps"]
    nstat = ts_["nstat"]
    slices = [(s, min(512, T_ - s)) for s in range(0, T_, 512)]
    run = run_prev
    sq_reader = ts_.setdefault("_nb_sq_reader", [None, None])
    for (s0, sl) in slices:
        mm = None
        for qc in range(8):
            pb = qc % 2
            sq = p.s(lambda s, qc=qc, s0=s0, sl=sl, pb=pb: s.activation(
                sqscr[pb][:, 0:sl], XT[:, qc, s0:s0 + sl], AF.Square),
                deps + [sq_reader[pb]])
            mm = p.t(lambda t_, qc=qc, pb=pb, sl=sl: t_.matmul(
                nb_ps[0:16, 0:sl], indm[:, qc, :], sqscr[pb][:, 0:sl],
                start=(qc == 0), stop=(qc == 7)),
                [sq] + ([ts_.get("_nb_ps_reader")] if qc == 0 else []))
            sq_reader[pb] = mm
        rm = p.v(lambda v, col=col, sl=sl: v.reduce_max(
            nstat[0:16, col + 2:col + 3], nb_ps[0:16, 0:sl], axis=AX.X), [mm])
        ts_["_nb_ps_reader"] = rm
        if run is None:
            run = p.v(lambda v, col=col: v.tensor_copy(
                nstat[0:16, col:col + 1], nstat[0:16, col + 2:col + 3]), [rm])
        else:
            run = p.v(lambda v, col=col: v.tensor_tensor(
                nstat[0:16, col:col + 1], nstat[0:16, col:col + 1],
                nstat[0:16, col + 2:col + 3], ALU.max), [rm, run])
    return run


def emit_normbound_finish(p, ts_, ones1, rq, rk, scale):
    nb_ps = ts_["nb_ps"]
    nstat = ts_["nstat"]
    ntr = ts_["ntr"]
    negmh = ts_["negmh"]
    pr = p.v(lambda v: v.tensor_tensor(nstat[0:16, 4:5], nstat[0:16, 0:1], nstat[0:16, 1:2], ALU.mult), [rq, rk])
    ln = p.s(lambda s: s.activation(nstat[0:16, 5:6], nstat[0:16, 4:5], AF.Ln), [pr])
    sq = p.s(lambda s: s.activation(nstat[0:16, 6:7], nstat[0:16, 5:6], AF.Exp, scale=0.5), [ln])
    ng = p.v(lambda v: v.tensor_scalar(nstat[0:16, 0:1], nstat[0:16, 6:7], -float(scale), None, ALU.mult), [sq, pr])
    tr = p.v(lambda v: v.transpose(ntr[:, :], nstat[:, 0:32]), [ng])
    bc = p.t(lambda t_: t_.matmul(nb_ps[:, 0:16], ones1[0:1, :], ntr[0:1, 0:16],
                                  start=True, stop=True), [tr, ts_.get("_nb_ps_reader")])
    ev = p.v(lambda v: v.tensor_copy(negmh[:, :], nb_ps[:, 0:16]), [bc])
    ts_["_nb_ps_reader"] = ev
    return ev


def build_attn_l1():
    TQ = 2048
    SB_ = 8192
    nc = bass.Bass("TRN2", target_bir_lowering=False)
    xq = nc.dram_tensor("xq", [TQ, D], F32, kind="ExternalInput")
    xb = nc.dram_tensor("xb", [SB_, D], F32, kind="ExternalInput")
    wqkv = nc.dram_tensor("wqkv", [3 * D, D], F32, kind="ExternalInput")
    identin = nc.dram_tensor("identin", [128, 128], BF16, kind="ExternalInput")
    identfin = nc.dram_tensor("identfin", [128, 128], F32, kind="ExternalInput")
    indmin = nc.dram_tensor("indmin", [128, 8, 16], BF16, kind="ExternalInput")
    onesin = nc.dram_tensor("onesin", [1, 128], F32, kind="ExternalInput")
    xout = nc.dram_tensor("xout", [TQ, D], F32, kind="ExternalOutput")
    kt_d = nc.dram_tensor("kt_d", [D, SB_], BF16, kind="Internal")
    v_d = nc.dram_tensor("v_d", [SB_, 16, 65], BF16, kind="Internal")

    es = ExitStack()
    es1 = ExitStack()
    sb = lambda name, shape, dt: es.enter_context(nc.sbuf_tensor(name, shape, dt))
    sb1 = lambda name, shape, dt: es1.enter_context(nc.sbuf_tensor(name, shape, dt))
    ps = lambda name, shape, dt: es.enter_context(nc.psum_tensor(name, shape, dt))

    ts_ = {}
    ts_["ident"] = sb("ident", [128, 128], BF16)
    identf = sb("identf", [128, 128], F32)
    QT = sb("QT", [128, 8, TQ], BF16)
    indm = sb("indm", [128, 8, 16], BF16)
    ones1 = sb("ones1", [1, 128], F32)
    ts_["negmh"] = sb("negmh", [128, 16], F32)
    ts_["nstat"] = sb("nstat", [32, 32], F32)
    ts_["ntr"] = sb("ntr", [32, 32], F32)
    xlnT = sb1("xlnT", [128, 8, 2048], BF16)
    KTg = sb1("KTg", [128, 8, 2048], BF16)
    WTk = sb1("WTk", [128, 8, D], BF16)
    WTv = sb1("WTv", [128, 8, D], BF16)
    ts_["stats"] = sb1("stats", [128, 16, 8], F32)
    ts_["sq"] = sb1("sq", [128, D], F32)
    ts_["xln_scr"] = [sb1("xlns0", [128, D], BF16), sb1("xlns1", [128, D], BF16)]
    ts_["x_stream"] = [sb1("xs0", [128, D], F32), sb1("xs1", [128, D], F32)]
    ts_["wstage"] = sb1("wstage", [128, 4, D], BF16)
    ts_["qraw"] = [sb1("qraw0", [128, 512], BF16), sb1("qraw1", [128, 512], BF16)]
    ts_["rope_scr"] = [sb1("rsc0", [128, 512], F32), sb1("rsc1", [128, 512], F32)]
    ts_["sqscr"] = [sb1("sqs0", [128, 512], BF16), sb1("sqs1", [128, 512], BF16)]
    v_stage = [sb1("vst0", [128, 16, 65], BF16), sb1("vst1", [128, 16, 65], BF16)]

    ts_["tr_ps"] = ps("tr_ps", [128, 8, 128], BF16)
    s_ps = [ps("sps0", [128, 512], F32), ps("sps1", [128, 512], F32)]
    y_ps = [[ps("yps00", [128, 512], F32), ps("yps01", [128, 512], F32)],
            [ps("yps10", [128, 512], F32), ps("yps11", [128, 512], F32)]]
    ts_["pr_ps"] = [y_ps[0][0], y_ps[0][1]]
    ts_["rt_ps"] = s_ps[0]
    ts_["nb_ps"] = s_ps[0]
    ltr_ps = s_ps[1]

    with nc.Block() as block:
        p = Plan(nc, es)
        ld_id = p.sy(lambda s: s.dma_start(ts_["ident"][:], identin[:, :]))
        ld_if = p.sy(lambda s: s.dma_start(identf[:], identfin[:, :]))
        ld_im = p.sy(lambda s: s.dma_start(indm[:, :, :], indmin[:, :, :]))
        ld_on = p.sy(lambda s: s.dma_start(ones1[:], onesin[:, :]))
        ms0 = p.v(lambda v: v.memset(v_stage[0][:, :, 64:65], 1.0))
        ms1 = p.v(lambda v: v.memset(v_stage[1][:, :, 64:65], 1.0))

        xq_tiled = xq.rearrange("(t p) d -> t p d", p=128)
        evq = emit_ln_stream(p, ts_, 16, xq_tiled, xlnT, [ld_id])
        _, wqT = emit_w_transpose(p, ts_, wqkv[0:D, :], WTk, [ld_id], evq[-1])
        q_ops = emit_proj_fm(p, ts_, WTk, xlnT, QT, TQ, wqT, evq[-1], None, None)
        rq = emit_normsq_half(p, ts_, QT, TQ, indm, 0, [q_ops[-1], ld_im], None)

        _, wkT = emit_w_transpose(p, ts_, wqkv[D:2 * D, :], WTk, [], q_ops[-1])
        _, wvT = emit_w_transpose(p, ts_, wqkv[2 * D:3 * D, :], WTv, [], wkT)
        xb_tiled = xb.rearrange("(t p) d -> t p d", p=128)
        rk = None
        kt_dT = kt_d.rearrange("(dc p) j -> p dc j", p=128)
        v_dT = v_d.rearrange("(tt p) h c -> tt p (h c)", p=128)
        st_ops = []
        vst_reader = [ms0, ms1]
        tr_war = wvT
        for g in range(4):
            evg = emit_ln_stream(p, ts_, 16, xb_tiled[g * 16:(g + 1) * 16], xlnT,
                                 [q_ops[-1], rq], tr_war=tr_war)
            tr_war = evg[-1]
            kg_ops = emit_proj_fm(p, ts_, WTk, xlnT, KTg, 2048, wkT, evg[-1], None,
                                  st_ops[-1] if st_ops else None)
            rk = emit_normsq_half(p, ts_, KTg, 2048, indm, 1, [kg_ops[-1]], rk)
            stk = p.sy(lambda s, g=g: s.dma_start(kt_dT[:, :, g * 2048:(g + 1) * 2048],
                                                  KTg[:, :, :]), [kg_ops[-1]])
            st_ops.append(stk)
            pr_ps = ts_["pr_ps"]
            bank_reader = [kg_ops[-1], kg_ops[-1]]
            k = 0
            for tt in range(16):
                vb = tt % 2
                evs = []
                for fcol in range(2):
                    pb = k % 2
                    k += 1
                    mm = None
                    for dc in range(8):
                        mm = p.t(lambda t_, tt=tt, fcol=fcol, dc=dc, pb=pb: t_.matmul(
                            pr_ps[pb][:, :], xlnT[:, dc, tt * 128:(tt + 1) * 128],
                            WTv[:, dc, fcol * 512:(fcol + 1) * 512],
                            start=(dc == 0), stop=(dc == 7)),
                            ([wvT, evg[-1], bank_reader[pb]] if dc == 0 else []))
                    ev = p.v(lambda v, tt=tt, fcol=fcol, pb=pb, vb=vb: v.tensor_copy(
                        v_stage[vb][:, fcol * 8:(fcol + 1) * 8, 0:64],
                        pr_ps[pb][:, :].rearrange("p (h c) -> p h c", c=64)), [mm, vst_reader[vb]])
                    bank_reader[pb] = ev
                    evs.append(ev)
                stv = p.sy(lambda s, g=g, tt=tt, vb=vb: s.dma_start(
                    v_dT[g * 16 + tt], v_stage[vb].rearrange("p h c -> p (h c)")), evs)
                vst_reader[vb] = stv
                st_ops.append(stv)

        nb_ev = emit_normbound_finish(p, ts_, ones1, rq, rk, 0.125)

        es1.close()
        kt_sb = [sb("kt0", [128, SB_], BF16), sb("kt1", [128, SB_], BF16)]
        vp_sb = [sb("vp0", [128, 64, 130], BF16), sb("vp1", [128, 64, 130], BF16)]
        yT_all = sb("yT_all", [128, 8, TQ], BF16)
        pT_sb = [sb("pt0", [128, 512], BF16), sb("pt1", [128, 512], BF16)]
        lrow = sb("lrow", [1, 512], F32)
        ltok = sb("ltok", [128, 16, 16], F32)
        linv_all = sb("linv", [128, 16, 16], F32)
        xadd = [sb("xadd0", [128, D], F32), sb("xadd1", [128, D], F32)]

        negmh = ts_["negmh"]
        kv_dep = [nb_ev] + st_ops[-3:]
        kt_ld, vp_ld = {}, {}
        kt_reader = [nb_ev, nb_ev]
        vp_reader = [nb_ev, nb_ev]

        def load_pair(pair):
            b = pair % 2
            l1 = p.g(lambda g_, pair=pair, b=b: g_.dma_start(
                kt_sb[b][:, :], kt_d[pair * 128:(pair + 1) * 128, :]),
                kv_dep + [kt_reader[b]])
            l2 = p.g(lambda g_, pair=pair, b=b: g_.dma_start(
                vp_sb[b][:, :, :],
                v_d[:, 2 * pair:2 * pair + 2, :].rearrange("(jt p) h c -> p jt (h c)", p=128)),
                kv_dep + [vp_reader[b]])
            kt_ld[pair] = (l1, l2)

        load_pair(0)
        load_pair(1)

        s_reader = [nb_ev, nb_ev]
        psb_reader = [None, None]
        y_reader = [[None, None], [None, None]]
        ltr_reader = nb_ev
        lrow_reader = None
        yT_reader = None
        last_pair_mm = {}
        for pair in range(8):
            b = pair % 2
            dW = list(kt_ld[pair])
            for qh in range(2):
                pv_last = [[None, None], [None, None]]
                for jt in range(64):
                    for hh in range(2):
                        for qb in range(2):
                            par = (jt * 4 + hh * 2 + qb) % 2
                            q0 = qh * 1024 + qb * 512
                            smm = p.t(lambda t_, pair=pair, b=b, jt=jt, hh=hh, q0=q0, par=par: t_.matmul(
                                s_ps[par][:, :],
                                kt_sb[b][hh * 64:(hh + 1) * 64, jt * 128:(jt + 1) * 128],
                                QT[hh * 64:(hh + 1) * 64, pair, q0:q0 + 512],
                                start=True, stop=True),
                                (dW + [s_reader[par]]) + ([q_ops[-1]] if (pair == 0 and jt == 0) else []))
                            h_ = 2 * pair + hh
                            expo = p.s(lambda s, h_=h_, par=par: s.activation(
                                pT_sb[par][:], s_ps[par][:, :], AF.Exp, scale=0.125,
                                bias=negmh[:, h_:h_ + 1]), [smm, nb_ev, psb_reader[par]])
                            pv = p.t(lambda t_, b=b, jt=jt, hh=hh, qb=qb, par=par: t_.matmul(
                                y_ps[hh][qb][0:65, :], vp_sb[b][:, jt, hh * 65:(hh + 1) * 65],
                                pT_sb[par][:], start=(jt == 0), stop=(jt == 63)),
                                [expo, y_reader[hh][qb]] if jt == 0 else [expo])
                            psb_reader[par] = pv
                            s_reader[par] = expo
                            pv_last[hh][qb] = pv
                for hh in range(2):
                    for qb in range(2):
                        q0 = qh * 1024 + qb * 512
                        h = 2 * pair + hh
                        ye = p.v(lambda v, pair=pair, hh=hh, qb=qb, q0=q0: v.tensor_copy(
                            yT_all[hh * 64:(hh + 1) * 64, pair, q0:q0 + 512],
                            y_ps[hh][qb][0:64, :]), [pv_last[hh][qb], yT_reader])
                        le = p.v(lambda v, hh=hh, qb=qb: v.tensor_copy(
                            lrow[0:1, :], y_ps[hh][qb][64:65, :]), [pv_last[hh][qb], lrow_reader])
                        trs = []
                        for i in range(4):
                            trs.append(p.t(lambda t_, i=i: t_.transpose(
                                ltr_ps[0:128, i:i + 1], lrow[0:1, i * 128:(i + 1) * 128],
                                identf[0:1, 0:1]), [le] + ([ltr_reader] if i == 0 else [])))
                        t0 = q0 // 128
                        lte = p.v(lambda v, t0=t0, h=h: v.tensor_copy(
                            ltok[:, t0:t0 + 4, h:h + 1].rearrange("p a b -> p (a b)"),
                            ltr_ps[0:128, 0:4]), [trs[-1]])
                        ltr_reader = lte
                        lrow_reader = trs[-1]
                        y_reader[hh][qb] = le
                        s_reader[1] = lte
            last_pair_mm[pair] = pv_last[1][1]
            if pair + 2 < 8:
                kt_reader[b] = pv_last[1][1]
                vp_reader[b] = pv_last[1][1]
                load_pair(pair + 2)
        yT_reader = None

        rc = p.v(lambda v: v.reciprocal(linv_all[:, :, :].rearrange("p a b -> p (a b)"),
                                        ltok[:, :, :].rearrange("p a b -> p (a b)")), [ltr_reader])

        ytr_ps = ts_["tr_ps"]
        ytr_reader = None
        xadd_reader = [None, None]
        last_store = None
        for t in range(16):
            ldx = p.sy(lambda s, t=t: s.dma_start(xadd[t % 2][:], xq_tiled[t]), [xadd_reader[t % 2]])
            adds = []
            for pc in range(8):
                tr = p.t(lambda t_, t=t, pc=pc: t_.transpose(
                    ytr_ps[:, pc, :], yT_all[:, pc, t * 128:(t + 1) * 128], ts_["ident"][:]),
                    [rc if (t == 0 and pc == 0) else None, ytr_reader])
                for hh in range(2):
                    h = 2 * pc + hh
                    ad = p.v(lambda v, t=t, pc=pc, hh=hh, h=h: v.scalar_tensor_tensor(
                        xadd[t % 2][:, h * 64:(h + 1) * 64], ytr_ps[:, pc, hh * 64:(hh + 1) * 64],
                        linv_all[:, t, h:h + 1], xadd[t % 2][:, h * 64:(h + 1) * 64],
                        ALU.mult, ALU.add), [tr, ldx, rc])
                    adds.append(ad)
                ytr_reader = adds[-1]
            stv = p.sy(lambda s, t=t: s.dma_start(
                xout.rearrange("(t p) d -> t p d", p=128)[t], xadd[t % 2][:]), adds[-16:])
            xadd_reader[t % 2] = stv
            last_store = stv
        p.sy(lambda s: s.nop(), [last_store], dma=False)
        p.finalize_and_emit(block)
    es.close()
    return nc


def run_attn_l1(xfull, Wqkv):
    nc = _get_nc('attn1', build_attn_l1)
    in_maps = []
    for c in range(NCORES):
        b, q = c // 4, c % 4
        in_maps.append(dict(xq=np.ascontiguousarray(xfull[b, q * 2048:(q + 1) * 2048]),
                            xb=np.ascontiguousarray(xfull[b]), wqkv=Wqkv,
                            identin=_IDENT, identfin=_IDENTF, indmin=_INDM, onesin=_ONES1))
    res = run_bass_kernel_spmd(nc, in_maps, core_ids=list(range(NCORES)))
    out = np.stack([res.results[c]["xout"] for c in range(NCORES)])
    return out.reshape(2, 8192, D)


_IDENT = np.eye(128, dtype=BF)
_IDENTF = np.eye(128, dtype=np.float32)


_NC_CACHE = {}


def _get_nc(key, builder):
    if key not in _NC_CACHE:
        _NC_CACHE[key] = builder()
    return _NC_CACHE[key]


def kernel(**inputs):
    return kernel_fused(**inputs)


def kernel_multilaunch(L0_attn_x, L0_attn_Wq, L0_attn_Wk, L0_attn_Wv, L0_attn_Wo,
           L0_ffn_router_w, L0_ffn_up_w, L0_ffn_gate_w, L0_ffn_down_w,
           L0_ffn_shared_up_w, L0_ffn_shared_gate_w, L0_ffn_shared_down_w,
           L1_attn_W_qkv, L1_ffn_Wg, L1_ffn_W1, L1_ffn_W2, L1_ffn_W3):
    x = np.asarray(L0_attn_x)
    w0 = dict(rw=np.asarray(L0_ffn_router_w), up7=np.asarray(L0_ffn_up_w),
              gate7=np.asarray(L0_ffn_gate_w), down7=np.asarray(L0_ffn_down_w),
              sup=np.asarray(L0_ffn_shared_up_w), sgate=np.asarray(L0_ffn_shared_gate_w),
              sdown=np.asarray(L0_ffn_shared_down_w))
    w1 = dict(rw=np.asarray(L1_ffn_Wg), w1=np.asarray(L1_ffn_W1),
              w2=np.asarray(L1_ffn_W2), w3=np.asarray(L1_ffn_W3))
    x = run_ffn(x, 0, w0)
    x = run_attn_l0(x, np.asarray(L0_attn_Wq), np.asarray(L0_attn_Wk),
                    np.asarray(L0_attn_Wv), np.asarray(L0_attn_Wo))
    x = run_ffn(x, 0, w0)
    x = run_ffn(x, 1, w1)
    x = run_attn_l1(x, np.asarray(L1_attn_W_qkv))
    x = run_ffn(x, 1, w1)
    return x


def _perm_const():
    pm = np.zeros((128, 128), dtype=np.float32)
    for blk in range(2):
        o = blk * 64
        for pr in range(32):
            pm[o + 2 * pr + 1, o + 2 * pr] = -1.0
            pm[o + 2 * pr, o + 2 * pr + 1] = 1.0
    return pm.astype(BF)


_PERM = _perm_const()


def _rope_tables(pos):
    invf = (1.0 / 10000.0 ** (np.arange(0, 64, 2, dtype=np.float32) / 64.0)).astype(np.float32)
    j = np.arange(64)
    freqs = pos[None, :].astype(np.float32) * invf[j % 32][:, None]
    cos = np.cos(freqs)
    sin = np.sin(freqs)
    cosT = np.concatenate([cos, cos], 0).astype(BF)
    sinT = np.concatenate([sin, sin], 0).astype(BF)
    return cosT, sinT


def _l0_masks(edge):
    qi = np.arange(128)[:, None]
    m = np.zeros((128, 3, 384), dtype=np.float32)
    for t in range(3):
        ji = np.arange(384)[None, :]
        allow = (ji - qi >= 1) & (ji - qi <= 256)
        if edge:
            allow = allow & (t * 128 + ji >= 256)
        m[:, t, :] = np.where(allow, 0.0, -1e30)
    return m


def _indm_const():
    im = np.zeros((128, 8, 16), dtype=np.float32)
    for p_ in range(128):
        for qc in range(8):
            im[p_, qc, 2 * qc + (1 if p_ >= 64 else 0)] = 1.0
    return im.astype(BF)


_INDM = _indm_const()
_ONES1 = np.ones((1, 128), dtype=np.float32)


def run_attn_l0(xfull, Wq, Wk, Wv, Wo):
    nc = _get_nc('attn0', build_attn_l0)
    in_maps = []
    for c in range(NCORES):
        b, q = c // 4, c % 4
        own = xfull[b, q * 2048:(q + 1) * 2048]
        if q == 0:
            halo = np.zeros((256, D), dtype=np.float32)
        else:
            halo = xfull[b, q * 2048 - 256:q * 2048]
        xin = np.ascontiguousarray(np.concatenate([halo, own], 0))
        pos = np.maximum(q * 2048 - 256 + np.arange(2304), 0)
        cosT, sinT = _rope_tables(pos)
        in_maps.append(dict(xin=xin, wq=Wq, wk=Wk, wv=Wv, wo=Wo, identin=_IDENT,
                            permin=_PERM, cosin=cosT, sinin=sinT,
                            maskin=_l0_masks(q == 0), indmin=_INDM, onesin=_ONES1))
    res = run_bass_kernel_spmd(nc, in_maps, core_ids=list(range(NCORES)))
    out = np.stack([res.results[c]["xout"] for c in range(NCORES)])
    return out.reshape(2, 8192, D)


def run_ffn(xfull, layer, weights):
    T = 2048
    nc = _get_nc(('ffn', layer), lambda: build_ffn(T, layer))
    xf = np.ascontiguousarray(xfull.reshape(NCORES, T, D))
    in_maps = []
    for c in range(NCORES):
        m = dict(weights)
        m["xin"] = xf[c]
        m["identin"] = _IDENT
        in_maps.append(m)
    res = run_bass_kernel_spmd(nc, in_maps, core_ids=list(range(NCORES)))
    out = np.stack([res.results[c]["xout"] for c in range(NCORES)])
    return out.reshape(2, 8192, D)



_FFN_PHASE_N = [0]


def _alloc_ffn_ts(nc, scope, NT, T):
    _FFN_PHASE_N[0] += 1
    tag = "f%d_" % _FFN_PHASE_N[0]
    sb = lambda name, shape, dt: scope.enter_context(nc.sbuf_tensor(tag + name, shape, dt))
    ps = lambda name, shape, dt: scope.enter_context(nc.psum_tensor(tag + name, shape, dt))
    ts_ = {}
    ts_["xlnT"] = sb("f_xlnT", [128, 8, T], BF16)
    ts_["stats"] = sb("f_stats", [128, NT, 8], F32)
    ts_["sq"] = sb("f_sq", [128, D], F32)
    ts_["xln_scr"] = [sb("f_xl0", [128, D], BF16), sb("f_xl1", [128, D], BF16)]
    ts_["coefs"] = sb("f_coefs", [128, NT, 8], F32)
    ts_["exps_scr"] = sb("f_exps", [128, 8], F32)
    ts_["eq1"] = sb("f_eq1", [128, 8], F32)
    ts_["eq2"] = sb("f_eq2", [128, 8], F32)
    ts_["masked"] = sb("f_msk", [128, 8], F32)
    ts_["rwb"] = sb("f_rwb", [128, 8, 8], BF16)
    ts_["w_up"] = [sb("f_wu0", [128, 8, 512], BF16), sb("f_wu1", [128, 8, 512], BF16)]
    ts_["w_gate"] = [sb("f_wg0", [128, 8, 512], BF16), sb("f_wg1", [128, 8, 512], BF16)]
    ts_["w_down"] = [sb("f_wd0", [128, 4, D], BF16), sb("f_wd1", [128, 4, D], BF16)]
    ts_["hidT"] = sb("f_hidT", [128, 4, T], BF16)
    ts_["silu_sb"] = [sb("f_si0", [128, 512], F32), sb("f_si1", [128, 512], F32)]
    ts_["tr_ps"] = ps("f_trps", [128, 8, 128], BF16)
    ts_["logits_ps"] = ps("f_lgps", [128, 512], F32)
    ts_["g_ps"] = [ps("f_gps0", [128, 512], F32), ps("f_gps1", [128, 512], F32)]
    ts_["u_ps"] = [ps("f_ups0", [128, 512], F32), ps("f_ups1", [128, 512], F32)]
    ts_["d_ps"] = [ps("f_dps0", [128, 512], F32), ps("f_dps1", [128, 512], F32)]
    return ts_


def _emit_ffn_phase(p, nc, scope_mk, x_view, NT, T, layer, wsrc_mk, rw_dram,
                    ident, phase_dep):
    with scope_mk() as scope:
        ts_ = _alloc_ffn_ts(nc, scope, NT, T)
        ts_["ident"] = ident
        ld_rw = p.g(lambda g: g.dma_start(
            ts_["rwb"][:, :, :], rw_dram.rearrange("(c p) e -> p c e", p=128)), phase_dep)
        evac_ops, _ = emit_ln(p, ts_, NT, x_view, ts_["xlnT"], phase_dep)
        if layer == 0:
            coef_ops = emit_router_l0(p, ts_, NT, ts_["xlnT"], ts_["rwb"], evac_ops, ld_rw)
            n_routed, shared_coef = 7, 0.5
        else:
            coef_ops = emit_router_l1(p, ts_, NT, ts_["xlnT"], ts_["rwb"], evac_ops, ld_rw)
            n_routed, shared_coef = 8, None
        wsrc = wsrc_mk()
        last_x = emit_moe(p, ts_, NT, T, ts_["xlnT"], x_view, ts_["coefs"], wsrc,
                          ln_done=[evac_ops[-1], ld_rw] + phase_dep,
                          router_done=[coef_ops[-1]],
                          H=4096 if layer == 0 else 2048,
                          n_routed=n_routed, shared_coef=shared_coef,
                          silu_on_gate=True)
    return last_x


def _emit_attn0_phase(p, nc, es_mk, x_sb, wq, wk, wv, wo, consts_dram, ident,
                      identf, indm, ones1, phase_dep):
    permin, cosin, sinin, maskin = consts_dram
    TQ, TK, NTK = 2048, 2304, 18
    esC = ExitStack()
    sb = lambda name, shape, dt: esC.enter_context(nc.sbuf_tensor(name, shape, dt))
    ps = lambda name, shape, dt: esC.enter_context(nc.psum_tensor(name, shape, dt))
    ts_ = {}
    ts_["ident"] = ident
    xlnT = sb("c_xlnT", [128, 8, TK], BF16)
    ts_["stats"] = sb("c_stats", [128, NTK, 8], F32)
    ts_["sq"] = sb("c_sq", [128, D], F32)
    ts_["xln_scr"] = [sb("c_xl0", [128, D], BF16), sb("c_xl1", [128, D], BF16)]
    perm = sb("c_perm", [128, 128], BF16)
    cosT = sb("c_cos", [128, TK], BF16)
    sinT = sb("c_sin", [128, TK], BF16)
    masks = sb("c_masks", [128, 3, 384], F32)
    y_sb = sb("c_y", [128, 16, D], BF16)
    ts_["negmh"] = sb("c_negmh", [128, 16], F32)
    ts_["nstat"] = sb("c_nstat", [32, 32], F32)
    ts_["ntr"] = sb("c_ntr", [32, 32], F32)
    lbuf = sb("c_lbuf", [128, 16], F32)
    linv = sb("c_linv", [128, 2, 16], F32)
    esC1 = ExitStack()
    sb1 = lambda name, shape, dt: esC1.enter_context(nc.sbuf_tensor(name, shape, dt))
    wstg = sb1("c_wstg", [128, D], BF16)
    WqTp = sb1("c_wqtp", [128, 8, 128], BF16)
    WkTp = sb1("c_wktp", [128, 8, 128], BF16)
    WvTp = sb1("c_wvtp", [128, 8, 128], BF16)
    QTp = sb1("c_qtp", [128, TQ], BF16)
    KTp = sb1("c_ktp", [128, TK], BF16)
    Vp = sb1("c_vp", [128, NTK, 128], BF16)
    ts_["qraw"] = [sb1("c_qr0", [128, 512], BF16), sb1("c_qr1", [128, 512], BF16)]
    ts_["rope_scr"] = [sb1("c_rs0", [128, 512], F32), sb1("c_rs1", [128, 512], F32)]
    ts_["sqscr"] = [sb1("c_sc0", [128, 512], BF16), sb1("c_sc1", [128, 512], BF16)]
    P_sb = [sb1("c_p0", [128, 384], BF16), sb1("c_p1", [128, 384], BF16)]
    PT_sb = [sb1("c_pt0", [128, 3, 128], BF16), sb1("c_pt1", [128, 3, 128], BF16)]

    ts_["tr_ps"] = ps("c_trps", [128, 8, 128], BF16)
    ts_["pr_ps"] = [ps("c_pr0", [128, 512], F32), ps("c_pr1", [128, 512], F32)]
    ts_["rt_ps"] = ps("c_rtps", [128, 512], F32)
    s_ps = [ps("c_sps0", [128, 512], F32), ps("c_sps1", [128, 512], F32)]
    y_ps = [ps("c_yps0", [128, 512], F32), ps("c_yps1", [128, 512], F32)]
    ts_["nb_ps"] = s_ps[0]

    ld_pm = p.sy(lambda s: s.dma_start(perm[:], permin[:, :]), phase_dep)
    ld_cs = p.sy(lambda s: s.dma_start(cosT[:], cosin[:, :]), phase_dep)
    ld_sn = p.sy(lambda s: s.dma_start(sinT[:], sinin[:, :]), phase_dep)
    ld_mk = p.sy(lambda s: s.dma_start(masks[:, :, :], maskin[:, :, :]), phase_dep)
    cdep = [ld_pm, ld_cs, ld_sn]

    evac_ops, _ = emit_ln(p, ts_, NTK, x_sb, xlnT, phase_dep)
    xln_dep = evac_ops[-1]

    state = {"tr_reader": xln_dep, "s_reader": [None, None], "y_reader": [None, None],
             "psb_reader": [None, None], "pt_reader": None, "ysb_reader": None,
             "wstg_reader": None}
    ts_["_nb_ps_reader"] = None

    def wtr_pair(w_dram, WTp, pair, dep0):
        ld = p.g(lambda g: g.dma_start(wstg[:, :], w_dram[pair * 128:(pair + 1) * 128, :]),
                 [state["wstg_reader"]] + dep0)
        trs = []
        for dc in range(8):
            trs.append(p.t(lambda t_, dc=dc: t_.transpose(
                ts_["tr_ps"][:, dc, :], wstg[:, dc * 128:(dc + 1) * 128], ident[:]),
                [ld] + ([state["tr_reader"]] if dc == 0 else [])))
        state["wstg_reader"] = trs[-1]
        ev = p.v(lambda v: v.tensor_copy(WTp[:, :, :], ts_["tr_ps"][:, :, :]), [trs[-1]])
        state["tr_reader"] = ev
        return ev

    def proj_pair_fm(WTp, OUT, Tn, off, wt_dep, rope_on, bank_war):
        slices = [(s, min(512, Tn - s)) for s in range(0, Tn, 512)]
        pr_ps = ts_["pr_ps"]; rt_ps = ts_["rt_ps"]
        qraw = ts_["qraw"]; rsc = ts_["rope_scr"]
        br = [bank_war, bank_war]
        qrr = [None, None]; rtr = [None]; rscr = [None, None]
        k = 0
        out_last = None
        for (s0, sl) in slices:
            pb = k % 2
            k += 1
            mm = None
            for dc in range(8):
                mm = p.t(lambda t_, s0=s0, sl=sl, dc=dc, pb=pb: t_.matmul(
                    pr_ps[pb][:, 0:sl], WTp[:, dc, :], xlnT[:, dc, off + s0:off + s0 + sl],
                    start=(dc == 0), stop=(dc == 7)),
                    ([wt_dep, xln_dep, br[pb]] if dc == 0 else []))
            qr = p.v(lambda v, pb=pb, sl=sl: v.tensor_copy(qraw[pb][:, 0:sl], pr_ps[pb][:, 0:sl]),
                     [mm, qrr[pb]])
            br[pb] = qr
            rt = p.t(lambda t_, pb=pb, sl=sl: t_.matmul(
                rt_ps[:, 0:sl], perm[:], qraw[pb][:, 0:sl], start=True, stop=True),
                [qr, rtr[0]] + (cdep if k == 1 else []))
            t1 = p.v(lambda v, pb=pb, s0=s0, sl=sl: v.tensor_tensor(
                rsc[pb][:, 0:sl], qraw[pb][:, 0:sl], cosT[:, off + s0:off + s0 + sl], ALU.mult),
                [qr, rscr[pb]])
            qrr[pb] = t1
            t2 = p.v(lambda v, pb=pb, s0=s0, sl=sl: v.tensor_tensor(
                rt_ps[:, 0:sl], rt_ps[:, 0:sl], sinT[:, off + s0:off + s0 + sl], ALU.mult), [rt])
            ev = p.v(lambda v, s0=s0, sl=sl, pb=pb: v.tensor_tensor(
                OUT[:, s0:s0 + sl], rsc[pb][:, 0:sl], rt_ps[:, 0:sl], ALU.add), [t2, t1])
            rtr[0] = ev
            rscr[pb] = ev
            out_last = ev
        return out_last

    def normsq_pair(XT, Tn, pair, col, dep):
        slices = [(s, min(512, Tn - s)) for s in range(0, Tn, 512)]
        sqscr = ts_["sqscr"]; nb_ps = ts_["nb_ps"]; nstat = ts_["nstat"]
        run = None
        for i, (s0, sl) in enumerate(slices):
            pb = i % 2
            sq = p.s(lambda s, s0=s0, sl=sl, pb=pb, XT=XT: s.activation(
                sqscr[pb][:, 0:sl], XT[:, s0:s0 + sl], AF.Square), [dep, ts_.get("_sq_r%d" % pb)])
            mm = p.t(lambda t_, pb=pb, sl=sl, pair=pair: t_.matmul(
                nb_ps[0:16, 0:sl], indm[:, pair, :], sqscr[pb][:, 0:sl],
                start=True, stop=True), [sq, ts_["_nb_ps_reader"]])
            ts_["_sq_r%d" % pb] = mm
            rm = p.v(lambda v, col=col, sl=sl: v.reduce_max(
                nstat[0:16, col + 2:col + 3], nb_ps[0:16, 0:sl], axis=AX.X), [mm])
            ts_["_nb_ps_reader"] = rm
            if run is None:
                run = p.v(lambda v, col=col: v.tensor_copy(
                    nstat[0:16, col:col + 1], nstat[0:16, col + 2:col + 3]), [rm])
            else:
                run = p.v(lambda v, col=col: v.tensor_tensor(
                    nstat[0:16, col:col + 1], nstat[0:16, col:col + 1],
                    nstat[0:16, col + 2:col + 3], ALU.max), [rm, run])
        return run

    ysb_last = {}
    for pair in range(8):
        wq_ev = wtr_pair(wq, WqTp, pair, phase_dep)
        wk_ev = wtr_pair(wk, WkTp, pair, [])
        wv_ev = wtr_pair(wv, WvTp, pair, [])
        q_last = proj_pair_fm(WqTp, QTp, TQ, 256, wq_ev, True,
                              state.get("qtp_reader"))
        k_last = proj_pair_fm(WkTp, KTp, TK, 0, wk_ev, True, q_last)
        v_last = None
        vbr = [k_last, k_last]
        for tt in range(NTK):
            pb = tt % 2
            mm = None
            for dc in range(8):
                mm = p.t(lambda t_, tt=tt, dc=dc, pb=pb: t_.matmul(
                    ts_["pr_ps"][pb][:, 0:128], xlnT[:, dc, tt * 128:(tt + 1) * 128],
                    WvTp[:, dc, :], start=(dc == 0), stop=(dc == 7)),
                    ([wv_ev, xln_dep, vbr[pb]] if dc == 0 else []))
            ev = p.v(lambda v, tt=tt, pb=pb: v.tensor_copy(Vp[:, tt, :], ts_["pr_ps"][pb][:, 0:128]),
                     [mm, state.get("vp_reader_%d" % pb)])
            vbr[pb] = ev
            v_last = ev
        rq = normsq_pair(QTp, TQ, pair, 0, q_last)
        rk = normsq_pair(KTp, TK, pair, 1, k_last)
        nb_ev = emit_normbound_finish(p, ts_, ones1, rq, rk, 0.125)
        sr = state["s_reader"]
        if sr[0] is None:
            sr[0] = nb_ev
            sr[1] = nb_ev
        last_se = None
        first_t_dep = [nb_ev, ld_mk, v_last]
        items = [(t, hh) for t in range(16) for hh in range(2)]

        def stage_a(i):
            t, hh = items[i]
            mi = min(t, 2)
            h = 2 * pair + hh
            par = i % 2
            smm = p.t(lambda t_, t=t, hh=hh, par=par: t_.matmul(
                s_ps[par][:, 0:384], QTp[hh * 64:(hh + 1) * 64, t * 128:(t + 1) * 128],
                KTp[hh * 64:(hh + 1) * 64, t * 128:t * 128 + 384],
                start=True, stop=True),
                [sr[par]] + (first_t_dep if i == 0 else []))
            madd = p.v(lambda v, par=par, mi=mi: v.tensor_tensor(
                s_ps[par][:, 0:384], s_ps[par][:, 0:384], masks[:, mi, :], ALU.add),
                [smm, ld_mk])
            expo = p.s(lambda s, h=h, par=par: s.activation(
                P_sb[par][:], s_ps[par][:, 0:384], AF.Exp, scale=0.125,
                bias=ts_["negmh"][:, h:h + 1], accum_out=lbuf[:, h:h + 1]),
                [madd, nb_ev, state["psb_reader"][par], state.get("li_g%d" % par)])
            sr[par] = expo
            li = p.v(lambda v, h=h, t=t: v.reciprocal(linv[:, t % 2, h:h + 1], lbuf[:, h:h + 1]), [expo])
            state["li_g%d" % par] = li
            return (expo, li)

        def stage_b(i, expo):
            par = i % 2
            trs = []
            for jc in range(3):
                trs.append(p.t(lambda t_, par=par, jc=jc: t_.transpose(
                    ts_["tr_ps"][:, jc, :], P_sb[par][:, jc * 128:(jc + 1) * 128], ident[:]),
                    [expo] + ([state["tr_reader"]] if jc == 0 else [])))
            pte = p.v(lambda v, par=par: v.tensor_copy(PT_sb[par][:, :, :], ts_["tr_ps"][:, 0:3, :]),
                      [trs[-1]])
            state["psb_reader"][par] = trs[-1]
            state["tr_reader"] = pte
            return pte

        def stage_c(i, pte, li):
            t, hh = items[i]
            h = 2 * pair + hh
            par = i % 2
            pvs = None
            for jc in range(3):
                pvs = p.t(lambda t_, t=t, hh=hh, par=par, jc=jc: t_.matmul(
                    y_ps[par][:, 0:64], PT_sb[par][:, jc, :],
                    Vp[:, t + jc, hh * 64:(hh + 1) * 64],
                    start=(jc == 0), stop=(jc == 2)),
                    [pte, state["y_reader"][par]] if jc == 0 else [])
            se = p.v(lambda v, t=t, h=h, par=par: v.tensor_scalar(
                y_sb[:, t, h * 64:(h + 1) * 64], y_ps[par][:, 0:64],
                linv[:, t % 2, h:h + 1], None, ALU.mult),
                [pvs, li])
            state["y_reader"][par] = se
            return se

        hist = {}
        for i in range(len(items)):
            hist[i] = {}
            hist[i]["a"] = stage_a(i)
            if i - 1 in hist:
                hist[i - 1]["b"] = stage_b(i - 1, hist[i - 1]["a"][0])
            if i - 2 in hist:
                last_se = stage_c(i - 2, hist[i - 2]["b"], hist[i - 2]["a"][1])
        for j in (len(items) - 2, len(items) - 1):
            if "b" not in hist[j]:
                hist[j]["b"] = stage_b(j, hist[j]["a"][0])
            last_se = stage_c(j, hist[j]["b"], hist[j]["a"][1])
        state["qtp_reader"] = last_se
        state["vp_reader_0"] = last_se
        state["vp_reader_1"] = last_se
    y_done = last_se
    esC1.close()
    wstage4 = sb("c_wst4", [128, 4, D], BF16)
    WoT = sb("c_woT", [128, 8, D], BF16)
    yT_sb = sb("c_yT", [128, 8, 128], BF16)
    ts_["wstage"] = wstage4
    _, woT = emit_w_transpose(p, ts_, wo[:, :], WoT, [y_done], state["tr_reader"])
    wo_ps = ts_["pr_ps"]
    wo_reader = [y_done, y_done]
    ytr_reader = woT
    yT_reader = None
    last_ad = None
    for t in range(16):
        ytrs = []
        for fc in range(8):
            ytrs.append(p.t(lambda t_, t=t, fc=fc: t_.transpose(
                ts_["tr_ps"][:, fc, :], y_sb[:, t, fc * 128:(fc + 1) * 128], ident[:]),
                [y_done] + ([ytr_reader] if fc == 0 else [])))
        yte = p.v(lambda v: v.tensor_copy(yT_sb[:, :, :], ts_["tr_ps"][:, :, :]), [ytrs[-1], yT_reader])
        ytr_reader = yte
        adds = []
        wmm = None
        for dcol in range(2):
            for fc in range(8):
                wmm = p.t(lambda t_, t=t, dcol=dcol, fc=fc: t_.matmul(
                    wo_ps[dcol][:, :], yT_sb[:, fc, :], WoT[:, fc, dcol * 512:(dcol + 1) * 512],
                    start=(fc == 0), stop=(fc == 7)),
                    [yte, wo_reader[dcol]] if fc == 0 else [])
            ad = p.v(lambda v, t=t, dcol=dcol: v.tensor_tensor(
                x_sb[:, t + 2, dcol * 512:(dcol + 1) * 512], wo_ps[dcol][:, :],
                x_sb[:, t + 2, dcol * 512:(dcol + 1) * 512], ALU.add), [wmm])
            wo_reader[dcol] = ad
            adds.append(ad)
        yT_reader = wmm
        last_ad = adds[-1]
    esC.close()
    return last_ad


def _emit_attn1_phase(p, nc, x_own, wqkv, ident, identf, indm, ones1,
                      kv_in, kv_g, KTN, phase_dep):
    TQ = 2048
    esF = ExitStack()
    sb = lambda name, shape, dt: esF.enter_context(nc.sbuf_tensor(name, shape, dt))
    ps = lambda name, shape, dt: esF.enter_context(nc.psum_tensor(name, shape, dt))
    QT = sb("g_QT", [128, 8, TQ], BF16)
    ts_ = {"ident": ident}
    ts_["negmh"] = sb("g_negmh", [128, 16], F32)
    ts_["nstat"] = sb("g_nstat", [32, 32], F32)
    ts_["ntr"] = sb("g_ntr", [32, 32], F32)
    s2_ps = [ps("g_s2ps0", [128, 1024], F32), ps("g_s2ps1", [128, 1024], F32)]
    y_ps = [ps("g_yps0", [128, 512], F32), ps("g_yps1", [128, 512], F32)]
    tr_ps = ps("g_trps", [128, 8, 128], BF16)
    ts_["tr_ps"] = tr_ps
    ts_["pr_ps"] = [y_ps[0], y_ps[1]]
    ts_["rt_ps"] = s2_ps[0]
    ts_["nb_ps"] = s2_ps[0]
    ltr_ps = s2_ps[1]

    esF1 = ExitStack()
    sb1 = lambda name, shape, dt: esF1.enter_context(nc.sbuf_tensor(name, shape, dt))
    xlnT = sb1("g_xlnT", [128, 8, TQ], BF16)
    ts_["stats"] = sb1("g_stats", [128, 16, 8], F32)
    ts_["sq"] = sb1("g_sq", [128, D], F32)
    ts_["xln_scr"] = [sb1("g_xl0", [128, D], BF16), sb1("g_xl1", [128, D], BF16)]
    ts_["wstage"] = sb1("g_wstg", [128, 4, D], BF16)
    WTa = sb1("g_WTa", [128, 8, D], BF16)
    KTc = sb1("g_KTc", [128, TQ], BF16)
    ts_["sqscr"] = [sb1("g_sc0", [128, 512], BF16), sb1("g_sc1", [128, 512], BF16)]
    v_stage = [sb1("g_vst0", [128, 16, 65], BF16), sb1("g_vst1", [128, 16, 65], BF16)]

    ms0 = p.v(lambda v: v.memset(v_stage[0][:, :, 64:65], 1.0), phase_dep)
    ms1 = p.v(lambda v: v.memset(v_stage[1][:, :, 64:65], 1.0), phase_dep)

    evq = emit_ln(p, ts_, 16, x_own, xlnT, phase_dep)[0]
    xln_dep = evq[-1]
    ts_["_nb_sq_reader"] = [None, None]
    ts_["_nb_ps_reader"] = None

    _, wkT = emit_w_transpose(p, ts_, wqkv[D:2 * D, :], WTa, phase_dep, xln_dep)
    kt_inT = kv_in[0:1, 0:KTN].rearrange("a (qc p j) -> (a qc) p j", p=128, j=TQ)
    rk = None
    kbr = [phase_dep[0], phase_dep[0]]
    kst_ops = []
    sqscr = ts_["sqscr"]; nb_ps = ts_["nb_ps"]; nstat = ts_["nstat"]
    ktc_reader = None
    for qc in range(8):
        evs = []
        for i, s0 in enumerate(range(0, TQ, 512)):
            pb = (qc * 4 + i) % 2
            mm = None
            for dc in range(8):
                mm = p.t(lambda t_, qc=qc, s0=s0, dc=dc, pb=pb: t_.matmul(
                    ts_["pr_ps"][pb][:, :], WTa[:, dc, qc * 128:(qc + 1) * 128],
                    xlnT[:, dc, s0:s0 + 512], start=(dc == 0), stop=(dc == 7)),
                    ([wkT, xln_dep, kbr[pb]] if dc == 0 else []))
            ev = p.v(lambda v, s0=s0, pb=pb: v.tensor_copy(KTc[:, s0:s0 + 512],
                                                           ts_["pr_ps"][pb][:, :]),
                     [mm, ktc_reader])
            kbr[pb] = ev
            evs.append(ev)
        stk = p.sy(lambda s, qc=qc: s.dma_start(kt_inT[qc], KTc[:, :]), [evs[-1]])
        kst_ops.append(stk)
        for i, s0 in enumerate(range(0, TQ, 512)):
            pb = i % 2
            sq = p.s(lambda s, s0=s0, pb=pb: s.activation(
                sqscr[pb][:], KTc[:, s0:s0 + 512], AF.Square),
                [evs[-1], ts_["_nb_sq_reader"][pb]])
            mm2 = p.t(lambda t_, qc=qc, pb=pb: t_.matmul(
                nb_ps[0:16, 0:512], indm[:, qc, :], sqscr[pb][:],
                start=True, stop=True), [sq, ts_["_nb_ps_reader"]])
            ts_["_nb_sq_reader"][pb] = mm2
            rm = p.v(lambda v: v.reduce_max(nstat[0:16, 3:4], nb_ps[0:16, 0:512], axis=AX.X), [mm2])
            ts_["_nb_ps_reader"] = rm
            if rk is None:
                rk = p.v(lambda v: v.tensor_copy(nstat[0:16, 1:2], nstat[0:16, 3:4]), [rm])
            else:
                rk = p.v(lambda v: v.tensor_tensor(nstat[0:16, 1:2], nstat[0:16, 1:2],
                                                   nstat[0:16, 3:4], ALU.max), [rm, rk])
        ktc_reader = ts_["_nb_sq_reader"][1]

    _, wvT = emit_w_transpose(p, ts_, wqkv[2 * D:3 * D, :], WTa, [], kbr[0])
    v_inT = kv_in[0:1, KTN:].rearrange("a (tt p c) -> (a tt) p c", p=128, c=1040)
    vst_reader = [ms0, ms1]
    vbr = [kbr[0], kbr[1]]
    vst_ops = []
    k = 0
    for tt in range(16):
        vb = tt % 2
        evs = []
        for fcol in range(2):
            pb = k % 2
            k += 1
            mm = None
            for dc in range(8):
                mm = p.t(lambda t_, tt=tt, fcol=fcol, dc=dc, pb=pb: t_.matmul(
                    ts_["pr_ps"][pb][:, :], xlnT[:, dc, tt * 128:(tt + 1) * 128],
                    WTa[:, dc, fcol * 512:(fcol + 1) * 512],
                    start=(dc == 0), stop=(dc == 7)),
                    ([wvT, xln_dep, vbr[pb]] if dc == 0 else []))
            ev = p.v(lambda v, pb=pb, vb=vb, fcol=fcol: v.tensor_copy(
                v_stage[vb][:, fcol * 8:(fcol + 1) * 8, 0:64],
                ts_["pr_ps"][pb][:, :].rearrange("p (h c) -> p h c", c=64)),
                [mm, vst_reader[vb]])
            vbr[pb] = ev
            evs.append(ev)
        stv = p.sy(lambda s, tt=tt, vb=vb: s.dma_start(
            v_inT[tt], v_stage[vb].rearrange("p h c -> p (h c)")), evs)
        vst_reader[vb] = stv
        vst_ops.append(stv)

    _, wqT = emit_w_transpose(p, ts_, wqkv[0:D, :], WTa, [], vbr[0])
    groups = [[0, 1, 2, 3], [4, 5, 6, 7]]
    cc1 = p.add("gpsimd", lambda g: g.collective_compute(
        "AllGather", ALU.bypass, replica_groups=groups,
        ins=[kv_in.ap().opt()], outs=[kv_g.ap().opt()]),
        [kst_ops[-1], vst_ops[-1]])
    cc2 = cc1
    q_ops = emit_proj_fm(p, ts_, WTa, xlnT, QT, TQ, wqT, xln_dep, None, vbr[1])
    rq = emit_normsq_half(p, ts_, QT, TQ, indm, 0, [q_ops[-1]], None)
    nb_ev = emit_normbound_finish(p, ts_, ones1, rq, rk, 0.125)

    esF1.close()
    kt_sb0 = sb("g_kt0", [128, 8192], BF16)
    vp_sb0 = sb("g_vp0", [128, 64, 130], BF16)
    kt_sb = [kt_sb0, kt_sb0]
    vp_sb = [vp_sb0, vp_sb0]
    yT_all = sb("g_yT", [128, 8, TQ], BF16)
    pT_sb = [sb("g_pt0", [128, 1024], BF16), sb("g_pt1", [128, 1024], BF16)]
    lrow = sb("g_lrow", [1, 512], F32)
    ltok = sb("g_ltok", [128, 16, 16], F32)
    linv_all = ltok

    negmh = ts_["negmh"]
    kv_dep = [nb_ev, cc1, cc2]
    kt_ld = {}
    kt_reader = [nb_ev, nb_ev]
    vp_reader = [nb_ev, nb_ev]

    def ktg_view(r, pair):
        return kv_g[r:r + 1, 0:KTN].rearrange("a (f j) -> (a f) j", j=TQ)[
            pair * 128:(pair + 1) * 128, :]

    def vg_view(r, pair):
        return kv_g[r:r + 1, KTN:].rearrange("a (j c) -> (a j) c", c=1040)[
            :, 2 * pair * 65:(2 * pair + 2) * 65].rearrange("(jt p) c -> p jt c", p=128)

    def load_pair(pair):
        b = pair % 2
        ls = []
        for r in range(4):
            ls.append(p.g(lambda g_, pair=pair, b=b, r=r: g_.dma_start(
                kt_sb[b][:, r * 2048:(r + 1) * 2048], ktg_view(r, pair)),
                kv_dep + [kt_reader[b]] if r == 0 else []))
            ls.append(p.g(lambda g_, pair=pair, b=b, r=r: g_.dma_start(
                vp_sb[b][:, r * 16:(r + 1) * 16, :], vg_view(r, pair)),
                kv_dep + [vp_reader[b]] if r == 0 else []))
        kt_ld[pair] = ls

    load_pair(0)

    s_reader = [nb_ev, nb_ev]
    psb_reader = [None, None]
    y_reader = [None, None]
    ltr_reader = nb_ev
    lrow_reader = None
    first_smm_dep = [q_ops[-1]]
    for pair in range(8):
        b = pair % 2
        dW = list(kt_ld[pair])
        pv_last = [None, None]
        for qh in range(4):
            q0 = qh * 512
            for hh in range(2):
                h_ = 2 * pair + hh
                pend = None
                for k in range(32):
                    par = k % 2
                    smm2 = None
                    for half in range(2):
                        jt = 2 * k + half
                        smm2 = p.t(lambda t_, b=b, jt=jt, hh=hh, q0=q0, par=par, half=half, pair=pair: t_.matmul(
                            s2_ps[par][:, half * 512:(half + 1) * 512],
                            kt_sb[b][hh * 64:(hh + 1) * 64, jt * 128:(jt + 1) * 128],
                            QT[hh * 64:(hh + 1) * 64, pair, q0:q0 + 512],
                            start=True, stop=True),
                            ((dW + [s_reader[par]] + first_smm_dep) if half == 0 else []))
                    expo = p.s(lambda s, h_=h_, par=par: s.activation(
                        pT_sb[par][:], s2_ps[par][:, :], AF.Exp, scale=0.125,
                        bias=negmh[:, h_:h_ + 1]), [smm2, nb_ev, psb_reader[par]])
                    s_reader[par] = expo
                    first_smm_dep = []
                    if pend is not None:
                        e0, k0, par0 = pend
                        pv = None
                        for half in range(2):
                            jt0 = 2 * k0 + half
                            pv = p.t(lambda t_, b=b, jt0=jt0, hh=hh, par0=par0, half=half: t_.matmul(
                                y_ps[hh][0:65, :], vp_sb[b][:, jt0, hh * 65:(hh + 1) * 65],
                                pT_sb[par0][:, half * 512:(half + 1) * 512],
                                start=(jt0 == 0), stop=(jt0 == 63)),
                                ([e0, y_reader[hh]] if jt0 == 0 else ([e0] if half == 0 else [])))
                        psb_reader[par0] = pv
                        pv_last[hh] = pv
                    pend = (expo, k, par)
                e0, k0, par0 = pend
                pv = None
                for half in range(2):
                    jt0 = 2 * k0 + half
                    pv = p.t(lambda t_, b=b, jt0=jt0, hh=hh, par0=par0, half=half: t_.matmul(
                        y_ps[hh][0:65, :], vp_sb[b][:, jt0, hh * 65:(hh + 1) * 65],
                        pT_sb[par0][:, half * 512:(half + 1) * 512],
                        start=(jt0 == 0), stop=(jt0 == 63)),
                        [e0] if half == 0 else [])
                psb_reader[par0] = pv
                pv_last[hh] = pv
                h = 2 * pair + hh
                ye = p.v(lambda v, pair=pair, hh=hh, q0=q0: v.tensor_copy(
                    yT_all[hh * 64:(hh + 1) * 64, pair, q0:q0 + 512],
                    y_ps[hh][0:64, :]), [pv_last[hh]])
                le = p.v(lambda v, hh=hh: v.tensor_copy(
                    lrow[0:1, :], y_ps[hh][64:65, :]), [pv_last[hh], lrow_reader])
                trs = []
                for i in range(4):
                    trs.append(p.t(lambda t_, i=i: t_.transpose(
                        ltr_ps[0:128, i:i + 1], lrow[0:1, i * 128:(i + 1) * 128],
                        identf[0:1, 0:1]), [le] + ([ltr_reader] if i == 0 else [])))
                t0 = q0 // 128
                lte = p.v(lambda v, t0=t0, h=h: v.tensor_copy(
                    ltok[:, t0:t0 + 4, h:h + 1].rearrange("p a b -> p (a b)"),
                    ltr_ps[0:128, 0:4]), [trs[-1]])
                ltr_reader = lte
                lrow_reader = trs[-1]
                y_reader[hh] = le
                s_reader[1] = lte
        if pair + 1 < 8:
            kt_reader[(pair + 1) % 2] = pv_last[1]
            vp_reader[(pair + 1) % 2] = pv_last[1]
            load_pair(pair + 1)

    rc = p.v(lambda v: v.reciprocal(ltok[:, :, :].rearrange("p a b -> p (a b)"),
                                    ltok[:, :, :].rearrange("p a b -> p (a b)")), [ltr_reader])

    ytr_reader = None
    last_ad = None
    for t in range(16):
        for pc in range(8):
            tr = p.t(lambda t_, t=t, pc=pc: t_.transpose(
                tr_ps[:, pc, :], yT_all[:, pc, t * 128:(t + 1) * 128], ident[:]),
                [rc if (t == 0 and pc == 0) else None, ytr_reader])
            for hh in range(2):
                h = 2 * pc + hh
                ad = p.v(lambda v, t=t, pc=pc, hh=hh, h=h: v.scalar_tensor_tensor(
                    x_own[:, t, h * 64:(h + 1) * 64], tr_ps[:, pc, hh * 64:(hh + 1) * 64],
                    linv_all[:, t, h:h + 1], x_own[:, t, h * 64:(h + 1) * 64],
                    ALU.mult, ALU.add), [tr, rc])
                ytr_reader = ad
                last_ad = ad
    esF.close()
    return last_ad


def build_fused(phases="BCDEFG"):
    TQ, TK, NTK = 2048, 2304, 18
    nc = bass.Bass("TRN2", target_bir_lowering=False)
    xin = nc.dram_tensor("xin", [TK, D], F32, kind="ExternalInput")
    wq = nc.dram_tensor("wq", [D, D], F32, kind="ExternalInput")
    wk = nc.dram_tensor("wk", [D, D], F32, kind="ExternalInput")
    wv = nc.dram_tensor("wv", [D, D], F32, kind="ExternalInput")
    wo = nc.dram_tensor("wo", [D, D], F32, kind="ExternalInput")
    rw0 = nc.dram_tensor("rw0", [D, 8], F32, kind="ExternalInput")
    up7 = nc.dram_tensor("up7", [7, D, 4096], F32, kind="ExternalInput")
    gate7 = nc.dram_tensor("gate7", [7, D, 4096], F32, kind="ExternalInput")
    down7 = nc.dram_tensor("down7", [7, 4096, D], F32, kind="ExternalInput")
    sup = nc.dram_tensor("sup", [D, 4096], F32, kind="ExternalInput")
    sgate = nc.dram_tensor("sgate", [D, 4096], F32, kind="ExternalInput")
    sdown = nc.dram_tensor("sdown", [4096, D], F32, kind="ExternalInput")
    wqkv = nc.dram_tensor("wqkv", [3 * D, D], F32, kind="ExternalInput")
    rw1 = nc.dram_tensor("rw1", [D, 8], F32, kind="ExternalInput")
    w1 = nc.dram_tensor("w1", [8, D, 2048], F32, kind="ExternalInput")
    w2 = nc.dram_tensor("w2", [8, D, 2048], F32, kind="ExternalInput")
    w3 = nc.dram_tensor("w3", [8, 2048, D], F32, kind="ExternalInput")
    identin = nc.dram_tensor("identin", [128, 128], BF16, kind="ExternalInput")
    identfin = nc.dram_tensor("identfin", [128, 128], F32, kind="ExternalInput")
    permin = nc.dram_tensor("permin", [128, 128], BF16, kind="ExternalInput")
    cosin = nc.dram_tensor("cosin", [128, TK], BF16, kind="ExternalInput")
    sinin = nc.dram_tensor("sinin", [128, TK], BF16, kind="ExternalInput")
    maskin = nc.dram_tensor("maskin", [128, 3, 384], F32, kind="ExternalInput")
    indmin = nc.dram_tensor("indmin", [128, 8, 16], BF16, kind="ExternalInput")
    onesin = nc.dram_tensor("onesin", [1, 128], F32, kind="ExternalInput")
    xout = nc.dram_tensor("xout", [TQ, D], F32, kind="ExternalOutput")
    KTN = D * TQ
    VN = TQ * 16 * 65
    kv_in = nc.dram_tensor("kv_in", [1, KTN + VN], BF16, kind="Internal")
    kv_g = nc.dram_tensor("kv_g", [4, KTN + VN], BF16, kind="Internal")

    es = ExitStack()
    sb = lambda name, shape, dt: es.enter_context(nc.sbuf_tensor(name, shape, dt))
    x_sb = sb("x", [128, NTK, D], F32)
    ident = sb("ident", [128, 128], BF16)
    identf = sb("identf", [128, 128], F32)
    indm = sb("indm", [128, 8, 16], BF16)
    ones1 = sb("ones1", [1, 128], F32)
    x_own = x_sb[:, 2:18, :]

    with nc.Block() as block:
        p = Plan(nc, es)
        ld_id = p.sy(lambda s: s.dma_start(ident[:], identin[:, :]))
        ld_if = p.sy(lambda s: s.dma_start(identf[:], identfin[:, :]))
        ld_im = p.sy(lambda s: s.dma_start(indm[:, :, :], indmin[:, :, :]))
        ld_on = p.sy(lambda s: s.dma_start(ones1[:], onesin[:, :]))
        ld_x = p.sy(lambda s: s.dma_start(x_sb[:, :, :], xin.rearrange("(t p) d -> p t d", p=128)))

        def wsrc0():
            def f(e, h8):
                hs = slice(h8 * 512, (h8 + 1) * 512)
                if e < 7:
                    return (up7[e, :, hs].rearrange("(c p) h -> p c h", p=128),
                            gate7[e, :, hs].rearrange("(c p) h -> p c h", p=128),
                            down7[e, hs, :].rearrange("(c p) d -> p c d", p=128))
                return (sup[:, hs].rearrange("(c p) h -> p c h", p=128),
                        sgate[:, hs].rearrange("(c p) h -> p c h", p=128),
                        sdown[hs, :].rearrange("(c p) d -> p c d", p=128))
            return f

        def wsrc1():
            def f(e, h8):
                hs = slice(h8 * 512, (h8 + 1) * 512)
                return (w2[e, :, hs].rearrange("(c p) h -> p c h", p=128),
                        w1[e, :, hs].rearrange("(c p) h -> p c h", p=128),
                        w3[e, hs, :].rearrange("(c p) d -> p c d", p=128))
            return f

        scope_mk = ExitStack
        end = ld_x
        if "B" in phases:
            end = _emit_ffn_phase(p, nc, scope_mk, x_sb, NTK, TK, 0, wsrc0, rw0,
                                  ident, [end, ld_id])
        if "C" in phases:
            end = _emit_attn0_phase(p, nc, None, x_sb, wq, wk, wv, wo,
                                    (permin, cosin, sinin, maskin), ident, identf,
                                    indm, ones1, [end, ld_im, ld_on, ld_id])
        if "D" in phases:
            end = _emit_ffn_phase(p, nc, scope_mk, x_own, 16, TQ, 0, wsrc0, rw0,
                                  ident, [end, ld_id])
        if "E" in phases:
            end = _emit_ffn_phase(p, nc, scope_mk, x_own, 16, TQ, 1, wsrc1, rw1,
                                  ident, [end, ld_id])
        if "F" in phases:
            end = _emit_attn1_phase(p, nc, x_own, wqkv, ident, identf, indm, ones1,
                                    kv_in, kv_g, KTN, [end, ld_if, ld_id, ld_im, ld_on])
        if "G" in phases:
            end = _emit_ffn_phase(p, nc, scope_mk, x_own, 16, TQ, 1, wsrc1, rw1,
                                  ident, [end, ld_id])
        st = p.sy(lambda s: s.dma_start(xout.rearrange("(t p) d -> p t d", p=128),
                                        x_own), [end])
        p.sy(lambda s: s.nop(), [st], dma=False)
        p.finalize_and_emit(block)
    es.close()
    return nc


def kernel_fused(L0_attn_x, L0_attn_Wq, L0_attn_Wk, L0_attn_Wv, L0_attn_Wo,
                 L0_ffn_router_w, L0_ffn_up_w, L0_ffn_gate_w, L0_ffn_down_w,
                 L0_ffn_shared_up_w, L0_ffn_shared_gate_w, L0_ffn_shared_down_w,
                 L1_attn_W_qkv, L1_ffn_Wg, L1_ffn_W1, L1_ffn_W2, L1_ffn_W3):
    x = np.asarray(L0_attn_x)
    nc = _get_nc("fused", build_fused)
    in_maps = []
    for c in range(NCORES):
        b, q = c // 4, c % 4
        own = x[b, q * 2048:(q + 1) * 2048]
        if q == 0:
            halo = np.zeros((256, D), dtype=np.float32)
        else:
            halo = x[b, q * 2048 - 256:q * 2048]
        xin = np.ascontiguousarray(np.concatenate([halo, own], 0))
        pos = np.maximum(q * 2048 - 256 + np.arange(2304), 0)
        cosT, sinT = _rope_tables(pos)
        in_maps.append(dict(
            xin=xin, wq=np.asarray(L0_attn_Wq), wk=np.asarray(L0_attn_Wk),
            wv=np.asarray(L0_attn_Wv), wo=np.asarray(L0_attn_Wo),
            rw0=np.asarray(L0_ffn_router_w), up7=np.asarray(L0_ffn_up_w),
            gate7=np.asarray(L0_ffn_gate_w), down7=np.asarray(L0_ffn_down_w),
            sup=np.asarray(L0_ffn_shared_up_w), sgate=np.asarray(L0_ffn_shared_gate_w),
            sdown=np.asarray(L0_ffn_shared_down_w), wqkv=np.asarray(L1_attn_W_qkv),
            rw1=np.asarray(L1_ffn_Wg), w1=np.asarray(L1_ffn_W1),
            w2=np.asarray(L1_ffn_W2), w3=np.asarray(L1_ffn_W3),
            identin=_IDENT, identfin=_IDENTF, permin=_PERM, cosin=cosT, sinin=sinT,
            maskin=_l0_masks(q == 0), indmin=_INDM, onesin=_ONES1))
    res = run_bass_kernel_spmd(nc, in_maps, core_ids=list(range(NCORES)))
    out = np.stack([res.results[c]["xout"] for c in range(NCORES)])
    return out.reshape(2, 8192, D)


# version 07 (final)
# speedup vs baseline: 1.3147x; 1.3147x over previous
import numpy as np
import ml_dtypes
from contextlib import ExitStack

import concourse.bass as bass
import concourse.mybir as mybir
from concourse.bass_utils import run_bass_kernel_spmd

F32 = mybir.dt.float32
BF16 = mybir.dt.bfloat16
AF = mybir.ActivationFunctionType
ALU = mybir.AluOpType
AX = mybir.AxisListType

BF = ml_dtypes.bfloat16

D = 1024
NCORES = 8


ENGS = ("tensor", "scalar", "vector", "sync", "gpsimd")


class _Op:
    __slots__ = ("eng", "fn", "deps", "inc", "gseq", "eseq", "promoted", "val")

    def __init__(self, eng, fn, deps, inc):
        self.eng = eng
        self.fn = fn
        self.deps = deps
        self.inc = inc
        self.promoted = inc == 16
        self.val = None


class Plan:
    def __init__(self, nc, es):
        self.nc = nc
        self.ops = {e: [] for e in ENGS}
        self.all = []
        self.sems = {e: es.enter_context(nc.semaphore(f"sem_{e}")) for e in ENGS}

    def add(self, eng, fn, deps=(), dma=False):
        op = _Op(eng, fn, [d for d in deps if d is not None], 16 if dma else 1)
        if not dma:
            op.promoted = False
        op.gseq = len(self.all)
        op.eseq = len(self.ops[eng])
        self.all.append(op)
        self.ops[eng].append(op)
        for d in op.deps:
            assert d.gseq < op.gseq
            if d.eng == op.eng and op.eseq - d.eseq >= 2:
                continue
            d.promoted = True
        return op

    def t(self, fn, deps=()):
        return self.add("tensor", fn, deps)

    def s(self, fn, deps=()):
        return self.add("scalar", fn, deps)

    def v(self, fn, deps=()):
        return self.add("vector", fn, deps)

    def sy(self, fn, deps=(), dma=True):
        return self.add("sync", fn, deps, dma=dma)

    def g(self, fn, deps=(), dma=True):
        return self.add("gpsimd", fn, deps, dma=dma)

    def finalize_and_emit(self, block):
        for e in ENGS:
            cum = 0
            for op in self.ops[e]:
                if op.promoted:
                    cum += op.inc
                    op.val = cum
        plan = self

        def engine_body(eng):
            def body(handle):
                watermark = {e: 0 for e in ENGS}
                for op in plan.ops[eng]:
                    for d in op.deps:
                        if d.eng == eng and op.eseq - d.eseq >= 2:
                            continue
                        if d.val is None:
                            raise AssertionError("dep not promoted")
                        if d.val > watermark[d.eng]:
                            handle.wait_ge(plan.sems[d.eng], d.val)
                            watermark[d.eng] = d.val
                    ins = op.fn(handle)
                    if op.promoted:
                        ins.then_inc(plan.sems[eng], op.inc)
            return body

        block.tensor(engine_body("tensor"))
        block.scalar(engine_body("scalar"))
        block.vector(engine_body("vector"))
        block.sync(engine_body("sync"))
        block.gpsimd(engine_body("gpsimd"))



def emit_ln(p, ts_, NT, x_sb, xlnT, deps_x):
    nc = p.nc
    stats = ts_["stats"]
    sq = ts_["sq"]
    xb = ts_["xln_scr"]
    tr_ps = ts_["tr_ps"]
    ident = ts_["ident"]

    def st(t, j):
        return stats[:, t, j:j + 1]

    r_sum, r_sq = [], []
    for t in range(NT):
        r_sum.append(p.v(lambda v, t=t: v.reduce_sum(st(t, 0), x_sb[:, t, :], axis=AX.X, negate=True), deps_x))
        r_sq.append(p.s(lambda s, t=t: s.activation(sq[:], x_sb[:, t, :], AF.Square, accum_out=st(t, 1)), deps_x))
    mean_ops = [p.v(lambda v, t=t: v.tensor_scalar(st(t, 2), st(t, 0), 1.0 / D, None, ALU.mult), [r_sum[t]]) for t in range(NT)]
    m2_ops = [p.v(lambda v, t=t: v.tensor_tensor(st(t, 3), st(t, 2), st(t, 2), ALU.mult), [mean_ops[t]]) for t in range(NT)]
    var_ops = [p.v(lambda v, t=t: v.tensor_scalar(st(t, 4), st(t, 1), 1.0 / D, 1e-5, ALU.mult, ALU.add), [r_sq[t]]) for t in range(NT)]
    var2_ops = [p.v(lambda v, t=t: v.tensor_tensor(st(t, 4), st(t, 4), st(t, 3), ALU.subtract), [var_ops[t], m2_ops[t]]) for t in range(NT)]
    lnv_ops = [p.s(lambda s, t=t: s.activation(st(t, 5), st(t, 4), AF.Ln), [var2_ops[t]]) for t in range(NT)]
    rstd_ops = [p.s(lambda s, t=t: s.activation(st(t, 6), st(t, 5), AF.Exp, scale=-0.5), [lnv_ops[t]]) for t in range(NT)]

    evac_ops = []
    xln_ops = []
    tr_ops_prev = None
    evac_prev2 = [None, None]
    for t in range(NT):
        xop = p.v(lambda v, t=t: v.tensor_scalar(
            xb[t % 2][:], x_sb[:, t, :], st(t, 2), st(t, 6), ALU.add, ALU.mult),
            [rstd_ops[t], evac_prev2[t % 2]])
        xln_ops.append(xop)
        trs = []
        for dc in range(8):
            trs.append(p.t(lambda t_, t=t, dc=dc: t_.transpose(
                tr_ps[:, dc, :], xb[t % 2][:, dc * 128:(dc + 1) * 128], ident[:]),
                [xop] + ([tr_ops_prev] if dc == 0 else [])))
        ev = p.v(lambda v, t=t: v.tensor_copy(xlnT[:, :, t * 128:(t + 1) * 128], tr_ps[:, :, :]), [trs[-1]])
        evac_ops.append(ev)
        evac_prev2[t % 2] = ev
        tr_ops_prev = ev
    return evac_ops, xln_ops


def emit_router_l0(p, ts_, NT, xlnT, rwb, evac_ops, rw_dep):
    logits_ps = ts_["logits_ps"]
    stats = ts_["stats"]
    coefs = ts_["coefs"]
    exps_scr = ts_["exps_scr"]

    def st(t, j):
        return stats[:, t, j:j + 1]

    mm_ops = []
    eq_prev = None
    exps_prev = None
    for t in range(NT):
        mms = []
        for dc in range(8):
            mms.append(p.t(lambda t_, t=t, dc=dc: t_.matmul(
                logits_ps[:, 0:8], xlnT[:, dc, t * 128:(t + 1) * 128], rwb[:, dc, 0:8],
                start=(dc == 0), stop=(dc == 7)),
                ([evac_ops[t], rw_dep, eq_prev, exps_prev] if dc == 0 else [])))
        mm = mms[-1]
        nm1 = p.v(lambda v, t=t: v.reduce_max(st(t, 7), logits_ps[:, 0:8], axis=AX.X, negate=True), [mm])
        ex = p.s(lambda s, t=t: s.activation(exps_scr[:], logits_ps[:, 0:8], AF.Exp, bias=st(t, 7), accum_out=st(t, 0)), [nm1, mm])
        t1 = p.v(lambda v, t=t: v.reciprocal(st(t, 1), st(t, 0)), [ex])
        eq = p.v(lambda v, t=t: v.tensor_scalar(coefs[:, t, :], logits_ps[:, 0:8], st(t, 7), 0.0, ALU.add, ALU.is_equal), [nm1, mm])
        cf = p.v(lambda v, t=t: v.tensor_scalar(coefs[:, t, :], coefs[:, t, :], st(t, 1), None, ALU.mult), [t1, eq])
        eq_prev = cf
        exps_prev = ex
        mm_ops.append(cf)
    return mm_ops


def emit_moe(p, ts_, NT, T, xlnT, x_sb, coefs, wsrc, ln_done, router_done,
             H, n_routed, shared_coef, silu_on_gate):
    w_up = ts_["w_up"]; w_gate = ts_["w_gate"]; w_down = ts_["w_down"]
    hidT = ts_["hidT"]
    silu_sb = ts_["silu_sb"]
    g_ps = ts_["g_ps"]; u_ps = ts_["u_ps"]
    d_ps = ts_["d_ps"]
    tslices = [(s, min(512, T - s)) for s in range(0, T, 512)]
    n_exp = n_routed + (1 if shared_coef is not None else 0)

    wload = {}
    buf_last_reader = [None, None]
    seq = [(e, h8) for e in range(n_exp) for h8 in range(H // 512)]

    def load_w(i):
        e, h8 = seq[i]
        b = i % 2
        up_ap, gate_ap, down_ap = wsrc(e, h8)
        dep = [buf_last_reader[b]]
        o1 = p.g(lambda g, b=b, ap=up_ap: g.dma_start(w_up[b][:], ap), dep)
        o2 = p.g(lambda g, b=b, ap=gate_ap: g.dma_start(w_gate[b][:], ap), dep)
        o3 = p.g(lambda g, b=b, ap=down_ap: g.dma_start(w_down[b][:], ap), dep)
        wload[i] = (o1, o2, o3)

    load_w(0)
    load_w(1)

    silu_last = [None, None]
    g_last_reader = [None, None]
    u_last_reader = [None, None]
    d_last_reader = [None, None]
    last_x_op = None

    for i, (e, h8) in enumerate(seq):
        b = i % 2
        dW = list(wload[i])
        hid_ops = []
        step = 0
        for hc in range(4):
            for (s0, sl) in tslices:
                pg = step % 2
                gmm = None
                for dc in range(8):
                    gmm = p.t(lambda t_, b=b, hc=hc, s0=s0, sl=sl, dc=dc, pg=pg: t_.matmul(
                        g_ps[pg][:, 0:sl], w_gate[b][:, dc, hc * 128:(hc + 1) * 128],
                        xlnT[:, dc, s0:s0 + sl],
                        start=(dc == 0), stop=(dc == 7)),
                        (dW + ln_done + [g_last_reader[pg]]) if dc == 0 else [])
                sil = p.s(lambda s, pg=pg, sl=sl: s.activation(silu_sb[pg][:, 0:sl], g_ps[pg][:, 0:sl], AF.Silu),
                          [gmm, silu_last[pg]])
                g_last_reader[pg] = sil
                umm = None
                for dc in range(8):
                    umm = p.t(lambda t_, b=b, hc=hc, s0=s0, sl=sl, dc=dc, pg=pg: t_.matmul(
                        u_ps[pg][:, 0:sl], w_up[b][:, dc, hc * 128:(hc + 1) * 128],
                        xlnT[:, dc, s0:s0 + sl],
                        start=(dc == 0), stop=(dc == 7)),
                        [u_last_reader[pg]] if dc == 0 else [])
                mul = p.v(lambda v, hc=hc, s0=s0, sl=sl, pg=pg: v.tensor_tensor(
                    hidT[:, hc, s0:s0 + sl], u_ps[pg][:, 0:sl], silu_sb[pg][:, 0:sl], ALU.mult),
                    [umm, sil])
                u_last_reader[pg] = mul
                silu_last[pg] = mul
                hid_ops.append(mul)
                step += 1
        hid_done = hid_ops[-1]
        b_last_mm = None
        for tt in range(NT):
            for dcol in range(2):
                pd = dcol
                dmm = None
                for hc in range(4):
                    dmm = p.t(lambda t_, b=b, tt=tt, dcol=dcol, hc=hc, pd=pd: t_.matmul(
                        d_ps[pd][:, :], hidT[:, hc, tt * 128:(tt + 1) * 128],
                        w_down[b][:, hc, dcol * 512:(dcol + 1) * 512],
                        start=(hc == 0), stop=(hc == 3)),
                        ([hid_done, d_last_reader[pd]] + router_done) if hc == 0 else [])
                if e < n_routed:
                    stt = p.v(lambda v, tt=tt, dcol=dcol, pd=pd, e=e: v.scalar_tensor_tensor(
                        x_sb[:, tt, dcol * 512:(dcol + 1) * 512], d_ps[pd][:, :],
                        coefs[:, tt, e:e + 1], x_sb[:, tt, dcol * 512:(dcol + 1) * 512],
                        ALU.mult, ALU.add), [dmm])
                else:
                    stt = p.v(lambda v, tt=tt, dcol=dcol, pd=pd: v.scalar_tensor_tensor(
                        x_sb[:, tt, dcol * 512:(dcol + 1) * 512], d_ps[pd][:, :],
                        float(shared_coef), x_sb[:, tt, dcol * 512:(dcol + 1) * 512],
                        ALU.mult, ALU.add), [dmm])
                d_last_reader[pd] = stt
                last_x_op = stt
                b_last_mm = dmm
        buf_last_reader[b] = b_last_mm
        if i + 2 < len(seq):
            load_w(i + 2)
    return last_x_op


def emit_router_l1(p, ts_, NT, xlnT, rwb, evac_ops, rw_dep):
    logits_ps = ts_["logits_ps"]
    stats = ts_["stats"]
    coefs = ts_["coefs"]
    eq1 = ts_["eq1"]
    masked = ts_["masked"]
    eq2 = ts_["eq2"]

    def st(t, j):
        return stats[:, t, j:j + 1]

    out_ops = []
    last_ps_reader = None
    for t in range(NT):
        mms = []
        for dc in range(8):
            mms.append(p.t(lambda t_, t=t, dc=dc: t_.matmul(
                logits_ps[:, 0:8], xlnT[:, dc, t * 128:(t + 1) * 128], rwb[:, dc, 0:8],
                start=(dc == 0), stop=(dc == 7)),
                ([evac_ops[t], rw_dep, last_ps_reader] if dc == 0 else [])))
        mm = mms[-1]
        nm1 = p.v(lambda v, t=t: v.reduce_max(st(t, 0), logits_ps[:, 0:8], axis=AX.X, negate=True), [mm])
        e1 = p.v(lambda v, t=t: v.tensor_scalar(eq1[:], logits_ps[:, 0:8], st(t, 0), 0.0, ALU.add, ALU.is_equal), [nm1, mm])
        mk = p.v(lambda v, t=t: v.scalar_tensor_tensor(masked[:], eq1[:], -1e30, logits_ps[:, 0:8], ALU.mult, ALU.add), [e1, mm])
        nm2 = p.v(lambda v, t=t: v.reduce_max(st(t, 1), masked[:], axis=AX.X, negate=True), [mk])
        e2 = p.v(lambda v, t=t: v.tensor_scalar(eq2[:], logits_ps[:, 0:8], st(t, 1), 0.0, ALU.add, ALU.is_equal), [nm2, mm])
        last_ps_reader = e2
        dm = p.v(lambda v, t=t: v.tensor_tensor(st(t, 2), st(t, 0), st(t, 1), ALU.subtract), [nm2, nm1])
        z = p.s(lambda s, t=t: s.activation(st(t, 3), st(t, 2), AF.Exp), [dm])
        zp = p.v(lambda v, t=t: v.tensor_scalar(st(t, 4), st(t, 3), 1.0, None, ALU.add), [z])
        w1 = p.v(lambda v, t=t: v.reciprocal(st(t, 5), st(t, 4)), [zp])
        w1h = p.v(lambda v, t=t: v.tensor_scalar(st(t, 6), st(t, 5), 0.5, None, ALU.mult), [w1])
        w2h = p.v(lambda v, t=t: v.tensor_scalar(st(t, 7), st(t, 6), -1.0, 0.5, ALU.mult, ALU.add), [w1h])
        c1 = p.v(lambda v, t=t: v.tensor_scalar(coefs[:, t, :], eq1[:], st(t, 6), None, ALU.mult), [w1h, e1])
        c2 = p.v(lambda v, t=t: v.scalar_tensor_tensor(coefs[:, t, :], eq2[:], st(t, 7), coefs[:, t, :], ALU.mult, ALU.add), [w2h, e2, c1])
        out_ops.append(c2)
    return out_ops



def build_ffn(T, layer):
    NT = T // 128
    H = 4096 if layer == 0 else 2048
    nc = bass.Bass("TRN2", target_bir_lowering=False)
    xin = nc.dram_tensor("xin", [T, D], F32, kind="ExternalInput")
    rw = nc.dram_tensor("rw", [D, 8], F32, kind="ExternalInput")
    if layer == 0:
        up7 = nc.dram_tensor("up7", [7, D, H], F32, kind="ExternalInput")
        gate7 = nc.dram_tensor("gate7", [7, D, H], F32, kind="ExternalInput")
        down7 = nc.dram_tensor("down7", [7, H, D], F32, kind="ExternalInput")
        sup = nc.dram_tensor("sup", [D, H], F32, kind="ExternalInput")
        sgate = nc.dram_tensor("sgate", [D, H], F32, kind="ExternalInput")
        sdown = nc.dram_tensor("sdown", [H, D], F32, kind="ExternalInput")
    else:
        w1 = nc.dram_tensor("w1", [8, D, H], F32, kind="ExternalInput")
        w2 = nc.dram_tensor("w2", [8, D, H], F32, kind="ExternalInput")
        w3 = nc.dram_tensor("w3", [8, H, D], F32, kind="ExternalInput")
    identin = nc.dram_tensor("identin", [128, 128], BF16, kind="ExternalInput")
    xout = nc.dram_tensor("xout", [T, D], F32, kind="ExternalOutput")

    es = ExitStack()
    sb = lambda name, shape, dt: es.enter_context(nc.sbuf_tensor(name, shape, dt))
    ps = lambda name, shape, dt: es.enter_context(nc.psum_tensor(name, shape, dt))

    ts_ = {}
    x_sb = sb("x", [128, NT, D], F32)
    xlnT = sb("xlnT", [128, 8, T], BF16)
    ts_["stats"] = sb("stats", [128, NT, 8], F32)
    ts_["sq"] = sb("sq", [128, D], F32)
    ts_["xln_scr"] = [sb("xlns0", [128, D], BF16), sb("xlns1", [128, D], BF16)]
    ts_["ident"] = sb("ident", [128, 128], BF16)
    ts_["coefs"] = sb("coefs", [128, NT, 8], F32)
    ts_["exps_scr"] = sb("exps_scr", [128, 8], F32)
    ts_["eq1"] = sb("eq1", [128, 8], F32)
    ts_["eq2"] = sb("eq2", [128, 8], F32)
    ts_["masked"] = sb("masked", [128, 8], F32)
    rwb = sb("rwb", [128, 8, 8], BF16)
    ts_["w_up"] = [sb("wup0", [128, 8, 512], BF16), sb("wup1", [128, 8, 512], BF16)]
    ts_["w_gate"] = [sb("wg0", [128, 8, 512], BF16), sb("wg1", [128, 8, 512], BF16)]
    ts_["w_down"] = [sb("wd0", [128, 4, D], BF16), sb("wd1", [128, 4, D], BF16)]
    ts_["hidT"] = sb("hidT", [128, 4, T], BF16)
    ts_["silu_sb"] = [sb("sil0", [128, 512], F32), sb("sil1", [128, 512], F32)]
    ts_["tr_ps"] = ps("tr_ps", [128, 8, 128], BF16)
    ts_["logits_ps"] = ps("logits_ps", [128, 512], F32)
    ts_["g_ps"] = [ps("gps0", [128, 512], F32), ps("gps1", [128, 512], F32)]
    ts_["u_ps"] = [ps("ups0", [128, 512], F32), ps("ups1", [128, 512], F32)]
    ts_["d_ps"] = [ps("dps0", [128, 512], F32), ps("dps1", [128, 512], F32)]

    with nc.Block() as block:
        p = Plan(nc, es)
        ld_x = p.sy(lambda s: s.dma_start(x_sb[:, :, :], xin.rearrange("(t p) d -> p t d", p=128)))
        ld_id = p.sy(lambda s: s.dma_start(ts_["ident"][:], identin[:, :]))
        ld_rw = p.g(lambda g: g.dma_start(rwb[:, :, :], rw.rearrange("(c p) e -> p c e", p=128)))

        evac_ops, _ = emit_ln(p, ts_, NT, x_sb, xlnT, [ld_x, ld_id])
        if layer == 0:
            coef_ops = emit_router_l0(p, ts_, NT, xlnT, rwb, evac_ops, ld_rw)

            def wsrc(e, h8):
                hs = slice(h8 * 512, (h8 + 1) * 512)
                if e < 7:
                    return (up7[e, :, hs].rearrange("(c p) h -> p c h", p=128),
                            gate7[e, :, hs].rearrange("(c p) h -> p c h", p=128),
                            down7[e, hs, :].rearrange("(c p) d -> p c d", p=128))
                return (sup[:, hs].rearrange("(c p) h -> p c h", p=128),
                        sgate[:, hs].rearrange("(c p) h -> p c h", p=128),
                        sdown[hs, :].rearrange("(c p) d -> p c d", p=128))

            n_routed, shared_coef = 7, 0.5
        else:
            coef_ops = emit_router_l1(p, ts_, NT, xlnT, rwb, evac_ops, ld_rw)

            def wsrc(e, h8):
                hs = slice(h8 * 512, (h8 + 1) * 512)
                return (w2[e, :, hs].rearrange("(c p) h -> p c h", p=128),
                        w1[e, :, hs].rearrange("(c p) h -> p c h", p=128),
                        w3[e, hs, :].rearrange("(c p) d -> p c d", p=128))

            n_routed, shared_coef = 8, None

        last_x = emit_moe(p, ts_, NT, T, xlnT, x_sb, ts_["coefs"], wsrc,
                          ln_done=[evac_ops[-1], ld_rw], router_done=[coef_ops[-1]],
                          H=H, n_routed=n_routed, shared_coef=shared_coef,
                          silu_on_gate=True)

        st = p.sy(lambda s: s.dma_start(xout.rearrange("(t p) d -> p t d", p=128), x_sb[:, :, :]), [last_x])
        p.sy(lambda s: s.nop(), [st], dma=False)
        p.finalize_and_emit(block)
    es.close()
    return nc



def emit_ln_stream(p, ts_, NT, xin_tiled, xlnT, deps0, tr_war=None):
    stats = ts_["stats"]
    sq = ts_["sq"]
    xb = ts_["xln_scr"]
    xs = ts_["x_stream"]
    tr_ps = ts_["tr_ps"]
    ident = ts_["ident"]

    def st(t, j):
        return stats[:, t, j:j + 1]

    evac_ops = []
    prev_use = [None, None]
    tr_prev = tr_war
    evac_prev2 = [None, None]
    for t in range(NT):
        ld = p.sy(lambda s, t=t: s.dma_start(xs[t % 2][:], xin_tiled[t]), [prev_use[t % 2]] + deps0)
        rs = p.v(lambda v, t=t: v.reduce_sum(st(t, 0), xs[t % 2][:], axis=AX.X, negate=True), [ld])
        rq = p.s(lambda s, t=t: s.activation(sq[:], xs[t % 2][:], AF.Square, accum_out=st(t, 1)), [ld])
        mn = p.v(lambda v, t=t: v.tensor_scalar(st(t, 2), st(t, 0), 1.0 / D, None, ALU.mult), [rs])
        m2 = p.v(lambda v, t=t: v.tensor_tensor(st(t, 3), st(t, 2), st(t, 2), ALU.mult), [mn])
        va = p.v(lambda v, t=t: v.tensor_scalar(st(t, 4), st(t, 1), 1.0 / D, 1e-5, ALU.mult, ALU.add), [rq])
        vb = p.v(lambda v, t=t: v.tensor_tensor(st(t, 4), st(t, 4), st(t, 3), ALU.subtract), [va, m2])
        ln = p.s(lambda s, t=t: s.activation(st(t, 5), st(t, 4), AF.Ln), [vb])
        rstd = p.s(lambda s, t=t: s.activation(st(t, 6), st(t, 5), AF.Exp, scale=-0.5), [ln])
        xop = p.v(lambda v, t=t: v.tensor_scalar(xb[t % 2][:], xs[t % 2][:], st(t, 2), st(t, 6), ALU.add, ALU.mult),
                  [rstd, evac_prev2[t % 2]])
        prev_use[t % 2] = xop
        trs = []
        for dc in range(8):
            trs.append(p.t(lambda t_, t=t, dc=dc: t_.transpose(
                tr_ps[:, dc, :], xb[t % 2][:, dc * 128:(dc + 1) * 128], ident[:]),
                [xop] + ([tr_prev] if dc == 0 else [])))
        ev = p.v(lambda v, t=t: v.tensor_copy(xlnT[:, :, t * 128:(t + 1) * 128], tr_ps[:, :, :]), [trs[-1]])
        evac_ops.append(ev)
        evac_prev2[t % 2] = ev
        tr_prev = ev
    return evac_ops


def emit_w_transpose(p, ts_, w_dram, WT, deps0, war_dep):
    wstage = ts_["wstage"]
    tr_ps = ts_["tr_ps"]
    ident = ts_["ident"]
    w4 = w_dram.rearrange("(fc p) d -> fc p d", p=128)
    tr_prev = war_dep
    ld_first = None
    for hf in range(2):
        ld = p.g(lambda g, hf=hf: g.dma_start(wstage[:, :, :], w4[hf * 4:(hf + 1) * 4].rearrange("fc p d -> p fc d")),
                 deps0 + [tr_prev])
        if ld_first is None:
            ld_first = ld
        for dc in range(8):
            trs = []
            for fc in range(4):
                trs.append(p.t(lambda t_, dc=dc, fc=fc: t_.transpose(
                    tr_ps[:, fc, :], wstage[:, fc, dc * 128:(dc + 1) * 128], ident[:]),
                    [ld] + ([tr_prev] if fc == 0 else [])))
            ev = p.v(lambda v, dc=dc, hf=hf: v.tensor_copy(
                WT[:, dc, hf * 512:(hf + 1) * 512], tr_ps[:, 0:4, :]), [trs[-1]])
            tr_prev = ev
    return ld_first, tr_prev


def emit_proj_fm(p, ts_, WT, xlnT, OUT, Tn, wt_dep, xln_dep, rope, bank_war):
    pr_ps = ts_["pr_ps"]
    rt_ps = ts_.get("rt_ps")
    qraw = ts_.get("qraw")
    rsc = ts_.get("rope_scr")
    slices = [(s, min(512, Tn - s)) for s in range(0, Tn, 512)]
    out_ops = []
    bank_reader = [bank_war, bank_war]
    qraw_reader = [None, None]
    rt_reader = None
    rsc_reader = [None, None]
    k = 0
    for qc in range(8):
        for (s0, sl) in slices:
            pb = k % 2
            k += 1
            mm = None
            for dc in range(8):
                mm = p.t(lambda t_, qc=qc, s0=s0, sl=sl, dc=dc, pb=pb: t_.matmul(
                    pr_ps[pb][:, 0:sl], WT[:, dc, qc * 128:(qc + 1) * 128],
                    xlnT[:, dc, s0:s0 + sl],
                    start=(dc == 0), stop=(dc == 7)),
                    ([wt_dep, xln_dep, bank_reader[pb]] if dc == 0 else []))
            if rope is None:
                ev = p.v(lambda v, qc=qc, s0=s0, sl=sl, pb=pb: v.tensor_copy(
                    OUT[:, qc, s0:s0 + sl], pr_ps[pb][:, 0:sl]), [mm])
                bank_reader[pb] = ev
                out_ops.append(ev)
                continue
            permb, cosT, sinT, off = rope
            cs = cosT[:, off + s0: off + s0 + sl]
            sn = sinT[:, off + s0: off + s0 + sl]
            qr = p.v(lambda v, pb=pb, sl=sl: v.tensor_copy(qraw[pb][:, 0:sl], pr_ps[pb][:, 0:sl]), [mm, qraw_reader[pb]])
            bank_reader[pb] = qr
            rt = p.t(lambda t_, pb=pb, sl=sl: t_.matmul(rt_ps[:, 0:sl], permb[:], qraw[pb][:, 0:sl], start=True, stop=True),
                     [qr, rt_reader])
            t1 = p.v(lambda v, pb=pb, cs=cs, sl=sl: v.tensor_tensor(rsc[pb][:, 0:sl], qraw[pb][:, 0:sl], cs, ALU.mult),
                     [qr, rsc_reader[pb]])
            qraw_reader[pb] = t1
            t2 = p.v(lambda v, pb=pb, sn=sn, sl=sl: v.tensor_tensor(rt_ps[:, 0:sl], rt_ps[:, 0:sl], sn, ALU.mult), [rt])
            ev = p.v(lambda v, qc=qc, s0=s0, sl=sl, pb=pb: v.tensor_tensor(
                OUT[:, qc, s0:s0 + sl], rsc[pb][:, 0:sl], rt_ps[:, 0:sl], ALU.add), [t2, t1])
            rt_reader = ev
            rsc_reader[pb] = ev
            out_ops.append(ev)
    return out_ops


def emit_normbound(p, ts_, QT, KT, Tq, Tk, indm, ones1, q_deps, k_deps, scale):
    sqscr = ts_["sqscr"]
    nb_ps = ts_["nb_ps"]
    nstat = ts_["nstat"]
    ntr = ts_["ntr"]
    negmh = ts_["negmh"]

    state = {"ps_reader": None}

    def half(T_, XT, deps, col):
        slices = [(s, min(512, T_ - s)) for s in range(0, T_, 512)]
        run = None
        sq_reader = [None, None]
        for (s0, sl) in slices:
            mm = None
            for qc in range(8):
                pb = qc % 2
                sq = p.s(lambda s, qc=qc, s0=s0, sl=sl, pb=pb, XT=XT: s.activation(
                    sqscr[pb][:, 0:sl], XT[:, qc, s0:s0 + sl], AF.Square),
                    deps + [sq_reader[pb]])
                mm = p.t(lambda t_, qc=qc, pb=pb, sl=sl: t_.matmul(
                    nb_ps[0:16, 0:sl], indm[:, qc, :], sqscr[pb][:, 0:sl],
                    start=(qc == 0), stop=(qc == 7)),
                    [sq] + ([state["ps_reader"]] if qc == 0 else []))
                sq_reader[pb] = mm
            rm = p.v(lambda v, col=col, sl=sl: v.reduce_max(
                nstat[0:16, col + 2:col + 3], nb_ps[0:16, 0:sl], axis=AX.X), [mm])
            state["ps_reader"] = rm
            if run is None:
                run = p.v(lambda v, col=col: v.tensor_copy(
                    nstat[0:16, col:col + 1], nstat[0:16, col + 2:col + 3]), [rm])
            else:
                run = p.v(lambda v, col=col: v.tensor_tensor(
                    nstat[0:16, col:col + 1], nstat[0:16, col:col + 1],
                    nstat[0:16, col + 2:col + 3], ALU.max), [rm, run])
        return run

    rq = half(Tq, QT, q_deps, 0)
    rk = half(Tk, KT, k_deps, 1)
    pr = p.v(lambda v: v.tensor_tensor(nstat[0:16, 4:5], nstat[0:16, 0:1], nstat[0:16, 1:2], ALU.mult), [rq, rk])
    ln = p.s(lambda s: s.activation(nstat[0:16, 5:6], nstat[0:16, 4:5], AF.Ln), [pr])
    sq = p.s(lambda s: s.activation(nstat[0:16, 6:7], nstat[0:16, 5:6], AF.Exp, scale=0.5), [ln])
    ng = p.v(lambda v: v.tensor_scalar(nstat[0:16, 0:1], nstat[0:16, 6:7], -float(scale), None, ALU.mult), [sq, pr])
    tr = p.v(lambda v: v.transpose(ntr[:, :], nstat[:, 0:32]), [ng])
    bc = p.t(lambda t_: t_.matmul(nb_ps[:, 0:16], ones1[0:1, :], ntr[0:1, 0:16],
                                  start=True, stop=True), [tr, state["ps_reader"]])
    ev = p.v(lambda v: v.tensor_copy(negmh[:, :], nb_ps[:, 0:16]), [bc])
    return ev


def emit_proj_tok(p, ts_, WT, xlnT, V, NTk, wt_dep, xln_dep, bank_war):
    pr_ps = ts_["pr_ps"]
    out_ops = []
    bank_reader = [bank_war, bank_war]
    k = 0
    for tt in range(NTk):
        for fcol in range(2):
            pb = k % 2
            mm = None
            for dc in range(8):
                mm = p.t(lambda t_, tt=tt, fcol=fcol, dc=dc, pb=pb: t_.matmul(
                    pr_ps[pb][:, :], xlnT[:, dc, tt * 128:(tt + 1) * 128],
                    WT[:, dc, fcol * 512:(fcol + 1) * 512],
                    start=(dc == 0), stop=(dc == 7)),
                    ([wt_dep, xln_dep, bank_reader[pb]] if dc == 0 else []))
            ev = p.v(lambda v, tt=tt, fcol=fcol, pb=pb: v.tensor_copy(
                V[:, tt, fcol * 512:(fcol + 1) * 512], pr_ps[pb][:, :]), [mm])
            bank_reader[pb] = ev
            out_ops.append(ev)
            k += 1
    return out_ops


def build_attn_l0(debug_stage=0):
    TQ, TK = 2048, 2304
    NTQ, NTK = 16, 18
    nc = bass.Bass("TRN2", target_bir_lowering=False)
    xin = nc.dram_tensor("xin", [TK, D], F32, kind="ExternalInput")
    wq = nc.dram_tensor("wq", [D, D], F32, kind="ExternalInput")
    wk = nc.dram_tensor("wk", [D, D], F32, kind="ExternalInput")
    wv = nc.dram_tensor("wv", [D, D], F32, kind="ExternalInput")
    wo = nc.dram_tensor("wo", [D, D], F32, kind="ExternalInput")
    identin = nc.dram_tensor("identin", [128, 128], BF16, kind="ExternalInput")
    permin = nc.dram_tensor("permin", [128, 128], BF16, kind="ExternalInput")
    cosin = nc.dram_tensor("cosin", [128, TK], BF16, kind="ExternalInput")
    sinin = nc.dram_tensor("sinin", [128, TK], BF16, kind="ExternalInput")
    maskin = nc.dram_tensor("maskin", [128, 3, 384], F32, kind="ExternalInput")
    indmin = nc.dram_tensor("indmin", [128, 8, 16], BF16, kind="ExternalInput")
    onesin = nc.dram_tensor("onesin", [1, 128], F32, kind="ExternalInput")
    xout = nc.dram_tensor("xout", [TQ, D], F32, kind="ExternalOutput")

    es = ExitStack()
    es1 = ExitStack()
    sb = lambda name, shape, dt: es.enter_context(nc.sbuf_tensor(name, shape, dt))
    sb1 = lambda name, shape, dt: es1.enter_context(nc.sbuf_tensor(name, shape, dt))
    ps = lambda name, shape, dt: es.enter_context(nc.psum_tensor(name, shape, dt))

    ts_ = {}
    ts_["ident"] = sb("ident", [128, 128], BF16)
    WT = sb("WT", [128, 8, D], BF16)
    QT = sb("QT", [128, 8, TQ], BF16)
    KT = sb("KT", [128, 8, TK], BF16)
    V = sb("V", [128, NTK, D], BF16)
    indm = sb("indm", [128, 8, 16], BF16)
    ones1 = sb("ones1", [1, 128], F32)
    ts_["negmh"] = sb("negmh", [128, 16], F32)
    xlnT = sb1("xlnT", [128, 8, TK], BF16)
    ts_["stats"] = sb1("stats", [128, NTK, 8], F32)
    ts_["sq"] = sb1("sq", [128, D], F32)
    ts_["xln_scr"] = [sb1("xlns0", [128, D], BF16), sb1("xlns1", [128, D], BF16)]
    ts_["x_stream"] = [sb1("xs0", [128, D], F32), sb1("xs1", [128, D], F32)]
    ts_["wstage"] = sb1("wstage", [128, 4, D], BF16)
    perm = sb1("perm", [128, 128], BF16)
    cosT = sb1("cosT", [128, TK], BF16)
    sinT = sb1("sinT", [128, TK], BF16)
    ts_["qraw"] = [sb1("qraw0", [128, 512], BF16), sb1("qraw1", [128, 512], BF16)]
    ts_["rope_scr"] = [sb1("rsc0", [128, 512], F32), sb1("rsc1", [128, 512], F32)]
    ts_["sqscr"] = [sb1("sqs0", [128, 512], BF16), sb1("sqs1", [128, 512], BF16)]
    ts_["nstat"] = sb1("nstat", [32, 32], F32)
    ts_["ntr"] = sb1("ntr", [32, 32], F32)

    ts_["tr_ps"] = ps("tr_ps", [128, 8, 128], BF16)
    ts_["pr_ps"] = [ps("prps0", [128, 512], F32), ps("prps1", [128, 512], F32)]
    ts_["rt_ps"] = ps("rt_ps", [128, 512], F32)
    s_ps = [ps("sps0", [128, 512], F32), ps("sps1", [128, 512], F32)]
    y_ps = [ps("yps0", [128, 512], F32), ps("yps1", [128, 512], F32)]
    ts_["nb_ps"] = s_ps[0]
    ptr_ps = ts_["tr_ps"]
    ytr_ps = ts_["tr_ps"]
    wo_ps = ts_["pr_ps"]

    with nc.Block() as block:
        p = Plan(nc, es)
        ld_id = p.sy(lambda s: s.dma_start(ts_["ident"][:], identin[:, :]))
        ld_pm = p.sy(lambda s: s.dma_start(perm[:], permin[:, :]))
        ld_cs = p.sy(lambda s: s.dma_start(cosT[:], cosin[:, :]))
        ld_sn = p.sy(lambda s: s.dma_start(sinT[:], sinin[:, :]))
        ld_im = p.sy(lambda s: s.dma_start(indm[:, :, :], indmin[:, :, :]))
        ld_on = p.sy(lambda s: s.dma_start(ones1[:], onesin[:, :]))

        xin_tiled = xin.rearrange("(t p) d -> t p d", p=128)
        evac_ops = emit_ln_stream(p, ts_, NTK, xin_tiled, xlnT, [ld_id])
        xln_dep = evac_ops[-1]

        v_ops = q_ops = k_ops = [xln_dep]
        q_dep = k_dep = [xln_dep]
        nb_ev = woT = xln_dep
        if debug_stage == 0 or debug_stage >= 2 or debug_stage == 12:
            _, wvT = emit_w_transpose(p, ts_, wv[:, :], WT, [ld_id], xln_dep)
            v_ops = [wvT]
            if debug_stage != 12:
                v_ops = emit_proj_tok(p, ts_, WT, xlnT, V, NTK, wvT, xln_dep, None)
        if debug_stage == 0 or debug_stage >= 3:
            _, wqT = emit_w_transpose(p, ts_, wq[:, :], WT, [], v_ops[-1])
            q_ops = emit_proj_fm(p, ts_, WT, xlnT[:, :, 256:256 + TQ], QT, TQ, wqT, xln_dep,
                                 (perm, cosT, sinT, 256), v_ops[-1])
            q_dep = [q_ops[-1], ld_pm, ld_cs, ld_sn]
        if debug_stage == 0 or debug_stage >= 4:
            _, wkT = emit_w_transpose(p, ts_, wk[:, :], WT, [], q_ops[-1])
            k_ops = emit_proj_fm(p, ts_, WT, xlnT, KT, TK, wkT, xln_dep,
                                 (perm, cosT, sinT, 0), q_ops[-1])
            k_dep = [k_ops[-1], ld_pm, ld_cs, ld_sn]
        if debug_stage == 0 or debug_stage >= 5:
            _, woT = emit_w_transpose(p, ts_, wo[:, :], WT, [], k_ops[-1])
            nb_ev = emit_normbound(p, ts_, QT, KT, TQ, TK, indm, ones1,
                                   q_dep + [ld_im], k_dep, 0.125)

        if debug_stage:
            if debug_stage == 12:
                dw = nc.dram_tensor("dbg_wt", [128, 8, D], BF16, kind="ExternalOutput")
                dl = p.sy(lambda s: s.dma_start(dw[:, :, :], WT[:, :, :]), [wvT])
                p.sy(lambda s: s.nop(), [dl], dma=False)
                p.finalize_and_emit(block)
                es1.close(); es.close()
                return nc
            dq = nc.dram_tensor("dbg_qt", [128, 8, TQ], BF16, kind="ExternalOutput")
            dk = nc.dram_tensor("dbg_kt", [128, 8, TK], BF16, kind="ExternalOutput")
            dv = nc.dram_tensor("dbg_v", [128, NTK, D], BF16, kind="ExternalOutput")
            dn = nc.dram_tensor("dbg_nm", [128, 16], F32, kind="ExternalOutput")
            dls = []
            if debug_stage >= 2:
                dls.append(p.sy(lambda s: s.dma_start(dv[:, :, :], V[:, :, :]), [v_ops[-1]]))
            if debug_stage >= 3:
                dls.append(p.sy(lambda s: s.dma_start(dq[:, :, :], QT[:, :, :]), q_dep))
            if debug_stage >= 4:
                dls.append(p.sy(lambda s: s.dma_start(dk[:, :, :], KT[:, :, :]), k_dep))
            if debug_stage >= 5:
                dls.append(p.sy(lambda s: s.dma_start(dn[:, :], ts_["negmh"][:, :]), [nb_ev, woT]))
            p.sy(lambda s: s.nop(), [dls[-1] if dls else xln_dep], dma=False)
            p.finalize_and_emit(block)
            es1.close()
            es.close()
            return nc

        es1.close()
        masks = sb("masks", [128, 3, 384], F32)
        P_sb = [sb("psb0", [128, 384], BF16), sb("psb1", [128, 384], BF16)]
        PT_sb = [sb("ptsb0", [128, 3, 128], BF16), sb("ptsb1", [128, 3, 128], BF16)]
        lbuf = sb("lbuf", [128, 16], F32)
        linv = sb("linv", [128, 16], F32)
        y_tile = [sb("yt0", [128, D], BF16), sb("yt1", [128, D], BF16)]
        yT_sb = sb("yTsb", [128, 8, 128], BF16)
        xadd = [sb("xadd0", [128, D], F32), sb("xadd1", [128, D], F32)]
        ld_mk = p.sy(lambda s: s.dma_start(masks[:, :, :], maskin[:, :, :]),
                     [nb_ev, woT, xln_dep, q_ops[-1], k_ops[-1], v_ops[-1]])

        negmh = ts_["negmh"]
        s_reader = [nb_ev, nb_ev]
        y_reader = [None, None]
        pt_reader = {0: woT}
        psb_reader = [None, None]
        yT_reader = None
        ytile_reader = [None, None]
        wo_reader = [k_ops[-1], k_ops[-1]]
        xadd_reader = [None, None]
        store_ops = []
        last_store = None
        for t in range(NTQ):
            mi = min(t, 2)
            ytile_ops = []
            for h in range(16):
                hb = (h % 2) * 64
                qc = h // 2
                smm = p.t(lambda t_, t=t, h=h, hb=hb, qc=qc: t_.matmul(
                    s_ps[h % 2][:, 0:384],
                    QT[hb:hb + 64, qc, t * 128:(t + 1) * 128],
                    KT[hb:hb + 64, qc, t * 128:t * 128 + 384],
                    start=True, stop=True),
                    [s_reader[h % 2]] + (q_dep + k_dep if (t == 0 and h == 0) else []))
                madd = p.v(lambda v, t=t, h=h, mi=mi: v.tensor_tensor(
                    s_ps[h % 2][:, 0:384], s_ps[h % 2][:, 0:384], masks[:, mi, :], ALU.add),
                    [smm, ld_mk])
                expo = p.s(lambda s, t=t, h=h: s.activation(
                    P_sb[h % 2][:], s_ps[h % 2][:, 0:384], AF.Exp, scale=0.125,
                    bias=negmh[:, h:h + 1], accum_out=lbuf[:, h:h + 1]), [madd, nb_ev, psb_reader[h % 2]])
                s_reader[h % 2] = expo
                li = p.v(lambda v, h=h: v.reciprocal(linv[:, h:h + 1], lbuf[:, h:h + 1]), [expo])
                trs = []
                for jc in range(3):
                    trs.append(p.t(lambda t_, h=h, jc=jc: t_.transpose(
                        ptr_ps[:, jc, :], P_sb[h % 2][:, jc * 128:(jc + 1) * 128], ts_["ident"][:]),
                        [expo] + ([pt_reader[0]] if jc == 0 else [])))
                pte = p.v(lambda v, h=h: v.tensor_copy(PT_sb[h % 2][:, :, :], ptr_ps[:, 0:3, :]), [trs[-1]])
                psb_reader[h % 2] = trs[-1]
                pt_reader[0] = pte
                pvs = None
                for jc in range(3):
                    pvs = p.t(lambda t_, t=t, h=h, jc=jc: t_.matmul(
                        y_ps[h % 2][:, 0:64], PT_sb[h % 2][:, jc, :], V[:, t + jc, h * 64:(h + 1) * 64],
                        start=(jc == 0), stop=(jc == 2)),
                        [pte, v_ops[-1], y_reader[h % 2]] if jc == 0 else [])
                se = p.v(lambda v, t=t, h=h: v.tensor_scalar(
                    y_tile[t % 2][:, h * 64:(h + 1) * 64], y_ps[h % 2][:, 0:64],
                    linv[:, h:h + 1], None, ALU.mult), [pvs, li, ytile_reader[t % 2] if h == 0 else None])
                y_reader[h % 2] = se
                ytile_ops.append(se)
            ytrs = []
            for fc in range(8):
                ytrs.append(p.t(lambda t_, t=t, fc=fc: t_.transpose(
                    ytr_ps[:, fc, :], y_tile[t % 2][:, fc * 128:(fc + 1) * 128], ts_["ident"][:]),
                    [ytile_ops[-1]] + ([pt_reader[0]] if fc == 0 else [])))
            yte = p.v(lambda v, t=t: v.tensor_copy(yT_sb[:, :, :], ytr_ps[:, :, :]), [ytrs[-1], yT_reader])
            ytile_reader[t % 2] = ytrs[-1]
            pt_reader[0] = yte
            ldx = p.sy(lambda s, t=t: s.dma_start(
                xadd[t % 2][:], xin_tiled[t + 2]), [xadd_reader[t % 2]])
            adds = []
            for dcol in range(2):
                wmm = None
                for fc in range(8):
                    wmm = p.t(lambda t_, t=t, dcol=dcol, fc=fc: t_.matmul(
                        wo_ps[dcol][:, :], yT_sb[:, fc, :], WT[:, fc, dcol * 512:(dcol + 1) * 512],
                        start=(fc == 0), stop=(fc == 7)),
                        [yte, wo_reader[dcol]] if fc == 0 else [])
                ad = p.v(lambda v, t=t, dcol=dcol: v.tensor_tensor(
                    xadd[t % 2][:, dcol * 512:(dcol + 1) * 512], wo_ps[dcol][:, :],
                    xadd[t % 2][:, dcol * 512:(dcol + 1) * 512], ALU.add), [wmm, ldx])
                wo_reader[dcol] = ad
                adds.append(ad)
            yT_reader = wmm
            stv = p.sy(lambda s, t=t: s.dma_start(
                xout.rearrange("(t p) d -> t p d", p=128)[t], xadd[t % 2][:]), adds)
            xadd_reader[t % 2] = stv
            store_ops.append(stv)
            last_store = stv
        p.sy(lambda s: s.nop(), [last_store], dma=False)
        p.finalize_and_emit(block)
    es.close()
    return nc


def emit_normsq_half(p, ts_, XT, T_, indm, col, deps, run_prev):
    sqscr = ts_["sqscr"]
    nb_ps = ts_["nb_ps"]
    nstat = ts_["nstat"]
    slices = [(s, min(512, T_ - s)) for s in range(0, T_, 512)]
    run = run_prev
    sq_reader = ts_.setdefault("_nb_sq_reader", [None, None])
    for (s0, sl) in slices:
        mm = None
        for qc in range(8):
            pb = qc % 2
            sq = p.s(lambda s, qc=qc, s0=s0, sl=sl, pb=pb: s.activation(
                sqscr[pb][:, 0:sl], XT[:, qc, s0:s0 + sl], AF.Square),
                deps + [sq_reader[pb]])
            mm = p.t(lambda t_, qc=qc, pb=pb, sl=sl: t_.matmul(
                nb_ps[0:16, 0:sl], indm[:, qc, :], sqscr[pb][:, 0:sl],
                start=(qc == 0), stop=(qc == 7)),
                [sq] + ([ts_.get("_nb_ps_reader")] if qc == 0 else []))
            sq_reader[pb] = mm
        rm = p.v(lambda v, col=col, sl=sl: v.reduce_max(
            nstat[0:16, col + 2:col + 3], nb_ps[0:16, 0:sl], axis=AX.X), [mm])
        ts_["_nb_ps_reader"] = rm
        if run is None:
            run = p.v(lambda v, col=col: v.tensor_copy(
                nstat[0:16, col:col + 1], nstat[0:16, col + 2:col + 3]), [rm])
        else:
            run = p.v(lambda v, col=col: v.tensor_tensor(
                nstat[0:16, col:col + 1], nstat[0:16, col:col + 1],
                nstat[0:16, col + 2:col + 3], ALU.max), [rm, run])
    return run


def emit_normbound_finish(p, ts_, ones1, rq, rk, scale):
    nb_ps = ts_["nb_ps"]
    nstat = ts_["nstat"]
    ntr = ts_["ntr"]
    negmh = ts_["negmh"]
    pr = p.v(lambda v: v.tensor_tensor(nstat[0:16, 4:5], nstat[0:16, 0:1], nstat[0:16, 1:2], ALU.mult), [rq, rk])
    ln = p.s(lambda s: s.activation(nstat[0:16, 5:6], nstat[0:16, 4:5], AF.Ln), [pr])
    sq = p.s(lambda s: s.activation(nstat[0:16, 6:7], nstat[0:16, 5:6], AF.Exp, scale=0.5), [ln])
    ng = p.v(lambda v: v.tensor_scalar(nstat[0:16, 0:1], nstat[0:16, 6:7], -float(scale), None, ALU.mult), [sq, pr])
    tr = p.v(lambda v: v.transpose(ntr[:, :], nstat[:, 0:32]), [ng])
    bc = p.t(lambda t_: t_.matmul(nb_ps[:, 0:16], ones1[0:1, :], ntr[0:1, 0:16],
                                  start=True, stop=True), [tr, ts_.get("_nb_ps_reader")])
    ev = p.v(lambda v: v.tensor_copy(negmh[:, :], nb_ps[:, 0:16]), [bc])
    ts_["_nb_ps_reader"] = ev
    return ev


def build_attn_l1():
    TQ = 2048
    SB_ = 8192
    nc = bass.Bass("TRN2", target_bir_lowering=False)
    xq = nc.dram_tensor("xq", [TQ, D], F32, kind="ExternalInput")
    xb = nc.dram_tensor("xb", [SB_, D], F32, kind="ExternalInput")
    wqkv = nc.dram_tensor("wqkv", [3 * D, D], F32, kind="ExternalInput")
    identin = nc.dram_tensor("identin", [128, 128], BF16, kind="ExternalInput")
    identfin = nc.dram_tensor("identfin", [128, 128], F32, kind="ExternalInput")
    indmin = nc.dram_tensor("indmin", [128, 8, 16], BF16, kind="ExternalInput")
    onesin = nc.dram_tensor("onesin", [1, 128], F32, kind="ExternalInput")
    xout = nc.dram_tensor("xout", [TQ, D], F32, kind="ExternalOutput")
    kt_d = nc.dram_tensor("kt_d", [D, SB_], BF16, kind="Internal")
    v_d = nc.dram_tensor("v_d", [SB_, 16, 65], BF16, kind="Internal")

    es = ExitStack()
    es1 = ExitStack()
    sb = lambda name, shape, dt: es.enter_context(nc.sbuf_tensor(name, shape, dt))
    sb1 = lambda name, shape, dt: es1.enter_context(nc.sbuf_tensor(name, shape, dt))
    ps = lambda name, shape, dt: es.enter_context(nc.psum_tensor(name, shape, dt))

    ts_ = {}
    ts_["ident"] = sb("ident", [128, 128], BF16)
    identf = sb("identf", [128, 128], F32)
    QT = sb("QT", [128, 8, TQ], BF16)
    indm = sb("indm", [128, 8, 16], BF16)
    ones1 = sb("ones1", [1, 128], F32)
    ts_["negmh"] = sb("negmh", [128, 16], F32)
    ts_["nstat"] = sb("nstat", [32, 32], F32)
    ts_["ntr"] = sb("ntr", [32, 32], F32)
    xlnT = sb1("xlnT", [128, 8, 2048], BF16)
    KTg = sb1("KTg", [128, 8, 2048], BF16)
    WTk = sb1("WTk", [128, 8, D], BF16)
    WTv = sb1("WTv", [128, 8, D], BF16)
    ts_["stats"] = sb1("stats", [128, 16, 8], F32)
    ts_["sq"] = sb1("sq", [128, D], F32)
    ts_["xln_scr"] = [sb1("xlns0", [128, D], BF16), sb1("xlns1", [128, D], BF16)]
    ts_["x_stream"] = [sb1("xs0", [128, D], F32), sb1("xs1", [128, D], F32)]
    ts_["wstage"] = sb1("wstage", [128, 4, D], BF16)
    ts_["qraw"] = [sb1("qraw0", [128, 512], BF16), sb1("qraw1", [128, 512], BF16)]
    ts_["rope_scr"] = [sb1("rsc0", [128, 512], F32), sb1("rsc1", [128, 512], F32)]
    ts_["sqscr"] = [sb1("sqs0", [128, 512], BF16), sb1("sqs1", [128, 512], BF16)]
    v_stage = [sb1("vst0", [128, 16, 65], BF16), sb1("vst1", [128, 16, 65], BF16)]

    ts_["tr_ps"] = ps("tr_ps", [128, 8, 128], BF16)
    s_ps = [ps("sps0", [128, 512], F32), ps("sps1", [128, 512], F32)]
    y_ps = [[ps("yps00", [128, 512], F32), ps("yps01", [128, 512], F32)],
            [ps("yps10", [128, 512], F32), ps("yps11", [128, 512], F32)]]
    ts_["pr_ps"] = [y_ps[0][0], y_ps[0][1]]
    ts_["rt_ps"] = s_ps[0]
    ts_["nb_ps"] = s_ps[0]
    ltr_ps = s_ps[1]

    with nc.Block() as block:
        p = Plan(nc, es)
        ld_id = p.sy(lambda s: s.dma_start(ts_["ident"][:], identin[:, :]))
        ld_if = p.sy(lambda s: s.dma_start(identf[:], identfin[:, :]))
        ld_im = p.sy(lambda s: s.dma_start(indm[:, :, :], indmin[:, :, :]))
        ld_on = p.sy(lambda s: s.dma_start(ones1[:], onesin[:, :]))
        ms0 = p.v(lambda v: v.memset(v_stage[0][:, :, 64:65], 1.0))
        ms1 = p.v(lambda v: v.memset(v_stage[1][:, :, 64:65], 1.0))

        xq_tiled = xq.rearrange("(t p) d -> t p d", p=128)
        evq = emit_ln_stream(p, ts_, 16, xq_tiled, xlnT, [ld_id])
        _, wqT = emit_w_transpose(p, ts_, wqkv[0:D, :], WTk, [ld_id], evq[-1])
        q_ops = emit_proj_fm(p, ts_, WTk, xlnT, QT, TQ, wqT, evq[-1], None, None)
        rq = emit_normsq_half(p, ts_, QT, TQ, indm, 0, [q_ops[-1], ld_im], None)

        _, wkT = emit_w_transpose(p, ts_, wqkv[D:2 * D, :], WTk, [], q_ops[-1])
        _, wvT = emit_w_transpose(p, ts_, wqkv[2 * D:3 * D, :], WTv, [], wkT)
        xb_tiled = xb.rearrange("(t p) d -> t p d", p=128)
        rk = None
        kt_dT = kt_d.rearrange("(dc p) j -> p dc j", p=128)
        v_dT = v_d.rearrange("(tt p) h c -> tt p (h c)", p=128)
        st_ops = []
        vst_reader = [ms0, ms1]
        tr_war = wvT
        for g in range(4):
            evg = emit_ln_stream(p, ts_, 16, xb_tiled[g * 16:(g + 1) * 16], xlnT,
                                 [q_ops[-1], rq], tr_war=tr_war)
            tr_war = evg[-1]
            kg_ops = emit_proj_fm(p, ts_, WTk, xlnT, KTg, 2048, wkT, evg[-1], None,
                                  st_ops[-1] if st_ops else None)
            rk = emit_normsq_half(p, ts_, KTg, 2048, indm, 1, [kg_ops[-1]], rk)
            stk = p.sy(lambda s, g=g: s.dma_start(kt_dT[:, :, g * 2048:(g + 1) * 2048],
                                                  KTg[:, :, :]), [kg_ops[-1]])
            st_ops.append(stk)
            pr_ps = ts_["pr_ps"]
            bank_reader = [kg_ops[-1], kg_ops[-1]]
            k = 0
            for tt in range(16):
                vb = tt % 2
                evs = []
                for fcol in range(2):
                    pb = k % 2
                    k += 1
                    mm = None
                    for dc in range(8):
                        mm = p.t(lambda t_, tt=tt, fcol=fcol, dc=dc, pb=pb: t_.matmul(
                            pr_ps[pb][:, :], xlnT[:, dc, tt * 128:(tt + 1) * 128],
                            WTv[:, dc, fcol * 512:(fcol + 1) * 512],
                            start=(dc == 0), stop=(dc == 7)),
                            ([wvT, evg[-1], bank_reader[pb]] if dc == 0 else []))
                    ev = p.v(lambda v, tt=tt, fcol=fcol, pb=pb, vb=vb: v.tensor_copy(
                        v_stage[vb][:, fcol * 8:(fcol + 1) * 8, 0:64],
                        pr_ps[pb][:, :].rearrange("p (h c) -> p h c", c=64)), [mm, vst_reader[vb]])
                    bank_reader[pb] = ev
                    evs.append(ev)
                stv = p.sy(lambda s, g=g, tt=tt, vb=vb: s.dma_start(
                    v_dT[g * 16 + tt], v_stage[vb].rearrange("p h c -> p (h c)")), evs)
                vst_reader[vb] = stv
                st_ops.append(stv)

        nb_ev = emit_normbound_finish(p, ts_, ones1, rq, rk, 0.125)

        es1.close()
        kt_sb = [sb("kt0", [128, SB_], BF16), sb("kt1", [128, SB_], BF16)]
        vp_sb = [sb("vp0", [128, 64, 130], BF16), sb("vp1", [128, 64, 130], BF16)]
        yT_all = sb("yT_all", [128, 8, TQ], BF16)
        pT_sb = [sb("pt0", [128, 512], BF16), sb("pt1", [128, 512], BF16)]
        lrow = sb("lrow", [1, 512], F32)
        ltok = sb("ltok", [128, 16, 16], F32)
        linv_all = sb("linv", [128, 16, 16], F32)
        xadd = [sb("xadd0", [128, D], F32), sb("xadd1", [128, D], F32)]

        negmh = ts_["negmh"]
        kv_dep = [nb_ev] + st_ops[-3:]
        kt_ld, vp_ld = {}, {}
        kt_reader = [nb_ev, nb_ev]
        vp_reader = [nb_ev, nb_ev]

        def load_pair(pair):
            b = pair % 2
            l1 = p.g(lambda g_, pair=pair, b=b: g_.dma_start(
                kt_sb[b][:, :], kt_d[pair * 128:(pair + 1) * 128, :]),
                kv_dep + [kt_reader[b]])
            l2 = p.g(lambda g_, pair=pair, b=b: g_.dma_start(
                vp_sb[b][:, :, :],
                v_d[:, 2 * pair:2 * pair + 2, :].rearrange("(jt p) h c -> p jt (h c)", p=128)),
                kv_dep + [vp_reader[b]])
            kt_ld[pair] = (l1, l2)

        load_pair(0)
        load_pair(1)

        s_reader = [nb_ev, nb_ev]
        psb_reader = [None, None]
        y_reader = [[None, None], [None, None]]
        ltr_reader = nb_ev
        lrow_reader = None
        yT_reader = None
        last_pair_mm = {}
        for pair in range(8):
            b = pair % 2
            dW = list(kt_ld[pair])
            for qh in range(2):
                pv_last = [[None, None], [None, None]]
                for jt in range(64):
                    for hh in range(2):
                        for qb in range(2):
                            par = (jt * 4 + hh * 2 + qb) % 2
                            q0 = qh * 1024 + qb * 512
                            smm = p.t(lambda t_, pair=pair, b=b, jt=jt, hh=hh, q0=q0, par=par: t_.matmul(
                                s_ps[par][:, :],
                                kt_sb[b][hh * 64:(hh + 1) * 64, jt * 128:(jt + 1) * 128],
                                QT[hh * 64:(hh + 1) * 64, pair, q0:q0 + 512],
                                start=True, stop=True),
                                (dW + [s_reader[par]]) + ([q_ops[-1]] if (pair == 0 and jt == 0) else []))
                            h_ = 2 * pair + hh
                            expo = p.s(lambda s, h_=h_, par=par: s.activation(
                                pT_sb[par][:], s_ps[par][:, :], AF.Exp, scale=0.125,
                                bias=negmh[:, h_:h_ + 1]), [smm, nb_ev, psb_reader[par]])
                            pv = p.t(lambda t_, b=b, jt=jt, hh=hh, qb=qb, par=par: t_.matmul(
                                y_ps[hh][qb][0:65, :], vp_sb[b][:, jt, hh * 65:(hh + 1) * 65],
                                pT_sb[par][:], start=(jt == 0), stop=(jt == 63)),
                                [expo, y_reader[hh][qb]] if jt == 0 else [expo])
                            psb_reader[par] = pv
                            s_reader[par] = expo
                            pv_last[hh][qb] = pv
                for hh in range(2):
                    for qb in range(2):
                        q0 = qh * 1024 + qb * 512
                        h = 2 * pair + hh
                        ye = p.v(lambda v, pair=pair, hh=hh, qb=qb, q0=q0: v.tensor_copy(
                            yT_all[hh * 64:(hh + 1) * 64, pair, q0:q0 + 512],
                            y_ps[hh][qb][0:64, :]), [pv_last[hh][qb], yT_reader])
                        le = p.v(lambda v, hh=hh, qb=qb: v.tensor_copy(
                            lrow[0:1, :], y_ps[hh][qb][64:65, :]), [pv_last[hh][qb], lrow_reader])
                        trs = []
                        for i in range(4):
                            trs.append(p.t(lambda t_, i=i: t_.transpose(
                                ltr_ps[0:128, i:i + 1], lrow[0:1, i * 128:(i + 1) * 128],
                                identf[0:1, 0:1]), [le] + ([ltr_reader] if i == 0 else [])))
                        t0 = q0 // 128
                        lte = p.v(lambda v, t0=t0, h=h: v.tensor_copy(
                            ltok[:, t0:t0 + 4, h:h + 1].rearrange("p a b -> p (a b)"),
                            ltr_ps[0:128, 0:4]), [trs[-1]])
                        ltr_reader = lte
                        lrow_reader = trs[-1]
                        y_reader[hh][qb] = le
                        s_reader[1] = lte
            last_pair_mm[pair] = pv_last[1][1]
            if pair + 2 < 8:
                kt_reader[b] = pv_last[1][1]
                vp_reader[b] = pv_last[1][1]
                load_pair(pair + 2)
        yT_reader = None

        rc = p.v(lambda v: v.reciprocal(linv_all[:, :, :].rearrange("p a b -> p (a b)"),
                                        ltok[:, :, :].rearrange("p a b -> p (a b)")), [ltr_reader])

        ytr_ps = ts_["tr_ps"]
        ytr_reader = None
        xadd_reader = [None, None]
        last_store = None
        for t in range(16):
            ldx = p.sy(lambda s, t=t: s.dma_start(xadd[t % 2][:], xq_tiled[t]), [xadd_reader[t % 2]])
            adds = []
            for pc in range(8):
                tr = p.t(lambda t_, t=t, pc=pc: t_.transpose(
                    ytr_ps[:, pc, :], yT_all[:, pc, t * 128:(t + 1) * 128], ts_["ident"][:]),
                    [rc if (t == 0 and pc == 0) else None, ytr_reader])
                for hh in range(2):
                    h = 2 * pc + hh
                    ad = p.v(lambda v, t=t, pc=pc, hh=hh, h=h: v.scalar_tensor_tensor(
                        xadd[t % 2][:, h * 64:(h + 1) * 64], ytr_ps[:, pc, hh * 64:(hh + 1) * 64],
                        linv_all[:, t, h:h + 1], xadd[t % 2][:, h * 64:(h + 1) * 64],
                        ALU.mult, ALU.add), [tr, ldx, rc])
                    adds.append(ad)
                ytr_reader = adds[-1]
            stv = p.sy(lambda s, t=t: s.dma_start(
                xout.rearrange("(t p) d -> t p d", p=128)[t], xadd[t % 2][:]), adds[-16:])
            xadd_reader[t % 2] = stv
            last_store = stv
        p.sy(lambda s: s.nop(), [last_store], dma=False)
        p.finalize_and_emit(block)
    es.close()
    return nc


def run_attn_l1(xfull, Wqkv):
    nc = _get_nc('attn1', build_attn_l1)
    in_maps = []
    for c in range(NCORES):
        b, q = c // 4, c % 4
        in_maps.append(dict(xq=np.ascontiguousarray(xfull[b, q * 2048:(q + 1) * 2048]),
                            xb=np.ascontiguousarray(xfull[b]), wqkv=Wqkv,
                            identin=_IDENT, identfin=_IDENTF, indmin=_INDM, onesin=_ONES1))
    res = run_bass_kernel_spmd(nc, in_maps, core_ids=list(range(NCORES)))
    out = np.stack([res.results[c]["xout"] for c in range(NCORES)])
    return out.reshape(2, 8192, D)


_IDENT = np.eye(128, dtype=BF)
_IDENTF = np.eye(128, dtype=np.float32)


_NC_CACHE = {}


def _get_nc(key, builder):
    if key not in _NC_CACHE:
        _NC_CACHE[key] = builder()
    return _NC_CACHE[key]


def kernel(**inputs):
    return kernel_fused(**inputs)


def kernel_multilaunch(L0_attn_x, L0_attn_Wq, L0_attn_Wk, L0_attn_Wv, L0_attn_Wo,
           L0_ffn_router_w, L0_ffn_up_w, L0_ffn_gate_w, L0_ffn_down_w,
           L0_ffn_shared_up_w, L0_ffn_shared_gate_w, L0_ffn_shared_down_w,
           L1_attn_W_qkv, L1_ffn_Wg, L1_ffn_W1, L1_ffn_W2, L1_ffn_W3):
    x = np.asarray(L0_attn_x)
    w0 = dict(rw=np.asarray(L0_ffn_router_w), up7=np.asarray(L0_ffn_up_w),
              gate7=np.asarray(L0_ffn_gate_w), down7=np.asarray(L0_ffn_down_w),
              sup=np.asarray(L0_ffn_shared_up_w), sgate=np.asarray(L0_ffn_shared_gate_w),
              sdown=np.asarray(L0_ffn_shared_down_w))
    w1 = dict(rw=np.asarray(L1_ffn_Wg), w1=np.asarray(L1_ffn_W1),
              w2=np.asarray(L1_ffn_W2), w3=np.asarray(L1_ffn_W3))
    x = run_ffn(x, 0, w0)
    x = run_attn_l0(x, np.asarray(L0_attn_Wq), np.asarray(L0_attn_Wk),
                    np.asarray(L0_attn_Wv), np.asarray(L0_attn_Wo))
    x = run_ffn(x, 0, w0)
    x = run_ffn(x, 1, w1)
    x = run_attn_l1(x, np.asarray(L1_attn_W_qkv))
    x = run_ffn(x, 1, w1)
    return x


def _perm_const():
    pm = np.zeros((128, 128), dtype=np.float32)
    for blk in range(2):
        o = blk * 64
        for pr in range(32):
            pm[o + 2 * pr + 1, o + 2 * pr] = -1.0
            pm[o + 2 * pr, o + 2 * pr + 1] = 1.0
    return pm.astype(BF)


_PERM = _perm_const()


def _rope_tables(pos):
    invf = (1.0 / 10000.0 ** (np.arange(0, 64, 2, dtype=np.float32) / 64.0)).astype(np.float32)
    j = np.arange(64)
    freqs = pos[None, :].astype(np.float32) * invf[j % 32][:, None]
    cos = np.cos(freqs)
    sin = np.sin(freqs)
    cosT = np.concatenate([cos, cos], 0).astype(BF)
    sinT = np.concatenate([sin, sin], 0).astype(BF)
    return cosT, sinT


def _l0_masks(edge):
    qi = np.arange(128)[:, None]
    m = np.zeros((128, 3, 384), dtype=np.float32)
    for t in range(3):
        ji = np.arange(384)[None, :]
        allow = (ji - qi >= 1) & (ji - qi <= 256)
        if edge:
            allow = allow & (t * 128 + ji >= 256)
        m[:, t, :] = np.where(allow, 0.0, -1e30)
    return m


def _indm_const():
    im = np.zeros((128, 8, 16), dtype=np.float32)
    for p_ in range(128):
        for qc in range(8):
            im[p_, qc, 2 * qc + (1 if p_ >= 64 else 0)] = 1.0
    return im.astype(BF)


_INDM = _indm_const()
_ONES1 = np.ones((1, 128), dtype=np.float32)


def run_attn_l0(xfull, Wq, Wk, Wv, Wo):
    nc = _get_nc('attn0', build_attn_l0)
    in_maps = []
    for c in range(NCORES):
        b, q = c // 4, c % 4
        own = xfull[b, q * 2048:(q + 1) * 2048]
        if q == 0:
            halo = np.zeros((256, D), dtype=np.float32)
        else:
            halo = xfull[b, q * 2048 - 256:q * 2048]
        xin = np.ascontiguousarray(np.concatenate([halo, own], 0))
        pos = np.maximum(q * 2048 - 256 + np.arange(2304), 0)
        cosT, sinT = _rope_tables(pos)
        in_maps.append(dict(xin=xin, wq=Wq, wk=Wk, wv=Wv, wo=Wo, identin=_IDENT,
                            permin=_PERM, cosin=cosT, sinin=sinT,
                            maskin=_l0_masks(q == 0), indmin=_INDM, onesin=_ONES1))
    res = run_bass_kernel_spmd(nc, in_maps, core_ids=list(range(NCORES)))
    out = np.stack([res.results[c]["xout"] for c in range(NCORES)])
    return out.reshape(2, 8192, D)


def run_ffn(xfull, layer, weights):
    T = 2048
    nc = _get_nc(('ffn', layer), lambda: build_ffn(T, layer))
    xf = np.ascontiguousarray(xfull.reshape(NCORES, T, D))
    in_maps = []
    for c in range(NCORES):
        m = dict(weights)
        m["xin"] = xf[c]
        m["identin"] = _IDENT
        in_maps.append(m)
    res = run_bass_kernel_spmd(nc, in_maps, core_ids=list(range(NCORES)))
    out = np.stack([res.results[c]["xout"] for c in range(NCORES)])
    return out.reshape(2, 8192, D)



_FFN_PHASE_N = [0]


def _alloc_ffn_ts(nc, scope, NT, T):
    _FFN_PHASE_N[0] += 1
    tag = "f%d_" % _FFN_PHASE_N[0]
    sb = lambda name, shape, dt: scope.enter_context(nc.sbuf_tensor(tag + name, shape, dt))
    ps = lambda name, shape, dt: scope.enter_context(nc.psum_tensor(tag + name, shape, dt))
    ts_ = {}
    ts_["xlnT"] = sb("f_xlnT", [128, 8, T], BF16)
    ts_["stats"] = sb("f_stats", [128, NT, 8], F32)
    ts_["sq"] = sb("f_sq", [128, D], F32)
    ts_["xln_scr"] = [sb("f_xl0", [128, D], BF16), sb("f_xl1", [128, D], BF16)]
    ts_["coefs"] = sb("f_coefs", [128, NT, 8], F32)
    ts_["exps_scr"] = sb("f_exps", [128, 8], F32)
    ts_["eq1"] = sb("f_eq1", [128, 8], F32)
    ts_["eq2"] = sb("f_eq2", [128, 8], F32)
    ts_["masked"] = sb("f_msk", [128, 8], F32)
    ts_["rwb"] = sb("f_rwb", [128, 8, 8], BF16)
    ts_["w_up"] = [sb("f_wu0", [128, 8, 512], BF16), sb("f_wu1", [128, 8, 512], BF16)]
    ts_["w_gate"] = [sb("f_wg0", [128, 8, 512], BF16), sb("f_wg1", [128, 8, 512], BF16)]
    ts_["w_down"] = [sb("f_wd0", [128, 4, D], BF16), sb("f_wd1", [128, 4, D], BF16)]
    ts_["hidT"] = sb("f_hidT", [128, 4, T], BF16)
    ts_["silu_sb"] = [sb("f_si0", [128, 512], F32), sb("f_si1", [128, 512], F32)]
    ts_["tr_ps"] = ps("f_trps", [128, 8, 128], BF16)
    ts_["logits_ps"] = ps("f_lgps", [128, 512], F32)
    ts_["g_ps"] = [ps("f_gps0", [128, 512], F32), ps("f_gps1", [128, 512], F32)]
    ts_["u_ps"] = [ps("f_ups0", [128, 512], F32), ps("f_ups1", [128, 512], F32)]
    ts_["d_ps"] = [ps("f_dps0", [128, 512], F32), ps("f_dps1", [128, 512], F32)]
    return ts_


def _emit_ffn_phase(p, nc, scope_mk, x_view, NT, T, layer, wsrc_mk, rw_dram,
                    ident, phase_dep):
    with scope_mk() as scope:
        ts_ = _alloc_ffn_ts(nc, scope, NT, T)
        ts_["ident"] = ident
        ld_rw = p.g(lambda g: g.dma_start(
            ts_["rwb"][:, :, :], rw_dram.rearrange("(c p) e -> p c e", p=128)), phase_dep)
        evac_ops, _ = emit_ln(p, ts_, NT, x_view, ts_["xlnT"], phase_dep)
        if layer == 0:
            coef_ops = emit_router_l0(p, ts_, NT, ts_["xlnT"], ts_["rwb"], evac_ops, ld_rw)
            n_routed, shared_coef = 7, 0.5
        else:
            coef_ops = emit_router_l1(p, ts_, NT, ts_["xlnT"], ts_["rwb"], evac_ops, ld_rw)
            n_routed, shared_coef = 8, None
        wsrc = wsrc_mk()
        last_x = emit_moe(p, ts_, NT, T, ts_["xlnT"], x_view, ts_["coefs"], wsrc,
                          ln_done=[evac_ops[-1], ld_rw] + phase_dep,
                          router_done=[coef_ops[-1]],
                          H=4096 if layer == 0 else 2048,
                          n_routed=n_routed, shared_coef=shared_coef,
                          silu_on_gate=True)
    return last_x


def _emit_attn0_phase(p, nc, es_mk, x_sb, wq, wk, wv, wo, consts_dram, ident,
                      identf, indm, ones1, phase_dep):
    permin, cosin, sinin, maskin = consts_dram
    TQ, TK, NTK = 2048, 2304, 18
    esC = ExitStack()
    sb = lambda name, shape, dt: esC.enter_context(nc.sbuf_tensor(name, shape, dt))
    ps = lambda name, shape, dt: esC.enter_context(nc.psum_tensor(name, shape, dt))
    ts_ = {}
    ts_["ident"] = ident
    xlnT = sb("c_xlnT", [128, 8, TK], BF16)
    ts_["stats"] = sb("c_stats", [128, NTK, 8], F32)
    ts_["sq"] = sb("c_sq", [128, D], F32)
    ts_["xln_scr"] = [sb("c_xl0", [128, D], BF16), sb("c_xl1", [128, D], BF16)]
    perm = sb("c_perm", [128, 128], BF16)
    cosT = sb("c_cos", [128, TK], BF16)
    sinT = sb("c_sin", [128, TK], BF16)
    masks = sb("c_masks", [128, 3, 384], F32)
    y_sb = sb("c_y", [128, 16, D], BF16)
    ts_["negmh"] = sb("c_negmh", [128, 16], F32)
    ts_["nstat"] = sb("c_nstat", [32, 32], F32)
    ts_["ntr"] = sb("c_ntr", [32, 32], F32)
    lbuf = sb("c_lbuf", [128, 16], F32)
    linv = sb("c_linv", [128, 2, 16], F32)
    esC1 = ExitStack()
    sb1 = lambda name, shape, dt: esC1.enter_context(nc.sbuf_tensor(name, shape, dt))
    wstg = sb1("c_wstg", [128, D], BF16)
    WqTp = sb1("c_wqtp", [128, 8, 128], BF16)
    WkTp = sb1("c_wktp", [128, 8, 128], BF16)
    WvTp = sb1("c_wvtp", [128, 8, 128], BF16)
    QTp = sb1("c_qtp", [128, TQ], BF16)
    KTp = sb1("c_ktp", [128, TK], BF16)
    Vp = sb1("c_vp", [128, NTK, 128], BF16)
    ts_["qraw"] = [sb1("c_qr0", [128, 512], BF16), sb1("c_qr1", [128, 512], BF16)]
    ts_["rope_scr"] = [sb1("c_rs0", [128, 512], F32), sb1("c_rs1", [128, 512], F32)]
    ts_["sqscr"] = [sb1("c_sc0", [128, 512], BF16), sb1("c_sc1", [128, 512], BF16)]
    P_sb = [sb1("c_p0", [128, 384], BF16), sb1("c_p1", [128, 384], BF16)]
    PT_sb = [sb1("c_pt0", [128, 3, 128], BF16), sb1("c_pt1", [128, 3, 128], BF16)]

    ts_["tr_ps"] = ps("c_trps", [128, 8, 128], BF16)
    ts_["pr_ps"] = [ps("c_pr0", [128, 512], F32), ps("c_pr1", [128, 512], F32)]
    ts_["rt_ps"] = ps("c_rtps", [128, 512], F32)
    s_ps = [ps("c_sps0", [128, 512], F32), ps("c_sps1", [128, 512], F32)]
    y_ps = [ps("c_yps0", [128, 512], F32), ps("c_yps1", [128, 512], F32)]
    ts_["nb_ps"] = s_ps[0]

    ld_pm = p.sy(lambda s: s.dma_start(perm[:], permin[:, :]), phase_dep)
    ld_cs = p.sy(lambda s: s.dma_start(cosT[:], cosin[:, :]), phase_dep)
    ld_sn = p.sy(lambda s: s.dma_start(sinT[:], sinin[:, :]), phase_dep)
    ld_mk = p.sy(lambda s: s.dma_start(masks[:, :, :], maskin[:, :, :]), phase_dep)
    cdep = [ld_pm, ld_cs, ld_sn]

    evac_ops, _ = emit_ln(p, ts_, NTK, x_sb, xlnT, phase_dep)
    xln_dep = evac_ops[-1]

    state = {"tr_reader": xln_dep, "s_reader": [None, None], "y_reader": [None, None],
             "psb_reader": [None, None], "pt_reader": None, "ysb_reader": None,
             "wstg_reader": None}
    ts_["_nb_ps_reader"] = None

    def wtr_pair(w_dram, WTp, pair, dep0):
        ld = p.g(lambda g: g.dma_start(wstg[:, :], w_dram[pair * 128:(pair + 1) * 128, :]),
                 [state["wstg_reader"]] + dep0)
        trs = []
        for dc in range(8):
            trs.append(p.t(lambda t_, dc=dc: t_.transpose(
                ts_["tr_ps"][:, dc, :], wstg[:, dc * 128:(dc + 1) * 128], ident[:]),
                [ld] + ([state["tr_reader"]] if dc == 0 else [])))
        state["wstg_reader"] = trs[-1]
        ev = p.v(lambda v: v.tensor_copy(WTp[:, :, :], ts_["tr_ps"][:, :, :]), [trs[-1]])
        state["tr_reader"] = ev
        return ev

    def proj_pair_fm(WTp, OUT, Tn, off, wt_dep, rope_on, bank_war):
        slices = [(s, min(512, Tn - s)) for s in range(0, Tn, 512)]
        pr_ps = ts_["pr_ps"]; rt_ps = ts_["rt_ps"]
        qraw = ts_["qraw"]; rsc = ts_["rope_scr"]
        br = [bank_war, bank_war]
        qrr = [None, None]; rtr = [None]; rscr = [None, None]
        k = 0
        out_last = None
        for (s0, sl) in slices:
            pb = k % 2
            k += 1
            mm = None
            for dc in range(8):
                mm = p.t(lambda t_, s0=s0, sl=sl, dc=dc, pb=pb: t_.matmul(
                    pr_ps[pb][:, 0:sl], WTp[:, dc, :], xlnT[:, dc, off + s0:off + s0 + sl],
                    start=(dc == 0), stop=(dc == 7)),
                    ([wt_dep, xln_dep, br[pb]] if dc == 0 else []))
            qr = p.v(lambda v, pb=pb, sl=sl: v.tensor_copy(qraw[pb][:, 0:sl], pr_ps[pb][:, 0:sl]),
                     [mm, qrr[pb]])
            br[pb] = qr
            rt = p.t(lambda t_, pb=pb, sl=sl: t_.matmul(
                rt_ps[:, 0:sl], perm[:], qraw[pb][:, 0:sl], start=True, stop=True),
                [qr, rtr[0]] + (cdep if k == 1 else []))
            t1 = p.v(lambda v, pb=pb, s0=s0, sl=sl: v.tensor_tensor(
                rsc[pb][:, 0:sl], qraw[pb][:, 0:sl], cosT[:, off + s0:off + s0 + sl], ALU.mult),
                [qr, rscr[pb]])
            qrr[pb] = t1
            t2 = p.v(lambda v, pb=pb, s0=s0, sl=sl: v.tensor_tensor(
                rt_ps[:, 0:sl], rt_ps[:, 0:sl], sinT[:, off + s0:off + s0 + sl], ALU.mult), [rt])
            ev = p.v(lambda v, s0=s0, sl=sl, pb=pb: v.tensor_tensor(
                OUT[:, s0:s0 + sl], rsc[pb][:, 0:sl], rt_ps[:, 0:sl], ALU.add), [t2, t1])
            rtr[0] = ev
            rscr[pb] = ev
            out_last = ev
        return out_last

    def normsq_pair(XT, Tn, pair, col, dep):
        slices = [(s, min(512, Tn - s)) for s in range(0, Tn, 512)]
        sqscr = ts_["sqscr"]; nb_ps = ts_["nb_ps"]; nstat = ts_["nstat"]
        run = None
        for i, (s0, sl) in enumerate(slices):
            pb = i % 2
            sq = p.s(lambda s, s0=s0, sl=sl, pb=pb, XT=XT: s.activation(
                sqscr[pb][:, 0:sl], XT[:, s0:s0 + sl], AF.Square), [dep, ts_.get("_sq_r%d" % pb)])
            mm = p.t(lambda t_, pb=pb, sl=sl, pair=pair: t_.matmul(
                nb_ps[0:16, 0:sl], indm[:, pair, :], sqscr[pb][:, 0:sl],
                start=True, stop=True), [sq, ts_["_nb_ps_reader"]])
            ts_["_sq_r%d" % pb] = mm
            rm = p.v(lambda v, col=col, sl=sl: v.reduce_max(
                nstat[0:16, col + 2:col + 3], nb_ps[0:16, 0:sl], axis=AX.X), [mm])
            ts_["_nb_ps_reader"] = rm
            if run is None:
                run = p.v(lambda v, col=col: v.tensor_copy(
                    nstat[0:16, col:col + 1], nstat[0:16, col + 2:col + 3]), [rm])
            else:
                run = p.v(lambda v, col=col: v.tensor_tensor(
                    nstat[0:16, col:col + 1], nstat[0:16, col:col + 1],
                    nstat[0:16, col + 2:col + 3], ALU.max), [rm, run])
        return run

    ysb_last = {}
    for pair in range(8):
        wq_ev = wtr_pair(wq, WqTp, pair, phase_dep)
        wk_ev = wtr_pair(wk, WkTp, pair, [])
        wv_ev = wtr_pair(wv, WvTp, pair, [])
        q_last = proj_pair_fm(WqTp, QTp, TQ, 256, wq_ev, True,
                              state.get("qtp_reader"))
        k_last = proj_pair_fm(WkTp, KTp, TK, 0, wk_ev, True, q_last)
        v_last = None
        vbr = [k_last, k_last]
        for tt in range(NTK):
            pb = tt % 2
            mm = None
            for dc in range(8):
                mm = p.t(lambda t_, tt=tt, dc=dc, pb=pb: t_.matmul(
                    ts_["pr_ps"][pb][:, 0:128], xlnT[:, dc, tt * 128:(tt + 1) * 128],
                    WvTp[:, dc, :], start=(dc == 0), stop=(dc == 7)),
                    ([wv_ev, xln_dep, vbr[pb]] if dc == 0 else []))
            ev = p.v(lambda v, tt=tt, pb=pb: v.tensor_copy(Vp[:, tt, :], ts_["pr_ps"][pb][:, 0:128]),
                     [mm, state.get("vp_reader_%d" % pb)])
            vbr[pb] = ev
            v_last = ev
        rq = normsq_pair(QTp, TQ, pair, 0, q_last)
        rk = normsq_pair(KTp, TK, pair, 1, k_last)
        nb_ev = emit_normbound_finish(p, ts_, ones1, rq, rk, 0.125)
        sr = state["s_reader"]
        if sr[0] is None:
            sr[0] = nb_ev
            sr[1] = nb_ev
        last_se = None
        first_t_dep = [nb_ev, ld_mk, v_last]
        items = [(t, hh) for t in range(16) for hh in range(2)]

        def stage_a(i):
            t, hh = items[i]
            mi = min(t, 2)
            h = 2 * pair + hh
            par = i % 2
            smm = p.t(lambda t_, t=t, hh=hh, par=par: t_.matmul(
                s_ps[par][:, 0:384], QTp[hh * 64:(hh + 1) * 64, t * 128:(t + 1) * 128],
                KTp[hh * 64:(hh + 1) * 64, t * 128:t * 128 + 384],
                start=True, stop=True),
                [sr[par]] + (first_t_dep if i == 0 else []))
            madd = p.v(lambda v, par=par, mi=mi: v.tensor_tensor(
                s_ps[par][:, 0:384], s_ps[par][:, 0:384], masks[:, mi, :], ALU.add),
                [smm, ld_mk])
            expo = p.s(lambda s, h=h, par=par: s.activation(
                P_sb[par][:], s_ps[par][:, 0:384], AF.Exp, scale=0.125,
                bias=ts_["negmh"][:, h:h + 1], accum_out=lbuf[:, h:h + 1]),
                [madd, nb_ev, state["psb_reader"][par], state.get("li_g%d" % par)])
            sr[par] = expo
            li = p.v(lambda v, h=h, t=t: v.reciprocal(linv[:, t % 2, h:h + 1], lbuf[:, h:h + 1]), [expo])
            state["li_g%d" % par] = li
            return (expo, li)

        def stage_b(i, expo):
            par = i % 2
            trs = []
            for jc in range(3):
                trs.append(p.t(lambda t_, par=par, jc=jc: t_.transpose(
                    ts_["tr_ps"][:, jc, :], P_sb[par][:, jc * 128:(jc + 1) * 128], ident[:]),
                    [expo] + ([state["tr_reader"]] if jc == 0 else [])))
            pte = p.v(lambda v, par=par: v.tensor_copy(PT_sb[par][:, :, :], ts_["tr_ps"][:, 0:3, :]),
                      [trs[-1]])
            state["psb_reader"][par] = trs[-1]
            state["tr_reader"] = pte
            return pte

        def stage_c(i, pte, li):
            t, hh = items[i]
            h = 2 * pair + hh
            par = i % 2
            pvs = None
            for jc in range(3):
                pvs = p.t(lambda t_, t=t, hh=hh, par=par, jc=jc: t_.matmul(
                    y_ps[par][:, 0:64], PT_sb[par][:, jc, :],
                    Vp[:, t + jc, hh * 64:(hh + 1) * 64],
                    start=(jc == 0), stop=(jc == 2)),
                    [pte, state["y_reader"][par]] if jc == 0 else [])
            se = p.v(lambda v, t=t, h=h, par=par: v.tensor_scalar(
                y_sb[:, t, h * 64:(h + 1) * 64], y_ps[par][:, 0:64],
                linv[:, t % 2, h:h + 1], None, ALU.mult),
                [pvs, li])
            state["y_reader"][par] = se
            return se

        hist = {}
        for i in range(len(items)):
            hist[i] = {}
            hist[i]["a"] = stage_a(i)
            if i - 1 in hist:
                hist[i - 1]["b"] = stage_b(i - 1, hist[i - 1]["a"][0])
            if i - 2 in hist:
                last_se = stage_c(i - 2, hist[i - 2]["b"], hist[i - 2]["a"][1])
        for j in (len(items) - 2, len(items) - 1):
            if "b" not in hist[j]:
                hist[j]["b"] = stage_b(j, hist[j]["a"][0])
            last_se = stage_c(j, hist[j]["b"], hist[j]["a"][1])
        state["qtp_reader"] = last_se
        state["vp_reader_0"] = last_se
        state["vp_reader_1"] = last_se
    y_done = last_se
    esC1.close()
    wstage4 = sb("c_wst4", [128, 4, D], BF16)
    WoT = sb("c_woT", [128, 8, D], BF16)
    yT_sb = sb("c_yT", [128, 8, 128], BF16)
    ts_["wstage"] = wstage4
    _, woT = emit_w_transpose(p, ts_, wo[:, :], WoT, [y_done], state["tr_reader"])
    wo_ps = ts_["pr_ps"]
    wo_reader = [y_done, y_done]
    ytr_reader = woT
    yT_reader = None
    last_ad = None
    for t in range(16):
        ytrs = []
        for fc in range(8):
            ytrs.append(p.t(lambda t_, t=t, fc=fc: t_.transpose(
                ts_["tr_ps"][:, fc, :], y_sb[:, t, fc * 128:(fc + 1) * 128], ident[:]),
                [y_done] + ([ytr_reader] if fc == 0 else [])))
        yte = p.v(lambda v: v.tensor_copy(yT_sb[:, :, :], ts_["tr_ps"][:, :, :]), [ytrs[-1], yT_reader])
        ytr_reader = yte
        adds = []
        wmm = None
        for dcol in range(2):
            for fc in range(8):
                wmm = p.t(lambda t_, t=t, dcol=dcol, fc=fc: t_.matmul(
                    wo_ps[dcol][:, :], yT_sb[:, fc, :], WoT[:, fc, dcol * 512:(dcol + 1) * 512],
                    start=(fc == 0), stop=(fc == 7)),
                    [yte, wo_reader[dcol]] if fc == 0 else [])
            ad = p.v(lambda v, t=t, dcol=dcol: v.tensor_tensor(
                x_sb[:, t + 2, dcol * 512:(dcol + 1) * 512], wo_ps[dcol][:, :],
                x_sb[:, t + 2, dcol * 512:(dcol + 1) * 512], ALU.add), [wmm])
            wo_reader[dcol] = ad
            adds.append(ad)
        yT_reader = wmm
        last_ad = adds[-1]
    esC.close()
    return last_ad


def _emit_attn1_phase(p, nc, x_own, wqkv, ident, identf, indm, ones1,
                      kv_in, kv_g, KTN, phase_dep):
    TQ = 2048
    esF = ExitStack()
    sb = lambda name, shape, dt: esF.enter_context(nc.sbuf_tensor(name, shape, dt))
    ps = lambda name, shape, dt: esF.enter_context(nc.psum_tensor(name, shape, dt))
    QT = sb("g_QT", [128, 8, TQ], BF16)
    ts_ = {"ident": ident}
    ts_["negmh"] = sb("g_negmh", [128, 16], F32)
    ts_["nstat"] = sb("g_nstat", [32, 32], F32)
    ts_["ntr"] = sb("g_ntr", [32, 32], F32)
    s2_ps = [ps("g_s2ps0", [128, 1024], F32), ps("g_s2ps1", [128, 1024], F32)]
    y_ps = [ps("g_yps0", [128, 512], F32), ps("g_yps1", [128, 512], F32)]
    tr_ps = ps("g_trps", [128, 8, 128], BF16)
    ts_["tr_ps"] = tr_ps
    ts_["pr_ps"] = [y_ps[0], y_ps[1]]
    ts_["rt_ps"] = s2_ps[0]
    ts_["nb_ps"] = s2_ps[0]
    ltr_ps = s2_ps[1]

    esF1 = ExitStack()
    sb1 = lambda name, shape, dt: esF1.enter_context(nc.sbuf_tensor(name, shape, dt))
    xlnT = sb1("g_xlnT", [128, 8, TQ], BF16)
    ts_["stats"] = sb1("g_stats", [128, 16, 8], F32)
    ts_["sq"] = sb1("g_sq", [128, D], F32)
    ts_["xln_scr"] = [sb1("g_xl0", [128, D], BF16), sb1("g_xl1", [128, D], BF16)]
    ts_["wstage"] = sb1("g_wstg", [128, 4, D], BF16)
    WTa = sb1("g_WTa", [128, 8, D], BF16)
    KTc = sb1("g_KTc", [128, TQ], BF16)
    ts_["sqscr"] = [sb1("g_sc0", [128, 512], BF16), sb1("g_sc1", [128, 512], BF16)]
    v_stage = [sb1("g_vst0", [128, 16, 65], BF16), sb1("g_vst1", [128, 16, 65], BF16)]

    ms0 = p.v(lambda v: v.memset(v_stage[0][:, :, 64:65], 1.0), phase_dep)
    ms1 = p.v(lambda v: v.memset(v_stage[1][:, :, 64:65], 1.0), phase_dep)

    evq = emit_ln(p, ts_, 16, x_own, xlnT, phase_dep)[0]
    xln_dep = evq[-1]
    ts_["_nb_sq_reader"] = [None, None]
    ts_["_nb_ps_reader"] = None

    _, wkT = emit_w_transpose(p, ts_, wqkv[D:2 * D, :], WTa, phase_dep, xln_dep)
    kt_inT = kv_in[0:1, 0:KTN].rearrange("a (qc p j) -> (a qc) p j", p=128, j=TQ)
    rk = None
    kbr = [phase_dep[0], phase_dep[0]]
    kst_ops = []
    sqscr = ts_["sqscr"]; nb_ps = ts_["nb_ps"]; nstat = ts_["nstat"]
    ktc_reader = None
    for qc in range(8):
        evs = []
        for i, s0 in enumerate(range(0, TQ, 512)):
            pb = (qc * 4 + i) % 2
            mm = None
            for dc in range(8):
                mm = p.t(lambda t_, qc=qc, s0=s0, dc=dc, pb=pb: t_.matmul(
                    ts_["pr_ps"][pb][:, :], WTa[:, dc, qc * 128:(qc + 1) * 128],
                    xlnT[:, dc, s0:s0 + 512], start=(dc == 0), stop=(dc == 7)),
                    ([wkT, xln_dep, kbr[pb]] if dc == 0 else []))
            ev = p.v(lambda v, s0=s0, pb=pb: v.tensor_copy(KTc[:, s0:s0 + 512],
                                                           ts_["pr_ps"][pb][:, :]),
                     [mm, ktc_reader])
            kbr[pb] = ev
            evs.append(ev)
        stk = p.sy(lambda s, qc=qc: s.dma_start(kt_inT[qc], KTc[:, :]), [evs[-1]])
        kst_ops.append(stk)
        for i, s0 in enumerate(range(0, TQ, 512)):
            pb = i % 2
            sq = p.s(lambda s, s0=s0, pb=pb: s.activation(
                sqscr[pb][:], KTc[:, s0:s0 + 512], AF.Square),
                [evs[-1], ts_["_nb_sq_reader"][pb]])
            mm2 = p.t(lambda t_, qc=qc, pb=pb: t_.matmul(
                nb_ps[0:16, 0:512], indm[:, qc, :], sqscr[pb][:],
                start=True, stop=True), [sq, ts_["_nb_ps_reader"]])
            ts_["_nb_sq_reader"][pb] = mm2
            rm = p.v(lambda v: v.reduce_max(nstat[0:16, 3:4], nb_ps[0:16, 0:512], axis=AX.X), [mm2])
            ts_["_nb_ps_reader"] = rm
            if rk is None:
                rk = p.v(lambda v: v.tensor_copy(nstat[0:16, 1:2], nstat[0:16, 3:4]), [rm])
            else:
                rk = p.v(lambda v: v.tensor_tensor(nstat[0:16, 1:2], nstat[0:16, 1:2],
                                                   nstat[0:16, 3:4], ALU.max), [rm, rk])
        ktc_reader = ts_["_nb_sq_reader"][1]

    _, wvT = emit_w_transpose(p, ts_, wqkv[2 * D:3 * D, :], WTa, [], kbr[0])
    v_inT = kv_in[0:1, KTN:].rearrange("a (tt p c) -> (a tt) p c", p=128, c=1040)
    vst_reader = [ms0, ms1]
    vbr = [kbr[0], kbr[1]]
    vst_ops = []
    k = 0
    for tt in range(16):
        vb = tt % 2
        evs = []
        for fcol in range(2):
            pb = k % 2
            k += 1
            mm = None
            for dc in range(8):
                mm = p.t(lambda t_, tt=tt, fcol=fcol, dc=dc, pb=pb: t_.matmul(
                    ts_["pr_ps"][pb][:, :], xlnT[:, dc, tt * 128:(tt + 1) * 128],
                    WTa[:, dc, fcol * 512:(fcol + 1) * 512],
                    start=(dc == 0), stop=(dc == 7)),
                    ([wvT, xln_dep, vbr[pb]] if dc == 0 else []))
            ev = p.v(lambda v, pb=pb, vb=vb, fcol=fcol: v.tensor_copy(
                v_stage[vb][:, fcol * 8:(fcol + 1) * 8, 0:64],
                ts_["pr_ps"][pb][:, :].rearrange("p (h c) -> p h c", c=64)),
                [mm, vst_reader[vb]])
            vbr[pb] = ev
            evs.append(ev)
        stv = p.sy(lambda s, tt=tt, vb=vb: s.dma_start(
            v_inT[tt], v_stage[vb].rearrange("p h c -> p (h c)")), evs)
        vst_reader[vb] = stv
        vst_ops.append(stv)

    _, wqT = emit_w_transpose(p, ts_, wqkv[0:D, :], WTa, [], vbr[0])
    groups = [[0, 1, 2, 3], [4, 5, 6, 7]]
    cc1 = p.add("gpsimd", lambda g: g.collective_compute(
        "AllGather", ALU.bypass, replica_groups=groups,
        ins=[kv_in.ap().opt()], outs=[kv_g.ap().opt()]),
        [kst_ops[-1], vst_ops[-1]])
    cc2 = cc1
    q_ops = emit_proj_fm(p, ts_, WTa, xlnT, QT, TQ, wqT, xln_dep, None, vbr[1])
    rq = emit_normsq_half(p, ts_, QT, TQ, indm, 0, [q_ops[-1]], None)
    nb_ev = emit_normbound_finish(p, ts_, ones1, rq, rk, 0.125)

    esF1.close()
    kt_sb0 = sb("g_kt0", [128, 8192], BF16)
    vp_sb0 = sb("g_vp0", [128, 64, 130], BF16)
    kt_sb = [kt_sb0, kt_sb0]
    vp_sb = [vp_sb0, vp_sb0]
    yT_all = sb("g_yT", [128, 8, TQ], BF16)
    pT_sb = [sb("g_pt0", [128, 1024], BF16), sb("g_pt1", [128, 1024], BF16)]
    lrow = sb("g_lrow", [1, 512], F32)
    ltok = sb("g_ltok", [128, 16, 16], F32)
    linv_all = ltok

    negmh = ts_["negmh"]
    kv_dep = [nb_ev, cc1, cc2]
    kt_ld = {}
    kt_reader = [nb_ev, nb_ev]
    vp_reader = [nb_ev, nb_ev]

    def ktg_view(r, pair):
        return kv_g[r:r + 1, 0:KTN].rearrange("a (f j) -> (a f) j", j=TQ)[
            pair * 128:(pair + 1) * 128, :]

    def vg_view(r, pair):
        return kv_g[r:r + 1, KTN:].rearrange("a (j c) -> (a j) c", c=1040)[
            :, 2 * pair * 65:(2 * pair + 2) * 65].rearrange("(jt p) c -> p jt c", p=128)

    def load_pair(pair):
        b = pair % 2
        ls = []
        for r in range(4):
            ls.append(p.g(lambda g_, pair=pair, b=b, r=r: g_.dma_start(
                kt_sb[b][:, r * 2048:(r + 1) * 2048], ktg_view(r, pair)),
                kv_dep + [kt_reader[b]] if r == 0 else []))
            ls.append(p.g(lambda g_, pair=pair, b=b, r=r: g_.dma_start(
                vp_sb[b][:, r * 16:(r + 1) * 16, :], vg_view(r, pair)),
                kv_dep + [vp_reader[b]] if r == 0 else []))
        kt_ld[pair] = ls

    load_pair(0)

    s_reader = [nb_ev, nb_ev]
    psb_reader = [None, None]
    y_reader = [None, None]
    ltr_reader = nb_ev
    lrow_reader = None
    first_smm_dep = [q_ops[-1]]
    for pair in range(8):
        b = pair % 2
        dW = list(kt_ld[pair])
        pv_last = [None, None]
        for qh in range(4):
            q0 = qh * 512
            pend = None
            for k in range(64):
                par = k % 2
                smm = None
                for hh in range(2):
                    smm = p.t(lambda t_, b=b, jt=k, hh=hh, q0=q0, par=par, pair=pair: t_.matmul(
                        s2_ps[par][:, hh * 512:(hh + 1) * 512],
                        kt_sb[b][hh * 64:(hh + 1) * 64, jt * 128:(jt + 1) * 128],
                        QT[hh * 64:(hh + 1) * 64, pair, q0:q0 + 512],
                        start=True, stop=True),
                        ((dW + [s_reader[par]] + first_smm_dep) if hh == 0 else []))
                expos = []
                for hh in range(2):
                    h_ = 2 * pair + hh
                    expos.append(p.s(lambda s, h_=h_, par=par, hh=hh: s.activation(
                        pT_sb[par][:, hh * 512:(hh + 1) * 512],
                        s2_ps[par][:, hh * 512:(hh + 1) * 512], AF.Exp, scale=0.125,
                        bias=negmh[:, h_:h_ + 1]),
                        [smm, nb_ev, psb_reader[par]] if hh == 0 else [smm]))
                s_reader[par] = expos[-1]
                first_smm_dep = []
                if pend is not None:
                    e1, k0, par0 = pend
                    pv = None
                    for hh in range(2):
                        pv = p.t(lambda t_, b=b, jt0=k0, hh=hh, par0=par0: t_.matmul(
                            y_ps[hh][0:65, :], vp_sb[b][:, jt0, hh * 65:(hh + 1) * 65],
                            pT_sb[par0][:, hh * 512:(hh + 1) * 512],
                            start=(jt0 == 0), stop=(jt0 == 63)),
                            ([e1, y_reader[hh]] if k0 == 0 else ([e1] if hh == 0 else [])))
                        pv_last[hh] = pv
                    psb_reader[par0] = pv
                pend = (expos[-1], k, par)
            e1, k0, par0 = pend
            for hh in range(2):
                pv = p.t(lambda t_, b=b, jt0=k0, hh=hh, par0=par0: t_.matmul(
                    y_ps[hh][0:65, :], vp_sb[b][:, jt0, hh * 65:(hh + 1) * 65],
                    pT_sb[par0][:, hh * 512:(hh + 1) * 512],
                    start=(jt0 == 0), stop=(jt0 == 63)),
                    [e1] if hh == 0 else [])
                pv_last[hh] = pv
                psb_reader[par0] = pv
            for hh in range(2):
                h = 2 * pair + hh
                ye = p.v(lambda v, pair=pair, hh=hh, q0=q0: v.tensor_copy(
                    yT_all[hh * 64:(hh + 1) * 64, pair, q0:q0 + 512],
                    y_ps[hh][0:64, :]), [pv_last[1]])
                le = p.v(lambda v, hh=hh: v.tensor_copy(
                    lrow[0:1, :], y_ps[hh][64:65, :]), [pv_last[1], lrow_reader])
                trs = []
                for i in range(4):
                    trs.append(p.t(lambda t_, i=i: t_.transpose(
                        ltr_ps[0:128, i:i + 1], lrow[0:1, i * 128:(i + 1) * 128],
                        identf[0:1, 0:1]), [le] + ([ltr_reader] if i == 0 else [])))
                t0 = q0 // 128
                lte = p.v(lambda v, t0=t0, h=h: v.tensor_copy(
                    ltok[:, t0:t0 + 4, h:h + 1].rearrange("p a b -> p (a b)"),
                    ltr_ps[0:128, 0:4]), [trs[-1]])
                ltr_reader = lte
                lrow_reader = trs[-1]
                y_reader[hh] = le
                s_reader[1] = lte
        if pair + 1 < 8:
            kt_reader[(pair + 1) % 2] = pv_last[1]
            vp_reader[(pair + 1) % 2] = pv_last[1]
            load_pair(pair + 1)

    rc = p.v(lambda v: v.reciprocal(ltok[:, :, :].rearrange("p a b -> p (a b)"),
                                    ltok[:, :, :].rearrange("p a b -> p (a b)")), [ltr_reader])

    ytr_reader = None
    last_ad = None
    for t in range(16):
        for pc in range(8):
            tr = p.t(lambda t_, t=t, pc=pc: t_.transpose(
                tr_ps[:, pc, :], yT_all[:, pc, t * 128:(t + 1) * 128], ident[:]),
                [rc if (t == 0 and pc == 0) else None, ytr_reader])
            for hh in range(2):
                h = 2 * pc + hh
                ad = p.v(lambda v, t=t, pc=pc, hh=hh, h=h: v.scalar_tensor_tensor(
                    x_own[:, t, h * 64:(h + 1) * 64], tr_ps[:, pc, hh * 64:(hh + 1) * 64],
                    linv_all[:, t, h:h + 1], x_own[:, t, h * 64:(h + 1) * 64],
                    ALU.mult, ALU.add), [tr, rc])
                ytr_reader = ad
                last_ad = ad
    esF.close()
    return last_ad


def build_fused(phases="BCDEFG"):
    TQ, TK, NTK = 2048, 2304, 18
    nc = bass.Bass("TRN2", target_bir_lowering=False)
    xin = nc.dram_tensor("xin", [TK, D], F32, kind="ExternalInput")
    wq = nc.dram_tensor("wq", [D, D], F32, kind="ExternalInput")
    wk = nc.dram_tensor("wk", [D, D], F32, kind="ExternalInput")
    wv = nc.dram_tensor("wv", [D, D], F32, kind="ExternalInput")
    wo = nc.dram_tensor("wo", [D, D], F32, kind="ExternalInput")
    rw0 = nc.dram_tensor("rw0", [D, 8], F32, kind="ExternalInput")
    up7 = nc.dram_tensor("up7", [7, D, 4096], F32, kind="ExternalInput")
    gate7 = nc.dram_tensor("gate7", [7, D, 4096], F32, kind="ExternalInput")
    down7 = nc.dram_tensor("down7", [7, 4096, D], F32, kind="ExternalInput")
    sup = nc.dram_tensor("sup", [D, 4096], F32, kind="ExternalInput")
    sgate = nc.dram_tensor("sgate", [D, 4096], F32, kind="ExternalInput")
    sdown = nc.dram_tensor("sdown", [4096, D], F32, kind="ExternalInput")
    wqkv = nc.dram_tensor("wqkv", [3 * D, D], F32, kind="ExternalInput")
    rw1 = nc.dram_tensor("rw1", [D, 8], F32, kind="ExternalInput")
    w1 = nc.dram_tensor("w1", [8, D, 2048], F32, kind="ExternalInput")
    w2 = nc.dram_tensor("w2", [8, D, 2048], F32, kind="ExternalInput")
    w3 = nc.dram_tensor("w3", [8, 2048, D], F32, kind="ExternalInput")
    identin = nc.dram_tensor("identin", [128, 128], BF16, kind="ExternalInput")
    identfin = nc.dram_tensor("identfin", [128, 128], F32, kind="ExternalInput")
    permin = nc.dram_tensor("permin", [128, 128], BF16, kind="ExternalInput")
    cosin = nc.dram_tensor("cosin", [128, TK], BF16, kind="ExternalInput")
    sinin = nc.dram_tensor("sinin", [128, TK], BF16, kind="ExternalInput")
    maskin = nc.dram_tensor("maskin", [128, 3, 384], F32, kind="ExternalInput")
    indmin = nc.dram_tensor("indmin", [128, 8, 16], BF16, kind="ExternalInput")
    onesin = nc.dram_tensor("onesin", [1, 128], F32, kind="ExternalInput")
    xout = nc.dram_tensor("xout", [TQ, D], F32, kind="ExternalOutput")
    KTN = D * TQ
    VN = TQ * 16 * 65
    kv_in = nc.dram_tensor("kv_in", [1, KTN + VN], BF16, kind="Internal")
    kv_g = nc.dram_tensor("kv_g", [4, KTN + VN], BF16, kind="Internal")

    es = ExitStack()
    sb = lambda name, shape, dt: es.enter_context(nc.sbuf_tensor(name, shape, dt))
    x_sb = sb("x", [128, NTK, D], F32)
    ident = sb("ident", [128, 128], BF16)
    identf = sb("identf", [128, 128], F32)
    indm = sb("indm", [128, 8, 16], BF16)
    ones1 = sb("ones1", [1, 128], F32)
    x_own = x_sb[:, 2:18, :]

    with nc.Block() as block:
        p = Plan(nc, es)
        ld_id = p.sy(lambda s: s.dma_start(ident[:], identin[:, :]))
        ld_if = p.sy(lambda s: s.dma_start(identf[:], identfin[:, :]))
        ld_im = p.sy(lambda s: s.dma_start(indm[:, :, :], indmin[:, :, :]))
        ld_on = p.sy(lambda s: s.dma_start(ones1[:], onesin[:, :]))
        ld_x = p.sy(lambda s: s.dma_start(x_sb[:, :, :], xin.rearrange("(t p) d -> p t d", p=128)))

        def wsrc0():
            def f(e, h8):
                hs = slice(h8 * 512, (h8 + 1) * 512)
                if e < 7:
                    return (up7[e, :, hs].rearrange("(c p) h -> p c h", p=128),
                            gate7[e, :, hs].rearrange("(c p) h -> p c h", p=128),
                            down7[e, hs, :].rearrange("(c p) d -> p c d", p=128))
                return (sup[:, hs].rearrange("(c p) h -> p c h", p=128),
                        sgate[:, hs].rearrange("(c p) h -> p c h", p=128),
                        sdown[hs, :].rearrange("(c p) d -> p c d", p=128))
            return f

        def wsrc1():
            def f(e, h8):
                hs = slice(h8 * 512, (h8 + 1) * 512)
                return (w2[e, :, hs].rearrange("(c p) h -> p c h", p=128),
                        w1[e, :, hs].rearrange("(c p) h -> p c h", p=128),
                        w3[e, hs, :].rearrange("(c p) d -> p c d", p=128))
            return f

        scope_mk = ExitStack
        end = ld_x
        if "B" in phases:
            end = _emit_ffn_phase(p, nc, scope_mk, x_sb, NTK, TK, 0, wsrc0, rw0,
                                  ident, [end, ld_id])
        if "C" in phases:
            end = _emit_attn0_phase(p, nc, None, x_sb, wq, wk, wv, wo,
                                    (permin, cosin, sinin, maskin), ident, identf,
                                    indm, ones1, [end, ld_im, ld_on, ld_id])
        if "D" in phases:
            end = _emit_ffn_phase(p, nc, scope_mk, x_own, 16, TQ, 0, wsrc0, rw0,
                                  ident, [end, ld_id])
        if "E" in phases:
            end = _emit_ffn_phase(p, nc, scope_mk, x_own, 16, TQ, 1, wsrc1, rw1,
                                  ident, [end, ld_id])
        if "F" in phases:
            end = _emit_attn1_phase(p, nc, x_own, wqkv, ident, identf, indm, ones1,
                                    kv_in, kv_g, KTN, [end, ld_if, ld_id, ld_im, ld_on])
        if "G" in phases:
            end = _emit_ffn_phase(p, nc, scope_mk, x_own, 16, TQ, 1, wsrc1, rw1,
                                  ident, [end, ld_id])
        st = p.sy(lambda s: s.dma_start(xout.rearrange("(t p) d -> p t d", p=128),
                                        x_own), [end])
        p.sy(lambda s: s.nop(), [st], dma=False)
        p.finalize_and_emit(block)
    es.close()
    return nc


def kernel_fused(L0_attn_x, L0_attn_Wq, L0_attn_Wk, L0_attn_Wv, L0_attn_Wo,
                 L0_ffn_router_w, L0_ffn_up_w, L0_ffn_gate_w, L0_ffn_down_w,
                 L0_ffn_shared_up_w, L0_ffn_shared_gate_w, L0_ffn_shared_down_w,
                 L1_attn_W_qkv, L1_ffn_Wg, L1_ffn_W1, L1_ffn_W2, L1_ffn_W3):
    x = np.asarray(L0_attn_x)
    nc = _get_nc("fused", build_fused)
    in_maps = []
    for c in range(NCORES):
        b, q = c // 4, c % 4
        own = x[b, q * 2048:(q + 1) * 2048]
        if q == 0:
            halo = np.zeros((256, D), dtype=np.float32)
        else:
            halo = x[b, q * 2048 - 256:q * 2048]
        xin = np.ascontiguousarray(np.concatenate([halo, own], 0))
        pos = np.maximum(q * 2048 - 256 + np.arange(2304), 0)
        cosT, sinT = _rope_tables(pos)
        in_maps.append(dict(
            xin=xin, wq=np.asarray(L0_attn_Wq), wk=np.asarray(L0_attn_Wk),
            wv=np.asarray(L0_attn_Wv), wo=np.asarray(L0_attn_Wo),
            rw0=np.asarray(L0_ffn_router_w), up7=np.asarray(L0_ffn_up_w),
            gate7=np.asarray(L0_ffn_gate_w), down7=np.asarray(L0_ffn_down_w),
            sup=np.asarray(L0_ffn_shared_up_w), sgate=np.asarray(L0_ffn_shared_gate_w),
            sdown=np.asarray(L0_ffn_shared_down_w), wqkv=np.asarray(L1_attn_W_qkv),
            rw1=np.asarray(L1_ffn_Wg), w1=np.asarray(L1_ffn_W1),
            w2=np.asarray(L1_ffn_W2), w3=np.asarray(L1_ffn_W3),
            identin=_IDENT, identfin=_IDENTF, permin=_PERM, cosin=cosT, sinin=sinT,
            maskin=_l0_masks(q == 0), indmin=_INDM, onesin=_ONES1))
    res = run_bass_kernel_spmd(nc, in_maps, core_ids=list(range(NCORES)))
    out = np.stack([res.results[c]["xout"] for c in range(NCORES)])
    return out.reshape(2, 8192, D)


# version 09
# speedup vs baseline: 1.3171x; 1.0018x over previous
import numpy as np
import ml_dtypes
from contextlib import ExitStack

import concourse.bass as bass
import concourse.mybir as mybir
from concourse.bass_utils import run_bass_kernel_spmd

F32 = mybir.dt.float32
BF16 = mybir.dt.bfloat16
AF = mybir.ActivationFunctionType
ALU = mybir.AluOpType
AX = mybir.AxisListType

BF = ml_dtypes.bfloat16

D = 1024
NCORES = 8


ENGS = ("tensor", "scalar", "vector", "sync", "gpsimd")


class _Op:
    __slots__ = ("eng", "fn", "deps", "inc", "gseq", "eseq", "promoted", "val")

    def __init__(self, eng, fn, deps, inc):
        self.eng = eng
        self.fn = fn
        self.deps = deps
        self.inc = inc
        self.promoted = inc == 16
        self.val = None


class Plan:
    def __init__(self, nc, es):
        self.nc = nc
        self.ops = {e: [] for e in ENGS}
        self.all = []
        self.sems = {e: es.enter_context(nc.semaphore(f"sem_{e}")) for e in ENGS}

    def add(self, eng, fn, deps=(), dma=False):
        op = _Op(eng, fn, [d for d in deps if d is not None], 16 if dma else 1)
        if not dma:
            op.promoted = False
        op.gseq = len(self.all)
        op.eseq = len(self.ops[eng])
        self.all.append(op)
        self.ops[eng].append(op)
        for d in op.deps:
            assert d.gseq < op.gseq
            if d.eng == op.eng and op.eseq - d.eseq >= 2:
                continue
            d.promoted = True
        return op

    def t(self, fn, deps=()):
        return self.add("tensor", fn, deps)

    def s(self, fn, deps=()):
        return self.add("scalar", fn, deps)

    def v(self, fn, deps=()):
        return self.add("vector", fn, deps)

    def sy(self, fn, deps=(), dma=True):
        return self.add("sync", fn, deps, dma=dma)

    def g(self, fn, deps=(), dma=True):
        return self.add("gpsimd", fn, deps, dma=dma)

    def finalize_and_emit(self, block):
        for e in ENGS:
            cum = 0
            for op in self.ops[e]:
                if op.promoted:
                    cum += op.inc
                    op.val = cum
        plan = self

        def engine_body(eng):
            def body(handle):
                watermark = {e: 0 for e in ENGS}
                for op in plan.ops[eng]:
                    for d in op.deps:
                        if d.eng == eng and op.eseq - d.eseq >= 2:
                            continue
                        if d.val is None:
                            raise AssertionError("dep not promoted")
                        if d.val > watermark[d.eng]:
                            handle.wait_ge(plan.sems[d.eng], d.val)
                            watermark[d.eng] = d.val
                    ins = op.fn(handle)
                    if op.promoted:
                        ins.then_inc(plan.sems[eng], op.inc)
            return body

        block.tensor(engine_body("tensor"))
        block.scalar(engine_body("scalar"))
        block.vector(engine_body("vector"))
        block.sync(engine_body("sync"))
        block.gpsimd(engine_body("gpsimd"))



def emit_ln(p, ts_, NT, x_sb, xlnT, deps_x):
    nc = p.nc
    stats = ts_["stats"]
    sq = ts_["sq"]
    xb = ts_["xln_scr"]
    tr_ps = ts_["tr_ps"]
    ident = ts_["ident"]

    def st(t, j):
        return stats[:, t, j:j + 1]

    r_sum, r_sq = [], []
    for t in range(NT):
        r_sum.append(p.v(lambda v, t=t: v.reduce_sum(st(t, 0), x_sb[:, t, :], axis=AX.X, negate=True), deps_x))
        r_sq.append(p.s(lambda s, t=t: s.activation(sq[:], x_sb[:, t, :], AF.Square, accum_out=st(t, 1)), deps_x))
    mean_ops = [p.v(lambda v, t=t: v.tensor_scalar(st(t, 2), st(t, 0), 1.0 / D, None, ALU.mult), [r_sum[t]]) for t in range(NT)]
    m2_ops = [p.v(lambda v, t=t: v.tensor_tensor(st(t, 3), st(t, 2), st(t, 2), ALU.mult), [mean_ops[t]]) for t in range(NT)]
    var_ops = [p.v(lambda v, t=t: v.tensor_scalar(st(t, 4), st(t, 1), 1.0 / D, 1e-5, ALU.mult, ALU.add), [r_sq[t]]) for t in range(NT)]
    var2_ops = [p.v(lambda v, t=t: v.tensor_tensor(st(t, 4), st(t, 4), st(t, 3), ALU.subtract), [var_ops[t], m2_ops[t]]) for t in range(NT)]
    lnv_ops = [p.s(lambda s, t=t: s.activation(st(t, 5), st(t, 4), AF.Ln), [var2_ops[t]]) for t in range(NT)]
    rstd_ops = [p.s(lambda s, t=t: s.activation(st(t, 6), st(t, 5), AF.Exp, scale=-0.5), [lnv_ops[t]]) for t in range(NT)]

    evac_ops = []
    xln_ops = []
    tr_ops_prev = None
    evac_prev2 = [None, None]
    for t in range(NT):
        xop = p.v(lambda v, t=t: v.tensor_scalar(
            xb[t % 2][:], x_sb[:, t, :], st(t, 2), st(t, 6), ALU.add, ALU.mult),
            [rstd_ops[t], evac_prev2[t % 2]])
        xln_ops.append(xop)
        trs = []
        for dc in range(8):
            trs.append(p.t(lambda t_, t=t, dc=dc: t_.transpose(
                tr_ps[:, dc, :], xb[t % 2][:, dc * 128:(dc + 1) * 128], ident[:]),
                [xop] + ([tr_ops_prev] if dc == 0 else [])))
        ev = p.v(lambda v, t=t: v.tensor_copy(xlnT[:, :, t * 128:(t + 1) * 128], tr_ps[:, :, :]), [trs[-1]])
        evac_ops.append(ev)
        evac_prev2[t % 2] = ev
        tr_ops_prev = ev
    return evac_ops, xln_ops


def emit_router_l0(p, ts_, NT, xlnT, rwb, evac_ops, rw_dep):
    logits_ps = ts_["logits_ps"]
    stats = ts_["stats"]
    coefs = ts_["coefs"]
    exps_scr = ts_["exps_scr"]

    def st(t, j):
        return stats[:, t, j:j + 1]

    mm_ops = []
    eq_prev = None
    exps_prev = None
    for t in range(NT):
        mms = []
        for dc in range(8):
            mms.append(p.t(lambda t_, t=t, dc=dc: t_.matmul(
                logits_ps[:, 0:8], xlnT[:, dc, t * 128:(t + 1) * 128], rwb[:, dc, 0:8],
                start=(dc == 0), stop=(dc == 7)),
                ([evac_ops[t], rw_dep, eq_prev, exps_prev] if dc == 0 else [])))
        mm = mms[-1]
        nm1 = p.v(lambda v, t=t: v.reduce_max(st(t, 7), logits_ps[:, 0:8], axis=AX.X, negate=True), [mm])
        ex = p.s(lambda s, t=t: s.activation(exps_scr[:], logits_ps[:, 0:8], AF.Exp, bias=st(t, 7), accum_out=st(t, 0)), [nm1, mm])
        t1 = p.v(lambda v, t=t: v.reciprocal(st(t, 1), st(t, 0)), [ex])
        eq = p.v(lambda v, t=t: v.tensor_scalar(coefs[:, t, :], logits_ps[:, 0:8], st(t, 7), 0.0, ALU.add, ALU.is_equal), [nm1, mm])
        cf = p.v(lambda v, t=t: v.tensor_scalar(coefs[:, t, :], coefs[:, t, :], st(t, 1), None, ALU.mult), [t1, eq])
        eq_prev = cf
        exps_prev = ex
        mm_ops.append(cf)
    return mm_ops


def emit_moe(p, ts_, NT, T, xlnT, x_sb, coefs, wsrc, ln_done, router_done,
             H, n_routed, shared_coef, silu_on_gate):
    w_up = ts_["w_up"]; w_gate = ts_["w_gate"]; w_down = ts_["w_down"]
    hidT = ts_["hidT"]
    silu_sb = ts_["silu_sb"]
    g_ps = ts_["g_ps"]; u_ps = ts_["u_ps"]
    d_ps = ts_["d_ps"]
    tslices = [(s, min(512, T - s)) for s in range(0, T, 512)]
    n_exp = n_routed + (1 if shared_coef is not None else 0)

    wload = {}
    buf_last_reader = [None, None]
    seq = [(e, h8) for e in range(n_exp) for h8 in range(H // 512)]

    def load_w(i):
        e, h8 = seq[i]
        b = i % 2
        up_ap, gate_ap, down_ap = wsrc(e, h8)
        dep = [buf_last_reader[b]]
        o1 = p.g(lambda g, b=b, ap=up_ap: g.dma_start(w_up[b][:], ap), dep)
        o2 = p.g(lambda g, b=b, ap=gate_ap: g.dma_start(w_gate[b][:], ap), dep)
        o3 = p.g(lambda g, b=b, ap=down_ap: g.dma_start(w_down[b][:], ap), dep)
        wload[i] = (o1, o2, o3)

    load_w(0)
    load_w(1)

    silu_last = [None, None]
    g_last_reader = [None, None]
    u_last_reader = [None, None]
    d_last_reader = [None, None]
    last_x_op = None

    for i, (e, h8) in enumerate(seq):
        b = i % 2
        dW = list(wload[i])
        hid_ops = []
        step = 0
        for hc in range(4):
            for (s0, sl) in tslices:
                pg = step % 2
                gmm = None
                for dc in range(8):
                    gmm = p.t(lambda t_, b=b, hc=hc, s0=s0, sl=sl, dc=dc, pg=pg: t_.matmul(
                        g_ps[pg][:, 0:sl], w_gate[b][:, dc, hc * 128:(hc + 1) * 128],
                        xlnT[:, dc, s0:s0 + sl],
                        start=(dc == 0), stop=(dc == 7)),
                        (dW + ln_done + [g_last_reader[pg]]) if dc == 0 else [])
                sil = p.s(lambda s, pg=pg, sl=sl: s.activation(silu_sb[pg][:, 0:sl], g_ps[pg][:, 0:sl], AF.Silu),
                          [gmm, silu_last[pg]])
                g_last_reader[pg] = sil
                umm = None
                for dc in range(8):
                    umm = p.t(lambda t_, b=b, hc=hc, s0=s0, sl=sl, dc=dc, pg=pg: t_.matmul(
                        u_ps[pg][:, 0:sl], w_up[b][:, dc, hc * 128:(hc + 1) * 128],
                        xlnT[:, dc, s0:s0 + sl],
                        start=(dc == 0), stop=(dc == 7)),
                        [u_last_reader[pg]] if dc == 0 else [])
                mul = p.v(lambda v, hc=hc, s0=s0, sl=sl, pg=pg: v.tensor_tensor(
                    hidT[:, hc, s0:s0 + sl], u_ps[pg][:, 0:sl], silu_sb[pg][:, 0:sl], ALU.mult),
                    [umm, sil])
                u_last_reader[pg] = mul
                silu_last[pg] = mul
                hid_ops.append(mul)
                step += 1
        hid_done = hid_ops[-1]
        b_last_mm = None
        for tt in range(NT):
            for dcol in range(2):
                pd = dcol
                dmm = None
                for hc in range(4):
                    dmm = p.t(lambda t_, b=b, tt=tt, dcol=dcol, hc=hc, pd=pd: t_.matmul(
                        d_ps[pd][:, :], hidT[:, hc, tt * 128:(tt + 1) * 128],
                        w_down[b][:, hc, dcol * 512:(dcol + 1) * 512],
                        start=(hc == 0), stop=(hc == 3)),
                        ([hid_done, d_last_reader[pd]] + router_done) if hc == 0 else [])
                if e < n_routed:
                    stt = p.v(lambda v, tt=tt, dcol=dcol, pd=pd, e=e: v.scalar_tensor_tensor(
                        x_sb[:, tt, dcol * 512:(dcol + 1) * 512], d_ps[pd][:, :],
                        coefs[:, tt, e:e + 1], x_sb[:, tt, dcol * 512:(dcol + 1) * 512],
                        ALU.mult, ALU.add), [dmm])
                else:
                    stt = p.v(lambda v, tt=tt, dcol=dcol, pd=pd: v.scalar_tensor_tensor(
                        x_sb[:, tt, dcol * 512:(dcol + 1) * 512], d_ps[pd][:, :],
                        float(shared_coef), x_sb[:, tt, dcol * 512:(dcol + 1) * 512],
                        ALU.mult, ALU.add), [dmm])
                d_last_reader[pd] = stt
                last_x_op = stt
                b_last_mm = dmm
        buf_last_reader[b] = b_last_mm
        if i + 2 < len(seq):
            load_w(i + 2)
    return last_x_op


def emit_router_l1(p, ts_, NT, xlnT, rwb, evac_ops, rw_dep):
    logits_ps = ts_["logits_ps"]
    stats = ts_["stats"]
    coefs = ts_["coefs"]
    eq1 = ts_["eq1"]
    masked = ts_["masked"]
    eq2 = ts_["eq2"]

    def st(t, j):
        return stats[:, t, j:j + 1]

    out_ops = []
    last_ps_reader = None
    for t in range(NT):
        mms = []
        for dc in range(8):
            mms.append(p.t(lambda t_, t=t, dc=dc: t_.matmul(
                logits_ps[:, 0:8], xlnT[:, dc, t * 128:(t + 1) * 128], rwb[:, dc, 0:8],
                start=(dc == 0), stop=(dc == 7)),
                ([evac_ops[t], rw_dep, last_ps_reader] if dc == 0 else [])))
        mm = mms[-1]
        nm1 = p.v(lambda v, t=t: v.reduce_max(st(t, 0), logits_ps[:, 0:8], axis=AX.X, negate=True), [mm])
        e1 = p.v(lambda v, t=t: v.tensor_scalar(eq1[:], logits_ps[:, 0:8], st(t, 0), 0.0, ALU.add, ALU.is_equal), [nm1, mm])
        mk = p.v(lambda v, t=t: v.scalar_tensor_tensor(masked[:], eq1[:], -1e30, logits_ps[:, 0:8], ALU.mult, ALU.add), [e1, mm])
        nm2 = p.v(lambda v, t=t: v.reduce_max(st(t, 1), masked[:], axis=AX.X, negate=True), [mk])
        e2 = p.v(lambda v, t=t: v.tensor_scalar(eq2[:], logits_ps[:, 0:8], st(t, 1), 0.0, ALU.add, ALU.is_equal), [nm2, mm])
        last_ps_reader = e2
        dm = p.v(lambda v, t=t: v.tensor_tensor(st(t, 2), st(t, 0), st(t, 1), ALU.subtract), [nm2, nm1])
        z = p.s(lambda s, t=t: s.activation(st(t, 3), st(t, 2), AF.Exp), [dm])
        zp = p.v(lambda v, t=t: v.tensor_scalar(st(t, 4), st(t, 3), 1.0, None, ALU.add), [z])
        w1 = p.v(lambda v, t=t: v.reciprocal(st(t, 5), st(t, 4)), [zp])
        w1h = p.v(lambda v, t=t: v.tensor_scalar(st(t, 6), st(t, 5), 0.5, None, ALU.mult), [w1])
        w2h = p.v(lambda v, t=t: v.tensor_scalar(st(t, 7), st(t, 6), -1.0, 0.5, ALU.mult, ALU.add), [w1h])
        c1 = p.v(lambda v, t=t: v.tensor_scalar(coefs[:, t, :], eq1[:], st(t, 6), None, ALU.mult), [w1h, e1])
        c2 = p.v(lambda v, t=t: v.scalar_tensor_tensor(coefs[:, t, :], eq2[:], st(t, 7), coefs[:, t, :], ALU.mult, ALU.add), [w2h, e2, c1])
        out_ops.append(c2)
    return out_ops



def build_ffn(T, layer):
    NT = T // 128
    H = 4096 if layer == 0 else 2048
    nc = bass.Bass("TRN2", target_bir_lowering=False)
    xin = nc.dram_tensor("xin", [T, D], F32, kind="ExternalInput")
    rw = nc.dram_tensor("rw", [D, 8], F32, kind="ExternalInput")
    if layer == 0:
        up7 = nc.dram_tensor("up7", [7, D, H], F32, kind="ExternalInput")
        gate7 = nc.dram_tensor("gate7", [7, D, H], F32, kind="ExternalInput")
        down7 = nc.dram_tensor("down7", [7, H, D], F32, kind="ExternalInput")
        sup = nc.dram_tensor("sup", [D, H], F32, kind="ExternalInput")
        sgate = nc.dram_tensor("sgate", [D, H], F32, kind="ExternalInput")
        sdown = nc.dram_tensor("sdown", [H, D], F32, kind="ExternalInput")
    else:
        w1 = nc.dram_tensor("w1", [8, D, H], F32, kind="ExternalInput")
        w2 = nc.dram_tensor("w2", [8, D, H], F32, kind="ExternalInput")
        w3 = nc.dram_tensor("w3", [8, H, D], F32, kind="ExternalInput")
    identin = nc.dram_tensor("identin", [128, 128], BF16, kind="ExternalInput")
    xout = nc.dram_tensor("xout", [T, D], F32, kind="ExternalOutput")

    es = ExitStack()
    sb = lambda name, shape, dt: es.enter_context(nc.sbuf_tensor(name, shape, dt))
    ps = lambda name, shape, dt: es.enter_context(nc.psum_tensor(name, shape, dt))

    ts_ = {}
    x_sb = sb("x", [128, NT, D], F32)
    xlnT = sb("xlnT", [128, 8, T], BF16)
    ts_["stats"] = sb("stats", [128, NT, 8], F32)
    ts_["sq"] = sb("sq", [128, D], F32)
    ts_["xln_scr"] = [sb("xlns0", [128, D], BF16), sb("xlns1", [128, D], BF16)]
    ts_["ident"] = sb("ident", [128, 128], BF16)
    ts_["coefs"] = sb("coefs", [128, NT, 8], F32)
    ts_["exps_scr"] = sb("exps_scr", [128, 8], F32)
    ts_["eq1"] = sb("eq1", [128, 8], F32)
    ts_["eq2"] = sb("eq2", [128, 8], F32)
    ts_["masked"] = sb("masked", [128, 8], F32)
    rwb = sb("rwb", [128, 8, 8], BF16)
    ts_["w_up"] = [sb("wup0", [128, 8, 512], BF16), sb("wup1", [128, 8, 512], BF16)]
    ts_["w_gate"] = [sb("wg0", [128, 8, 512], BF16), sb("wg1", [128, 8, 512], BF16)]
    ts_["w_down"] = [sb("wd0", [128, 4, D], BF16), sb("wd1", [128, 4, D], BF16)]
    ts_["hidT"] = sb("hidT", [128, 4, T], BF16)
    ts_["silu_sb"] = [sb("sil0", [128, 512], F32), sb("sil1", [128, 512], F32)]
    ts_["tr_ps"] = ps("tr_ps", [128, 8, 128], BF16)
    ts_["logits_ps"] = ps("logits_ps", [128, 512], F32)
    ts_["g_ps"] = [ps("gps0", [128, 512], F32), ps("gps1", [128, 512], F32)]
    ts_["u_ps"] = [ps("ups0", [128, 512], F32), ps("ups1", [128, 512], F32)]
    ts_["d_ps"] = [ps("dps0", [128, 512], F32), ps("dps1", [128, 512], F32)]

    with nc.Block() as block:
        p = Plan(nc, es)
        ld_x = p.sy(lambda s: s.dma_start(x_sb[:, :, :], xin.rearrange("(t p) d -> p t d", p=128)))
        ld_id = p.sy(lambda s: s.dma_start(ts_["ident"][:], identin[:, :]))
        ld_rw = p.g(lambda g: g.dma_start(rwb[:, :, :], rw.rearrange("(c p) e -> p c e", p=128)))

        evac_ops, _ = emit_ln(p, ts_, NT, x_sb, xlnT, [ld_x, ld_id])
        if layer == 0:
            coef_ops = emit_router_l0(p, ts_, NT, xlnT, rwb, evac_ops, ld_rw)

            def wsrc(e, h8):
                hs = slice(h8 * 512, (h8 + 1) * 512)
                if e < 7:
                    return (up7[e, :, hs].rearrange("(c p) h -> p c h", p=128),
                            gate7[e, :, hs].rearrange("(c p) h -> p c h", p=128),
                            down7[e, hs, :].rearrange("(c p) d -> p c d", p=128))
                return (sup[:, hs].rearrange("(c p) h -> p c h", p=128),
                        sgate[:, hs].rearrange("(c p) h -> p c h", p=128),
                        sdown[hs, :].rearrange("(c p) d -> p c d", p=128))

            n_routed, shared_coef = 7, 0.5
        else:
            coef_ops = emit_router_l1(p, ts_, NT, xlnT, rwb, evac_ops, ld_rw)

            def wsrc(e, h8):
                hs = slice(h8 * 512, (h8 + 1) * 512)
                return (w2[e, :, hs].rearrange("(c p) h -> p c h", p=128),
                        w1[e, :, hs].rearrange("(c p) h -> p c h", p=128),
                        w3[e, hs, :].rearrange("(c p) d -> p c d", p=128))

            n_routed, shared_coef = 8, None

        last_x = emit_moe(p, ts_, NT, T, xlnT, x_sb, ts_["coefs"], wsrc,
                          ln_done=[evac_ops[-1], ld_rw], router_done=[coef_ops[-1]],
                          H=H, n_routed=n_routed, shared_coef=shared_coef,
                          silu_on_gate=True)

        st = p.sy(lambda s: s.dma_start(xout.rearrange("(t p) d -> p t d", p=128), x_sb[:, :, :]), [last_x])
        p.sy(lambda s: s.nop(), [st], dma=False)
        p.finalize_and_emit(block)
    es.close()
    return nc



def emit_ln_stream(p, ts_, NT, xin_tiled, xlnT, deps0, tr_war=None):
    stats = ts_["stats"]
    sq = ts_["sq"]
    xb = ts_["xln_scr"]
    xs = ts_["x_stream"]
    tr_ps = ts_["tr_ps"]
    ident = ts_["ident"]

    def st(t, j):
        return stats[:, t, j:j + 1]

    evac_ops = []
    prev_use = [None, None]
    tr_prev = tr_war
    evac_prev2 = [None, None]
    for t in range(NT):
        ld = p.sy(lambda s, t=t: s.dma_start(xs[t % 2][:], xin_tiled[t]), [prev_use[t % 2]] + deps0)
        rs = p.v(lambda v, t=t: v.reduce_sum(st(t, 0), xs[t % 2][:], axis=AX.X, negate=True), [ld])
        rq = p.s(lambda s, t=t: s.activation(sq[:], xs[t % 2][:], AF.Square, accum_out=st(t, 1)), [ld])
        mn = p.v(lambda v, t=t: v.tensor_scalar(st(t, 2), st(t, 0), 1.0 / D, None, ALU.mult), [rs])
        m2 = p.v(lambda v, t=t: v.tensor_tensor(st(t, 3), st(t, 2), st(t, 2), ALU.mult), [mn])
        va = p.v(lambda v, t=t: v.tensor_scalar(st(t, 4), st(t, 1), 1.0 / D, 1e-5, ALU.mult, ALU.add), [rq])
        vb = p.v(lambda v, t=t: v.tensor_tensor(st(t, 4), st(t, 4), st(t, 3), ALU.subtract), [va, m2])
        ln = p.s(lambda s, t=t: s.activation(st(t, 5), st(t, 4), AF.Ln), [vb])
        rstd = p.s(lambda s, t=t: s.activation(st(t, 6), st(t, 5), AF.Exp, scale=-0.5), [ln])
        xop = p.v(lambda v, t=t: v.tensor_scalar(xb[t % 2][:], xs[t % 2][:], st(t, 2), st(t, 6), ALU.add, ALU.mult),
                  [rstd, evac_prev2[t % 2]])
        prev_use[t % 2] = xop
        trs = []
        for dc in range(8):
            trs.append(p.t(lambda t_, t=t, dc=dc: t_.transpose(
                tr_ps[:, dc, :], xb[t % 2][:, dc * 128:(dc + 1) * 128], ident[:]),
                [xop] + ([tr_prev] if dc == 0 else [])))
        ev = p.v(lambda v, t=t: v.tensor_copy(xlnT[:, :, t * 128:(t + 1) * 128], tr_ps[:, :, :]), [trs[-1]])
        evac_ops.append(ev)
        evac_prev2[t % 2] = ev
        tr_prev = ev
    return evac_ops


def emit_w_transpose(p, ts_, w_dram, WT, deps0, war_dep):
    wstage = ts_["wstage"]
    tr_ps = ts_["tr_ps"]
    ident = ts_["ident"]
    w4 = w_dram.rearrange("(fc p) d -> fc p d", p=128)
    tr_prev = war_dep
    ld_first = None
    for hf in range(2):
        ld = p.g(lambda g, hf=hf: g.dma_start(wstage[:, :, :], w4[hf * 4:(hf + 1) * 4].rearrange("fc p d -> p fc d")),
                 deps0 + [tr_prev])
        if ld_first is None:
            ld_first = ld
        for dc in range(8):
            trs = []
            for fc in range(4):
                trs.append(p.t(lambda t_, dc=dc, fc=fc: t_.transpose(
                    tr_ps[:, fc, :], wstage[:, fc, dc * 128:(dc + 1) * 128], ident[:]),
                    [ld] + ([tr_prev] if fc == 0 else [])))
            ev = p.v(lambda v, dc=dc, hf=hf: v.tensor_copy(
                WT[:, dc, hf * 512:(hf + 1) * 512], tr_ps[:, 0:4, :]), [trs[-1]])
            tr_prev = ev
    return ld_first, tr_prev


def emit_proj_fm(p, ts_, WT, xlnT, OUT, Tn, wt_dep, xln_dep, rope, bank_war):
    pr_ps = ts_["pr_ps"]
    rt_ps = ts_.get("rt_ps")
    qraw = ts_.get("qraw")
    rsc = ts_.get("rope_scr")
    slices = [(s, min(512, Tn - s)) for s in range(0, Tn, 512)]
    out_ops = []
    bank_reader = [bank_war, bank_war]
    qraw_reader = [None, None]
    rt_reader = None
    rsc_reader = [None, None]
    k = 0
    for qc in range(8):
        for (s0, sl) in slices:
            pb = k % 2
            k += 1
            mm = None
            for dc in range(8):
                mm = p.t(lambda t_, qc=qc, s0=s0, sl=sl, dc=dc, pb=pb: t_.matmul(
                    pr_ps[pb][:, 0:sl], WT[:, dc, qc * 128:(qc + 1) * 128],
                    xlnT[:, dc, s0:s0 + sl],
                    start=(dc == 0), stop=(dc == 7)),
                    ([wt_dep, xln_dep, bank_reader[pb]] if dc == 0 else []))
            if rope is None:
                ev = p.v(lambda v, qc=qc, s0=s0, sl=sl, pb=pb: v.tensor_copy(
                    OUT[:, qc, s0:s0 + sl], pr_ps[pb][:, 0:sl]), [mm])
                bank_reader[pb] = ev
                out_ops.append(ev)
                continue
            permb, cosT, sinT, off = rope
            cs = cosT[:, off + s0: off + s0 + sl]
            sn = sinT[:, off + s0: off + s0 + sl]
            qr = p.v(lambda v, pb=pb, sl=sl: v.tensor_copy(qraw[pb][:, 0:sl], pr_ps[pb][:, 0:sl]), [mm, qraw_reader[pb]])
            bank_reader[pb] = qr
            rt = p.t(lambda t_, pb=pb, sl=sl: t_.matmul(rt_ps[:, 0:sl], permb[:], qraw[pb][:, 0:sl], start=True, stop=True),
                     [qr, rt_reader])
            t1 = p.v(lambda v, pb=pb, cs=cs, sl=sl: v.tensor_tensor(rsc[pb][:, 0:sl], qraw[pb][:, 0:sl], cs, ALU.mult),
                     [qr, rsc_reader[pb]])
            qraw_reader[pb] = t1
            t2 = p.v(lambda v, pb=pb, sn=sn, sl=sl: v.tensor_tensor(rt_ps[:, 0:sl], rt_ps[:, 0:sl], sn, ALU.mult), [rt])
            ev = p.v(lambda v, qc=qc, s0=s0, sl=sl, pb=pb: v.tensor_tensor(
                OUT[:, qc, s0:s0 + sl], rsc[pb][:, 0:sl], rt_ps[:, 0:sl], ALU.add), [t2, t1])
            rt_reader = ev
            rsc_reader[pb] = ev
            out_ops.append(ev)
    return out_ops


def emit_normbound(p, ts_, QT, KT, Tq, Tk, indm, ones1, q_deps, k_deps, scale):
    sqscr = ts_["sqscr"]
    nb_ps = ts_["nb_ps"]
    nstat = ts_["nstat"]
    ntr = ts_["ntr"]
    negmh = ts_["negmh"]

    state = {"ps_reader": None}

    def half(T_, XT, deps, col):
        slices = [(s, min(512, T_ - s)) for s in range(0, T_, 512)]
        run = None
        sq_reader = [None, None]
        for (s0, sl) in slices:
            mm = None
            for qc in range(8):
                pb = qc % 2
                sq = p.s(lambda s, qc=qc, s0=s0, sl=sl, pb=pb, XT=XT: s.activation(
                    sqscr[pb][:, 0:sl], XT[:, qc, s0:s0 + sl], AF.Square),
                    deps + [sq_reader[pb]])
                mm = p.t(lambda t_, qc=qc, pb=pb, sl=sl: t_.matmul(
                    nb_ps[0:16, 0:sl], indm[:, qc, :], sqscr[pb][:, 0:sl],
                    start=(qc == 0), stop=(qc == 7)),
                    [sq] + ([state["ps_reader"]] if qc == 0 else []))
                sq_reader[pb] = mm
            rm = p.v(lambda v, col=col, sl=sl: v.reduce_max(
                nstat[0:16, col + 2:col + 3], nb_ps[0:16, 0:sl], axis=AX.X), [mm])
            state["ps_reader"] = rm
            if run is None:
                run = p.v(lambda v, col=col: v.tensor_copy(
                    nstat[0:16, col:col + 1], nstat[0:16, col + 2:col + 3]), [rm])
            else:
                run = p.v(lambda v, col=col: v.tensor_tensor(
                    nstat[0:16, col:col + 1], nstat[0:16, col:col + 1],
                    nstat[0:16, col + 2:col + 3], ALU.max), [rm, run])
        return run

    rq = half(Tq, QT, q_deps, 0)
    rk = half(Tk, KT, k_deps, 1)
    pr = p.v(lambda v: v.tensor_tensor(nstat[0:16, 4:5], nstat[0:16, 0:1], nstat[0:16, 1:2], ALU.mult), [rq, rk])
    ln = p.s(lambda s: s.activation(nstat[0:16, 5:6], nstat[0:16, 4:5], AF.Ln), [pr])
    sq = p.s(lambda s: s.activation(nstat[0:16, 6:7], nstat[0:16, 5:6], AF.Exp, scale=0.5), [ln])
    ng = p.v(lambda v: v.tensor_scalar(nstat[0:16, 0:1], nstat[0:16, 6:7], -float(scale), None, ALU.mult), [sq, pr])
    tr = p.v(lambda v: v.transpose(ntr[:, :], nstat[:, 0:32]), [ng])
    bc = p.t(lambda t_: t_.matmul(nb_ps[:, 0:16], ones1[0:1, :], ntr[0:1, 0:16],
                                  start=True, stop=True), [tr, state["ps_reader"]])
    ev = p.v(lambda v: v.tensor_copy(negmh[:, :], nb_ps[:, 0:16]), [bc])
    return ev


def emit_proj_tok(p, ts_, WT, xlnT, V, NTk, wt_dep, xln_dep, bank_war):
    pr_ps = ts_["pr_ps"]
    out_ops = []
    bank_reader = [bank_war, bank_war]
    k = 0
    for tt in range(NTk):
        for fcol in range(2):
            pb = k % 2
            mm = None
            for dc in range(8):
                mm = p.t(lambda t_, tt=tt, fcol=fcol, dc=dc, pb=pb: t_.matmul(
                    pr_ps[pb][:, :], xlnT[:, dc, tt * 128:(tt + 1) * 128],
                    WT[:, dc, fcol * 512:(fcol + 1) * 512],
                    start=(dc == 0), stop=(dc == 7)),
                    ([wt_dep, xln_dep, bank_reader[pb]] if dc == 0 else []))
            ev = p.v(lambda v, tt=tt, fcol=fcol, pb=pb: v.tensor_copy(
                V[:, tt, fcol * 512:(fcol + 1) * 512], pr_ps[pb][:, :]), [mm])
            bank_reader[pb] = ev
            out_ops.append(ev)
            k += 1
    return out_ops


def build_attn_l0(debug_stage=0):
    TQ, TK = 2048, 2304
    NTQ, NTK = 16, 18
    nc = bass.Bass("TRN2", target_bir_lowering=False)
    xin = nc.dram_tensor("xin", [TK, D], F32, kind="ExternalInput")
    wq = nc.dram_tensor("wq", [D, D], F32, kind="ExternalInput")
    wk = nc.dram_tensor("wk", [D, D], F32, kind="ExternalInput")
    wv = nc.dram_tensor("wv", [D, D], F32, kind="ExternalInput")
    wo = nc.dram_tensor("wo", [D, D], F32, kind="ExternalInput")
    identin = nc.dram_tensor("identin", [128, 128], BF16, kind="ExternalInput")
    permin = nc.dram_tensor("permin", [128, 128], BF16, kind="ExternalInput")
    cosin = nc.dram_tensor("cosin", [128, TK], BF16, kind="ExternalInput")
    sinin = nc.dram_tensor("sinin", [128, TK], BF16, kind="ExternalInput")
    maskin = nc.dram_tensor("maskin", [128, 3, 384], F32, kind="ExternalInput")
    indmin = nc.dram_tensor("indmin", [128, 8, 16], BF16, kind="ExternalInput")
    onesin = nc.dram_tensor("onesin", [1, 128], F32, kind="ExternalInput")
    xout = nc.dram_tensor("xout", [TQ, D], F32, kind="ExternalOutput")

    es = ExitStack()
    es1 = ExitStack()
    sb = lambda name, shape, dt: es.enter_context(nc.sbuf_tensor(name, shape, dt))
    sb1 = lambda name, shape, dt: es1.enter_context(nc.sbuf_tensor(name, shape, dt))
    ps = lambda name, shape, dt: es.enter_context(nc.psum_tensor(name, shape, dt))

    ts_ = {}
    ts_["ident"] = sb("ident", [128, 128], BF16)
    WT = sb("WT", [128, 8, D], BF16)
    QT = sb("QT", [128, 8, TQ], BF16)
    KT = sb("KT", [128, 8, TK], BF16)
    V = sb("V", [128, NTK, D], BF16)
    indm = sb("indm", [128, 8, 16], BF16)
    ones1 = sb("ones1", [1, 128], F32)
    ts_["negmh"] = sb("negmh", [128, 16], F32)
    xlnT = sb1("xlnT", [128, 8, TK], BF16)
    ts_["stats"] = sb1("stats", [128, NTK, 8], F32)
    ts_["sq"] = sb1("sq", [128, D], F32)
    ts_["xln_scr"] = [sb1("xlns0", [128, D], BF16), sb1("xlns1", [128, D], BF16)]
    ts_["x_stream"] = [sb1("xs0", [128, D], F32), sb1("xs1", [128, D], F32)]
    ts_["wstage"] = sb1("wstage", [128, 4, D], BF16)
    perm = sb1("perm", [128, 128], BF16)
    cosT = sb1("cosT", [128, TK], BF16)
    sinT = sb1("sinT", [128, TK], BF16)
    ts_["qraw"] = [sb1("qraw0", [128, 512], BF16), sb1("qraw1", [128, 512], BF16)]
    ts_["rope_scr"] = [sb1("rsc0", [128, 512], F32), sb1("rsc1", [128, 512], F32)]
    ts_["sqscr"] = [sb1("sqs0", [128, 512], BF16), sb1("sqs1", [128, 512], BF16)]
    ts_["nstat"] = sb1("nstat", [32, 32], F32)
    ts_["ntr"] = sb1("ntr", [32, 32], F32)

    ts_["tr_ps"] = ps("tr_ps", [128, 8, 128], BF16)
    ts_["pr_ps"] = [ps("prps0", [128, 512], F32), ps("prps1", [128, 512], F32)]
    ts_["rt_ps"] = ps("rt_ps", [128, 512], F32)
    s_ps = [ps("sps0", [128, 512], F32), ps("sps1", [128, 512], F32)]
    y_ps = [ps("yps0", [128, 512], F32), ps("yps1", [128, 512], F32)]
    ts_["nb_ps"] = s_ps[0]
    ptr_ps = ts_["tr_ps"]
    ytr_ps = ts_["tr_ps"]
    wo_ps = ts_["pr_ps"]

    with nc.Block() as block:
        p = Plan(nc, es)
        ld_id = p.sy(lambda s: s.dma_start(ts_["ident"][:], identin[:, :]))
        ld_pm = p.sy(lambda s: s.dma_start(perm[:], permin[:, :]))
        ld_cs = p.sy(lambda s: s.dma_start(cosT[:], cosin[:, :]))
        ld_sn = p.sy(lambda s: s.dma_start(sinT[:], sinin[:, :]))
        ld_im = p.sy(lambda s: s.dma_start(indm[:, :, :], indmin[:, :, :]))
        ld_on = p.sy(lambda s: s.dma_start(ones1[:], onesin[:, :]))

        xin_tiled = xin.rearrange("(t p) d -> t p d", p=128)
        evac_ops = emit_ln_stream(p, ts_, NTK, xin_tiled, xlnT, [ld_id])
        xln_dep = evac_ops[-1]

        v_ops = q_ops = k_ops = [xln_dep]
        q_dep = k_dep = [xln_dep]
        nb_ev = woT = xln_dep
        if debug_stage == 0 or debug_stage >= 2 or debug_stage == 12:
            _, wvT = emit_w_transpose(p, ts_, wv[:, :], WT, [ld_id], xln_dep)
            v_ops = [wvT]
            if debug_stage != 12:
                v_ops = emit_proj_tok(p, ts_, WT, xlnT, V, NTK, wvT, xln_dep, None)
        if debug_stage == 0 or debug_stage >= 3:
            _, wqT = emit_w_transpose(p, ts_, wq[:, :], WT, [], v_ops[-1])
            q_ops = emit_proj_fm(p, ts_, WT, xlnT[:, :, 256:256 + TQ], QT, TQ, wqT, xln_dep,
                                 (perm, cosT, sinT, 256), v_ops[-1])
            q_dep = [q_ops[-1], ld_pm, ld_cs, ld_sn]
        if debug_stage == 0 or debug_stage >= 4:
            _, wkT = emit_w_transpose(p, ts_, wk[:, :], WT, [], q_ops[-1])
            k_ops = emit_proj_fm(p, ts_, WT, xlnT, KT, TK, wkT, xln_dep,
                                 (perm, cosT, sinT, 0), q_ops[-1])
            k_dep = [k_ops[-1], ld_pm, ld_cs, ld_sn]
        if debug_stage == 0 or debug_stage >= 5:
            _, woT = emit_w_transpose(p, ts_, wo[:, :], WT, [], k_ops[-1])
            nb_ev = emit_normbound(p, ts_, QT, KT, TQ, TK, indm, ones1,
                                   q_dep + [ld_im], k_dep, 0.125)

        if debug_stage:
            if debug_stage == 12:
                dw = nc.dram_tensor("dbg_wt", [128, 8, D], BF16, kind="ExternalOutput")
                dl = p.sy(lambda s: s.dma_start(dw[:, :, :], WT[:, :, :]), [wvT])
                p.sy(lambda s: s.nop(), [dl], dma=False)
                p.finalize_and_emit(block)
                es1.close(); es.close()
                return nc
            dq = nc.dram_tensor("dbg_qt", [128, 8, TQ], BF16, kind="ExternalOutput")
            dk = nc.dram_tensor("dbg_kt", [128, 8, TK], BF16, kind="ExternalOutput")
            dv = nc.dram_tensor("dbg_v", [128, NTK, D], BF16, kind="ExternalOutput")
            dn = nc.dram_tensor("dbg_nm", [128, 16], F32, kind="ExternalOutput")
            dls = []
            if debug_stage >= 2:
                dls.append(p.sy(lambda s: s.dma_start(dv[:, :, :], V[:, :, :]), [v_ops[-1]]))
            if debug_stage >= 3:
                dls.append(p.sy(lambda s: s.dma_start(dq[:, :, :], QT[:, :, :]), q_dep))
            if debug_stage >= 4:
                dls.append(p.sy(lambda s: s.dma_start(dk[:, :, :], KT[:, :, :]), k_dep))
            if debug_stage >= 5:
                dls.append(p.sy(lambda s: s.dma_start(dn[:, :], ts_["negmh"][:, :]), [nb_ev, woT]))
            p.sy(lambda s: s.nop(), [dls[-1] if dls else xln_dep], dma=False)
            p.finalize_and_emit(block)
            es1.close()
            es.close()
            return nc

        es1.close()
        masks = sb("masks", [128, 3, 384], F32)
        P_sb = [sb("psb0", [128, 384], BF16), sb("psb1", [128, 384], BF16)]
        PT_sb = [sb("ptsb0", [128, 3, 128], BF16), sb("ptsb1", [128, 3, 128], BF16)]
        lbuf = sb("lbuf", [128, 16], F32)
        linv = sb("linv", [128, 16], F32)
        y_tile = [sb("yt0", [128, D], BF16), sb("yt1", [128, D], BF16)]
        yT_sb = sb("yTsb", [128, 8, 128], BF16)
        xadd = [sb("xadd0", [128, D], F32), sb("xadd1", [128, D], F32)]
        ld_mk = p.sy(lambda s: s.dma_start(masks[:, :, :], maskin[:, :, :]),
                     [nb_ev, woT, xln_dep, q_ops[-1], k_ops[-1], v_ops[-1]])

        negmh = ts_["negmh"]
        s_reader = [nb_ev, nb_ev]
        y_reader = [None, None]
        pt_reader = {0: woT}
        psb_reader = [None, None]
        yT_reader = None
        ytile_reader = [None, None]
        wo_reader = [k_ops[-1], k_ops[-1]]
        xadd_reader = [None, None]
        store_ops = []
        last_store = None
        for t in range(NTQ):
            mi = min(t, 2)
            ytile_ops = []
            for h in range(16):
                hb = (h % 2) * 64
                qc = h // 2
                smm = p.t(lambda t_, t=t, h=h, hb=hb, qc=qc: t_.matmul(
                    s_ps[h % 2][:, 0:384],
                    QT[hb:hb + 64, qc, t * 128:(t + 1) * 128],
                    KT[hb:hb + 64, qc, t * 128:t * 128 + 384],
                    start=True, stop=True),
                    [s_reader[h % 2]] + (q_dep + k_dep if (t == 0 and h == 0) else []))
                madd = p.v(lambda v, t=t, h=h, mi=mi: v.tensor_tensor(
                    s_ps[h % 2][:, 0:384], s_ps[h % 2][:, 0:384], masks[:, mi, :], ALU.add),
                    [smm, ld_mk])
                expo = p.s(lambda s, t=t, h=h: s.activation(
                    P_sb[h % 2][:], s_ps[h % 2][:, 0:384], AF.Exp, scale=0.125,
                    bias=negmh[:, h:h + 1], accum_out=lbuf[:, h:h + 1]), [madd, nb_ev, psb_reader[h % 2]])
                s_reader[h % 2] = expo
                li = p.v(lambda v, h=h: v.reciprocal(linv[:, h:h + 1], lbuf[:, h:h + 1]), [expo])
                trs = []
                for jc in range(3):
                    trs.append(p.t(lambda t_, h=h, jc=jc: t_.transpose(
                        ptr_ps[:, jc, :], P_sb[h % 2][:, jc * 128:(jc + 1) * 128], ts_["ident"][:]),
                        [expo] + ([pt_reader[0]] if jc == 0 else [])))
                pte = p.v(lambda v, h=h: v.tensor_copy(PT_sb[h % 2][:, :, :], ptr_ps[:, 0:3, :]), [trs[-1]])
                psb_reader[h % 2] = trs[-1]
                pt_reader[0] = pte
                pvs = None
                for jc in range(3):
                    pvs = p.t(lambda t_, t=t, h=h, jc=jc: t_.matmul(
                        y_ps[h % 2][:, 0:64], PT_sb[h % 2][:, jc, :], V[:, t + jc, h * 64:(h + 1) * 64],
                        start=(jc == 0), stop=(jc == 2)),
                        [pte, v_ops[-1], y_reader[h % 2]] if jc == 0 else [])
                se = p.v(lambda v, t=t, h=h: v.tensor_scalar(
                    y_tile[t % 2][:, h * 64:(h + 1) * 64], y_ps[h % 2][:, 0:64],
                    linv[:, h:h + 1], None, ALU.mult), [pvs, li, ytile_reader[t % 2] if h == 0 else None])
                y_reader[h % 2] = se
                ytile_ops.append(se)
            ytrs = []
            for fc in range(8):
                ytrs.append(p.t(lambda t_, t=t, fc=fc: t_.transpose(
                    ytr_ps[:, fc, :], y_tile[t % 2][:, fc * 128:(fc + 1) * 128], ts_["ident"][:]),
                    [ytile_ops[-1]] + ([pt_reader[0]] if fc == 0 else [])))
            yte = p.v(lambda v, t=t: v.tensor_copy(yT_sb[:, :, :], ytr_ps[:, :, :]), [ytrs[-1], yT_reader])
            ytile_reader[t % 2] = ytrs[-1]
            pt_reader[0] = yte
            ldx = p.sy(lambda s, t=t: s.dma_start(
                xadd[t % 2][:], xin_tiled[t + 2]), [xadd_reader[t % 2]])
            adds = []
            for dcol in range(2):
                wmm = None
                for fc in range(8):
                    wmm = p.t(lambda t_, t=t, dcol=dcol, fc=fc: t_.matmul(
                        wo_ps[dcol][:, :], yT_sb[:, fc, :], WT[:, fc, dcol * 512:(dcol + 1) * 512],
                        start=(fc == 0), stop=(fc == 7)),
                        [yte, wo_reader[dcol]] if fc == 0 else [])
                ad = p.v(lambda v, t=t, dcol=dcol: v.tensor_tensor(
                    xadd[t % 2][:, dcol * 512:(dcol + 1) * 512], wo_ps[dcol][:, :],
                    xadd[t % 2][:, dcol * 512:(dcol + 1) * 512], ALU.add), [wmm, ldx])
                wo_reader[dcol] = ad
                adds.append(ad)
            yT_reader = wmm
            stv = p.sy(lambda s, t=t: s.dma_start(
                xout.rearrange("(t p) d -> t p d", p=128)[t], xadd[t % 2][:]), adds)
            xadd_reader[t % 2] = stv
            store_ops.append(stv)
            last_store = stv
        p.sy(lambda s: s.nop(), [last_store], dma=False)
        p.finalize_and_emit(block)
    es.close()
    return nc


def emit_normsq_half(p, ts_, XT, T_, indm, col, deps, run_prev):
    sqscr = ts_["sqscr"]
    nb_ps = ts_["nb_ps"]
    nstat = ts_["nstat"]
    slices = [(s, min(512, T_ - s)) for s in range(0, T_, 512)]
    run = run_prev
    sq_reader = ts_.setdefault("_nb_sq_reader", [None, None])
    for (s0, sl) in slices:
        mm = None
        for qc in range(8):
            pb = qc % 2
            sq = p.s(lambda s, qc=qc, s0=s0, sl=sl, pb=pb: s.activation(
                sqscr[pb][:, 0:sl], XT[:, qc, s0:s0 + sl], AF.Square),
                deps + [sq_reader[pb]])
            mm = p.t(lambda t_, qc=qc, pb=pb, sl=sl: t_.matmul(
                nb_ps[0:16, 0:sl], indm[:, qc, :], sqscr[pb][:, 0:sl],
                start=(qc == 0), stop=(qc == 7)),
                [sq] + ([ts_.get("_nb_ps_reader")] if qc == 0 else []))
            sq_reader[pb] = mm
        rm = p.v(lambda v, col=col, sl=sl: v.reduce_max(
            nstat[0:16, col + 2:col + 3], nb_ps[0:16, 0:sl], axis=AX.X), [mm])
        ts_["_nb_ps_reader"] = rm
        if run is None:
            run = p.v(lambda v, col=col: v.tensor_copy(
                nstat[0:16, col:col + 1], nstat[0:16, col + 2:col + 3]), [rm])
        else:
            run = p.v(lambda v, col=col: v.tensor_tensor(
                nstat[0:16, col:col + 1], nstat[0:16, col:col + 1],
                nstat[0:16, col + 2:col + 3], ALU.max), [rm, run])
    return run


def emit_normbound_finish(p, ts_, ones1, rq, rk, scale):
    nb_ps = ts_["nb_ps"]
    nstat = ts_["nstat"]
    ntr = ts_["ntr"]
    negmh = ts_["negmh"]
    pr = p.v(lambda v: v.tensor_tensor(nstat[0:16, 4:5], nstat[0:16, 0:1], nstat[0:16, 1:2], ALU.mult), [rq, rk])
    ln = p.s(lambda s: s.activation(nstat[0:16, 5:6], nstat[0:16, 4:5], AF.Ln), [pr])
    sq = p.s(lambda s: s.activation(nstat[0:16, 6:7], nstat[0:16, 5:6], AF.Exp, scale=0.5), [ln])
    ng = p.v(lambda v: v.tensor_scalar(nstat[0:16, 0:1], nstat[0:16, 6:7], -float(scale), None, ALU.mult), [sq, pr])
    tr = p.v(lambda v: v.transpose(ntr[:, :], nstat[:, 0:32]), [ng])
    bc = p.t(lambda t_: t_.matmul(nb_ps[:, 0:16], ones1[0:1, :], ntr[0:1, 0:16],
                                  start=True, stop=True), [tr, ts_.get("_nb_ps_reader")])
    ev = p.v(lambda v: v.tensor_copy(negmh[:, :], nb_ps[:, 0:16]), [bc])
    ts_["_nb_ps_reader"] = ev
    return ev


def build_attn_l1():
    TQ = 2048
    SB_ = 8192
    nc = bass.Bass("TRN2", target_bir_lowering=False)
    xq = nc.dram_tensor("xq", [TQ, D], F32, kind="ExternalInput")
    xb = nc.dram_tensor("xb", [SB_, D], F32, kind="ExternalInput")
    wqkv = nc.dram_tensor("wqkv", [3 * D, D], F32, kind="ExternalInput")
    identin = nc.dram_tensor("identin", [128, 128], BF16, kind="ExternalInput")
    identfin = nc.dram_tensor("identfin", [128, 128], F32, kind="ExternalInput")
    indmin = nc.dram_tensor("indmin", [128, 8, 16], BF16, kind="ExternalInput")
    onesin = nc.dram_tensor("onesin", [1, 128], F32, kind="ExternalInput")
    xout = nc.dram_tensor("xout", [TQ, D], F32, kind="ExternalOutput")
    kt_d = nc.dram_tensor("kt_d", [D, SB_], BF16, kind="Internal")
    v_d = nc.dram_tensor("v_d", [SB_, 16, 65], BF16, kind="Internal")

    es = ExitStack()
    es1 = ExitStack()
    sb = lambda name, shape, dt: es.enter_context(nc.sbuf_tensor(name, shape, dt))
    sb1 = lambda name, shape, dt: es1.enter_context(nc.sbuf_tensor(name, shape, dt))
    ps = lambda name, shape, dt: es.enter_context(nc.psum_tensor(name, shape, dt))

    ts_ = {}
    ts_["ident"] = sb("ident", [128, 128], BF16)
    identf = sb("identf", [128, 128], F32)
    QT = sb("QT", [128, 8, TQ], BF16)
    indm = sb("indm", [128, 8, 16], BF16)
    ones1 = sb("ones1", [1, 128], F32)
    ts_["negmh"] = sb("negmh", [128, 16], F32)
    ts_["nstat"] = sb("nstat", [32, 32], F32)
    ts_["ntr"] = sb("ntr", [32, 32], F32)
    xlnT = sb1("xlnT", [128, 8, 2048], BF16)
    KTg = sb1("KTg", [128, 8, 2048], BF16)
    WTk = sb1("WTk", [128, 8, D], BF16)
    WTv = sb1("WTv", [128, 8, D], BF16)
    ts_["stats"] = sb1("stats", [128, 16, 8], F32)
    ts_["sq"] = sb1("sq", [128, D], F32)
    ts_["xln_scr"] = [sb1("xlns0", [128, D], BF16), sb1("xlns1", [128, D], BF16)]
    ts_["x_stream"] = [sb1("xs0", [128, D], F32), sb1("xs1", [128, D], F32)]
    ts_["wstage"] = sb1("wstage", [128, 4, D], BF16)
    ts_["qraw"] = [sb1("qraw0", [128, 512], BF16), sb1("qraw1", [128, 512], BF16)]
    ts_["rope_scr"] = [sb1("rsc0", [128, 512], F32), sb1("rsc1", [128, 512], F32)]
    ts_["sqscr"] = [sb1("sqs0", [128, 512], BF16), sb1("sqs1", [128, 512], BF16)]
    v_stage = [sb1("vst0", [128, 16, 65], BF16), sb1("vst1", [128, 16, 65], BF16)]

    ts_["tr_ps"] = ps("tr_ps", [128, 8, 128], BF16)
    s_ps = [ps("sps0", [128, 512], F32), ps("sps1", [128, 512], F32)]
    y_ps = [[ps("yps00", [128, 512], F32), ps("yps01", [128, 512], F32)],
            [ps("yps10", [128, 512], F32), ps("yps11", [128, 512], F32)]]
    ts_["pr_ps"] = [y_ps[0][0], y_ps[0][1]]
    ts_["rt_ps"] = s_ps[0]
    ts_["nb_ps"] = s_ps[0]
    ltr_ps = s_ps[1]

    with nc.Block() as block:
        p = Plan(nc, es)
        ld_id = p.sy(lambda s: s.dma_start(ts_["ident"][:], identin[:, :]))
        ld_if = p.sy(lambda s: s.dma_start(identf[:], identfin[:, :]))
        ld_im = p.sy(lambda s: s.dma_start(indm[:, :, :], indmin[:, :, :]))
        ld_on = p.sy(lambda s: s.dma_start(ones1[:], onesin[:, :]))
        ms0 = p.v(lambda v: v.memset(v_stage[0][:, :, 64:65], 1.0))
        ms1 = p.v(lambda v: v.memset(v_stage[1][:, :, 64:65], 1.0))

        xq_tiled = xq.rearrange("(t p) d -> t p d", p=128)
        evq = emit_ln_stream(p, ts_, 16, xq_tiled, xlnT, [ld_id])
        _, wqT = emit_w_transpose(p, ts_, wqkv[0:D, :], WTk, [ld_id], evq[-1])
        q_ops = emit_proj_fm(p, ts_, WTk, xlnT, QT, TQ, wqT, evq[-1], None, None)
        rq = emit_normsq_half(p, ts_, QT, TQ, indm, 0, [q_ops[-1], ld_im], None)

        _, wkT = emit_w_transpose(p, ts_, wqkv[D:2 * D, :], WTk, [], q_ops[-1])
        _, wvT = emit_w_transpose(p, ts_, wqkv[2 * D:3 * D, :], WTv, [], wkT)
        xb_tiled = xb.rearrange("(t p) d -> t p d", p=128)
        rk = None
        kt_dT = kt_d.rearrange("(dc p) j -> p dc j", p=128)
        v_dT = v_d.rearrange("(tt p) h c -> tt p (h c)", p=128)
        st_ops = []
        vst_reader = [ms0, ms1]
        tr_war = wvT
        for g in range(4):
            evg = emit_ln_stream(p, ts_, 16, xb_tiled[g * 16:(g + 1) * 16], xlnT,
                                 [q_ops[-1], rq], tr_war=tr_war)
            tr_war = evg[-1]
            kg_ops = emit_proj_fm(p, ts_, WTk, xlnT, KTg, 2048, wkT, evg[-1], None,
                                  st_ops[-1] if st_ops else None)
            rk = emit_normsq_half(p, ts_, KTg, 2048, indm, 1, [kg_ops[-1]], rk)
            stk = p.sy(lambda s, g=g: s.dma_start(kt_dT[:, :, g * 2048:(g + 1) * 2048],
                                                  KTg[:, :, :]), [kg_ops[-1]])
            st_ops.append(stk)
            pr_ps = ts_["pr_ps"]
            bank_reader = [kg_ops[-1], kg_ops[-1]]
            k = 0
            for tt in range(16):
                vb = tt % 2
                evs = []
                for fcol in range(2):
                    pb = k % 2
                    k += 1
                    mm = None
                    for dc in range(8):
                        mm = p.t(lambda t_, tt=tt, fcol=fcol, dc=dc, pb=pb: t_.matmul(
                            pr_ps[pb][:, :], xlnT[:, dc, tt * 128:(tt + 1) * 128],
                            WTv[:, dc, fcol * 512:(fcol + 1) * 512],
                            start=(dc == 0), stop=(dc == 7)),
                            ([wvT, evg[-1], bank_reader[pb]] if dc == 0 else []))
                    ev = p.v(lambda v, tt=tt, fcol=fcol, pb=pb, vb=vb: v.tensor_copy(
                        v_stage[vb][:, fcol * 8:(fcol + 1) * 8, 0:64],
                        pr_ps[pb][:, :].rearrange("p (h c) -> p h c", c=64)), [mm, vst_reader[vb]])
                    bank_reader[pb] = ev
                    evs.append(ev)
                stv = p.sy(lambda s, g=g, tt=tt, vb=vb: s.dma_start(
                    v_dT[g * 16 + tt], v_stage[vb].rearrange("p h c -> p (h c)")), evs)
                vst_reader[vb] = stv
                st_ops.append(stv)

        nb_ev = emit_normbound_finish(p, ts_, ones1, rq, rk, 0.125)

        es1.close()
        kt_sb = [sb("kt0", [128, SB_], BF16), sb("kt1", [128, SB_], BF16)]
        vp_sb = [sb("vp0", [128, 64, 130], BF16), sb("vp1", [128, 64, 130], BF16)]
        yT_all = sb("yT_all", [128, 8, TQ], BF16)
        pT_sb = [sb("pt0", [128, 512], BF16), sb("pt1", [128, 512], BF16)]
        lrow = sb("lrow", [1, 512], F32)
        ltok = sb("ltok", [128, 16, 16], F32)
        linv_all = sb("linv", [128, 16, 16], F32)
        xadd = [sb("xadd0", [128, D], F32), sb("xadd1", [128, D], F32)]

        negmh = ts_["negmh"]
        kv_dep = [nb_ev] + st_ops[-3:]
        kt_ld, vp_ld = {}, {}
        kt_reader = [nb_ev, nb_ev]
        vp_reader = [nb_ev, nb_ev]

        def load_pair(pair):
            b = pair % 2
            l1 = p.g(lambda g_, pair=pair, b=b: g_.dma_start(
                kt_sb[b][:, :], kt_d[pair * 128:(pair + 1) * 128, :]),
                kv_dep + [kt_reader[b]])
            l2 = p.g(lambda g_, pair=pair, b=b: g_.dma_start(
                vp_sb[b][:, :, :],
                v_d[:, 2 * pair:2 * pair + 2, :].rearrange("(jt p) h c -> p jt (h c)", p=128)),
                kv_dep + [vp_reader[b]])
            kt_ld[pair] = (l1, l2)

        load_pair(0)
        load_pair(1)

        s_reader = [nb_ev, nb_ev]
        psb_reader = [None, None]
        y_reader = [[None, None], [None, None]]
        ltr_reader = nb_ev
        lrow_reader = None
        yT_reader = None
        last_pair_mm = {}
        for pair in range(8):
            b = pair % 2
            dW = list(kt_ld[pair])
            for qh in range(2):
                pv_last = [[None, None], [None, None]]
                for jt in range(64):
                    for hh in range(2):
                        for qb in range(2):
                            par = (jt * 4 + hh * 2 + qb) % 2
                            q0 = qh * 1024 + qb * 512
                            smm = p.t(lambda t_, pair=pair, b=b, jt=jt, hh=hh, q0=q0, par=par: t_.matmul(
                                s_ps[par][:, :],
                                kt_sb[b][hh * 64:(hh + 1) * 64, jt * 128:(jt + 1) * 128],
                                QT[hh * 64:(hh + 1) * 64, pair, q0:q0 + 512],
                                start=True, stop=True),
                                (dW + [s_reader[par]]) + ([q_ops[-1]] if (pair == 0 and jt == 0) else []))
                            h_ = 2 * pair + hh
                            expo = p.s(lambda s, h_=h_, par=par: s.activation(
                                pT_sb[par][:], s_ps[par][:, :], AF.Exp, scale=0.125,
                                bias=negmh[:, h_:h_ + 1]), [smm, nb_ev, psb_reader[par]])
                            pv = p.t(lambda t_, b=b, jt=jt, hh=hh, qb=qb, par=par: t_.matmul(
                                y_ps[hh][qb][0:65, :], vp_sb[b][:, jt, hh * 65:(hh + 1) * 65],
                                pT_sb[par][:], start=(jt == 0), stop=(jt == 63)),
                                [expo, y_reader[hh][qb]] if jt == 0 else [expo])
                            psb_reader[par] = pv
                            s_reader[par] = expo
                            pv_last[hh][qb] = pv
                for hh in range(2):
                    for qb in range(2):
                        q0 = qh * 1024 + qb * 512
                        h = 2 * pair + hh
                        ye = p.v(lambda v, pair=pair, hh=hh, qb=qb, q0=q0: v.tensor_copy(
                            yT_all[hh * 64:(hh + 1) * 64, pair, q0:q0 + 512],
                            y_ps[hh][qb][0:64, :]), [pv_last[hh][qb], yT_reader])
                        le = p.v(lambda v, hh=hh, qb=qb: v.tensor_copy(
                            lrow[0:1, :], y_ps[hh][qb][64:65, :]), [pv_last[hh][qb], lrow_reader])
                        trs = []
                        for i in range(4):
                            trs.append(p.t(lambda t_, i=i: t_.transpose(
                                ltr_ps[0:128, i:i + 1], lrow[0:1, i * 128:(i + 1) * 128],
                                identf[0:1, 0:1]), [le] + ([ltr_reader] if i == 0 else [])))
                        t0 = q0 // 128
                        lte = p.v(lambda v, t0=t0, h=h: v.tensor_copy(
                            ltok[:, t0:t0 + 4, h:h + 1].rearrange("p a b -> p (a b)"),
                            ltr_ps[0:128, 0:4]), [trs[-1]])
                        ltr_reader = lte
                        lrow_reader = trs[-1]
                        y_reader[hh][qb] = le
                        s_reader[1] = lte
            last_pair_mm[pair] = pv_last[1][1]
            if pair + 2 < 8:
                kt_reader[b] = pv_last[1][1]
                vp_reader[b] = pv_last[1][1]
                load_pair(pair + 2)
        yT_reader = None

        rc = p.v(lambda v: v.reciprocal(linv_all[:, :, :].rearrange("p a b -> p (a b)"),
                                        ltok[:, :, :].rearrange("p a b -> p (a b)")), [ltr_reader])

        ytr_ps = ts_["tr_ps"]
        ytr_reader = None
        xadd_reader = [None, None]
        last_store = None
        for t in range(16):
            ldx = p.sy(lambda s, t=t: s.dma_start(xadd[t % 2][:], xq_tiled[t]), [xadd_reader[t % 2]])
            adds = []
            for pc in range(8):
                tr = p.t(lambda t_, t=t, pc=pc: t_.transpose(
                    ytr_ps[:, pc, :], yT_all[:, pc, t * 128:(t + 1) * 128], ts_["ident"][:]),
                    [rc if (t == 0 and pc == 0) else None, ytr_reader])
                for hh in range(2):
                    h = 2 * pc + hh
                    ad = p.v(lambda v, t=t, pc=pc, hh=hh, h=h: v.scalar_tensor_tensor(
                        xadd[t % 2][:, h * 64:(h + 1) * 64], ytr_ps[:, pc, hh * 64:(hh + 1) * 64],
                        linv_all[:, t, h:h + 1], xadd[t % 2][:, h * 64:(h + 1) * 64],
                        ALU.mult, ALU.add), [tr, ldx, rc])
                    adds.append(ad)
                ytr_reader = adds[-1]
            stv = p.sy(lambda s, t=t: s.dma_start(
                xout.rearrange("(t p) d -> t p d", p=128)[t], xadd[t % 2][:]), adds[-16:])
            xadd_reader[t % 2] = stv
            last_store = stv
        p.sy(lambda s: s.nop(), [last_store], dma=False)
        p.finalize_and_emit(block)
    es.close()
    return nc


def run_attn_l1(xfull, Wqkv):
    nc = _get_nc('attn1', build_attn_l1)
    in_maps = []
    for c in range(NCORES):
        b, q = c // 4, c % 4
        in_maps.append(dict(xq=np.ascontiguousarray(xfull[b, q * 2048:(q + 1) * 2048]),
                            xb=np.ascontiguousarray(xfull[b]), wqkv=Wqkv,
                            identin=_IDENT, identfin=_IDENTF, indmin=_INDM, onesin=_ONES1))
    res = run_bass_kernel_spmd(nc, in_maps, core_ids=list(range(NCORES)))
    out = np.stack([res.results[c]["xout"] for c in range(NCORES)])
    return out.reshape(2, 8192, D)


_IDENT = np.eye(128, dtype=BF)
_IDENTF = np.eye(128, dtype=np.float32)


_NC_CACHE = {}


def _get_nc(key, builder):
    if key not in _NC_CACHE:
        _NC_CACHE[key] = builder()
    return _NC_CACHE[key]


def kernel(**inputs):
    return kernel_fused(**inputs)


def kernel_multilaunch(L0_attn_x, L0_attn_Wq, L0_attn_Wk, L0_attn_Wv, L0_attn_Wo,
           L0_ffn_router_w, L0_ffn_up_w, L0_ffn_gate_w, L0_ffn_down_w,
           L0_ffn_shared_up_w, L0_ffn_shared_gate_w, L0_ffn_shared_down_w,
           L1_attn_W_qkv, L1_ffn_Wg, L1_ffn_W1, L1_ffn_W2, L1_ffn_W3):
    x = np.asarray(L0_attn_x)
    w0 = dict(rw=np.asarray(L0_ffn_router_w), up7=np.asarray(L0_ffn_up_w),
              gate7=np.asarray(L0_ffn_gate_w), down7=np.asarray(L0_ffn_down_w),
              sup=np.asarray(L0_ffn_shared_up_w), sgate=np.asarray(L0_ffn_shared_gate_w),
              sdown=np.asarray(L0_ffn_shared_down_w))
    w1 = dict(rw=np.asarray(L1_ffn_Wg), w1=np.asarray(L1_ffn_W1),
              w2=np.asarray(L1_ffn_W2), w3=np.asarray(L1_ffn_W3))
    x = run_ffn(x, 0, w0)
    x = run_attn_l0(x, np.asarray(L0_attn_Wq), np.asarray(L0_attn_Wk),
                    np.asarray(L0_attn_Wv), np.asarray(L0_attn_Wo))
    x = run_ffn(x, 0, w0)
    x = run_ffn(x, 1, w1)
    x = run_attn_l1(x, np.asarray(L1_attn_W_qkv))
    x = run_ffn(x, 1, w1)
    return x


def _perm_const():
    pm = np.zeros((128, 128), dtype=np.float32)
    for blk in range(2):
        o = blk * 64
        for pr in range(32):
            pm[o + 2 * pr + 1, o + 2 * pr] = -1.0
            pm[o + 2 * pr, o + 2 * pr + 1] = 1.0
    return pm.astype(BF)


_PERM = _perm_const()


def _rope_tables(pos):
    invf = (1.0 / 10000.0 ** (np.arange(0, 64, 2, dtype=np.float32) / 64.0)).astype(np.float32)
    j = np.arange(64)
    freqs = pos[None, :].astype(np.float32) * invf[j % 32][:, None]
    cos = np.cos(freqs)
    sin = np.sin(freqs)
    cosT = np.concatenate([cos, cos], 0).astype(BF)
    sinT = np.concatenate([sin, sin], 0).astype(BF)
    return cosT, sinT


def _l0_masks(edge):
    qi = np.arange(128)[:, None]
    m = np.zeros((128, 3, 384), dtype=np.float32)
    for t in range(3):
        ji = np.arange(384)[None, :]
        allow = (ji - qi >= 1) & (ji - qi <= 256)
        if edge:
            allow = allow & (t * 128 + ji >= 256)
        m[:, t, :] = np.where(allow, 0.0, -1e30)
    return m


def _indm_const():
    im = np.zeros((128, 8, 16), dtype=np.float32)
    for p_ in range(128):
        for qc in range(8):
            im[p_, qc, 2 * qc + (1 if p_ >= 64 else 0)] = 1.0
    return im.astype(BF)


_INDM = _indm_const()
_ONES1 = np.ones((1, 128), dtype=np.float32)


def run_attn_l0(xfull, Wq, Wk, Wv, Wo):
    nc = _get_nc('attn0', build_attn_l0)
    in_maps = []
    for c in range(NCORES):
        b, q = c // 4, c % 4
        own = xfull[b, q * 2048:(q + 1) * 2048]
        if q == 0:
            halo = np.zeros((256, D), dtype=np.float32)
        else:
            halo = xfull[b, q * 2048 - 256:q * 2048]
        xin = np.ascontiguousarray(np.concatenate([halo, own], 0))
        pos = np.maximum(q * 2048 - 256 + np.arange(2304), 0)
        cosT, sinT = _rope_tables(pos)
        in_maps.append(dict(xin=xin, wq=Wq, wk=Wk, wv=Wv, wo=Wo, identin=_IDENT,
                            permin=_PERM, cosin=cosT, sinin=sinT,
                            maskin=_l0_masks(q == 0), indmin=_INDM, onesin=_ONES1))
    res = run_bass_kernel_spmd(nc, in_maps, core_ids=list(range(NCORES)))
    out = np.stack([res.results[c]["xout"] for c in range(NCORES)])
    return out.reshape(2, 8192, D)


def run_ffn(xfull, layer, weights):
    T = 2048
    nc = _get_nc(('ffn', layer), lambda: build_ffn(T, layer))
    xf = np.ascontiguousarray(xfull.reshape(NCORES, T, D))
    in_maps = []
    for c in range(NCORES):
        m = dict(weights)
        m["xin"] = xf[c]
        m["identin"] = _IDENT
        in_maps.append(m)
    res = run_bass_kernel_spmd(nc, in_maps, core_ids=list(range(NCORES)))
    out = np.stack([res.results[c]["xout"] for c in range(NCORES)])
    return out.reshape(2, 8192, D)



_FFN_PHASE_N = [0]


def _alloc_ffn_ts(nc, scope, NT, T):
    _FFN_PHASE_N[0] += 1
    tag = "f%d_" % _FFN_PHASE_N[0]
    sb = lambda name, shape, dt: scope.enter_context(nc.sbuf_tensor(tag + name, shape, dt))
    ps = lambda name, shape, dt: scope.enter_context(nc.psum_tensor(tag + name, shape, dt))
    ts_ = {}
    ts_["xlnT"] = sb("f_xlnT", [128, 8, T], BF16)
    ts_["stats"] = sb("f_stats", [128, NT, 8], F32)
    ts_["sq"] = sb("f_sq", [128, D], F32)
    ts_["xln_scr"] = [sb("f_xl0", [128, D], BF16), sb("f_xl1", [128, D], BF16)]
    ts_["coefs"] = sb("f_coefs", [128, NT, 8], F32)
    ts_["exps_scr"] = sb("f_exps", [128, 8], F32)
    ts_["eq1"] = sb("f_eq1", [128, 8], F32)
    ts_["eq2"] = sb("f_eq2", [128, 8], F32)
    ts_["masked"] = sb("f_msk", [128, 8], F32)
    ts_["rwb"] = sb("f_rwb", [128, 8, 8], BF16)
    ts_["w_up"] = [sb("f_wu0", [128, 8, 512], BF16), sb("f_wu1", [128, 8, 512], BF16)]
    ts_["w_gate"] = [sb("f_wg0", [128, 8, 512], BF16), sb("f_wg1", [128, 8, 512], BF16)]
    ts_["w_down"] = [sb("f_wd0", [128, 4, D], BF16), sb("f_wd1", [128, 4, D], BF16)]
    ts_["hidT"] = sb("f_hidT", [128, 4, T], BF16)
    ts_["silu_sb"] = [sb("f_si0", [128, 512], F32), sb("f_si1", [128, 512], F32)]
    ts_["tr_ps"] = ps("f_trps", [128, 8, 128], BF16)
    ts_["logits_ps"] = ps("f_lgps", [128, 512], F32)
    ts_["g_ps"] = [ps("f_gps0", [128, 512], F32), ps("f_gps1", [128, 512], F32)]
    ts_["u_ps"] = [ps("f_ups0", [128, 512], F32), ps("f_ups1", [128, 512], F32)]
    ts_["d_ps"] = [ps("f_dps0", [128, 512], F32), ps("f_dps1", [128, 512], F32)]
    return ts_


def _emit_ffn_phase(p, nc, scope_mk, x_view, NT, T, layer, wsrc_mk, rw_dram,
                    ident, phase_dep):
    with scope_mk() as scope:
        ts_ = _alloc_ffn_ts(nc, scope, NT, T)
        ts_["ident"] = ident
        ld_rw = p.g(lambda g: g.dma_start(
            ts_["rwb"][:, :, :], rw_dram.rearrange("(c p) e -> p c e", p=128)), phase_dep)
        evac_ops, _ = emit_ln(p, ts_, NT, x_view, ts_["xlnT"], phase_dep)
        if layer == 0:
            coef_ops = emit_router_l0(p, ts_, NT, ts_["xlnT"], ts_["rwb"], evac_ops, ld_rw)
            n_routed, shared_coef = 7, 0.5
        else:
            coef_ops = emit_router_l1(p, ts_, NT, ts_["xlnT"], ts_["rwb"], evac_ops, ld_rw)
            n_routed, shared_coef = 8, None
        wsrc = wsrc_mk()
        last_x = emit_moe(p, ts_, NT, T, ts_["xlnT"], x_view, ts_["coefs"], wsrc,
                          ln_done=[evac_ops[-1], ld_rw] + phase_dep,
                          router_done=[coef_ops[-1]],
                          H=4096 if layer == 0 else 2048,
                          n_routed=n_routed, shared_coef=shared_coef,
                          silu_on_gate=True)
    return last_x


def _emit_attn0_phase(p, nc, es_mk, x_sb, wq, wk, wv, wo, consts_dram, ident,
                      identf, indm, ones1, phase_dep):
    permin, cosin, sinin, maskin = consts_dram
    TQ, TK, NTK = 2048, 2304, 18
    esC = ExitStack()
    sb = lambda name, shape, dt: esC.enter_context(nc.sbuf_tensor(name, shape, dt))
    ps = lambda name, shape, dt: esC.enter_context(nc.psum_tensor(name, shape, dt))
    ts_ = {}
    ts_["ident"] = ident
    xlnT = sb("c_xlnT", [128, 8, TK], BF16)
    ts_["stats"] = sb("c_stats", [128, NTK, 8], F32)
    ts_["sq"] = sb("c_sq", [128, D], F32)
    ts_["xln_scr"] = [sb("c_xl0", [128, D], BF16), sb("c_xl1", [128, D], BF16)]
    perm = sb("c_perm", [128, 128], BF16)
    cosT = sb("c_cos", [128, TK], BF16)
    sinT = sb("c_sin", [128, TK], BF16)
    masks = sb("c_masks", [128, 3, 384], F32)
    y_sb = sb("c_y", [128, 16, D], BF16)
    ts_["negmh"] = sb("c_negmh", [128, 16], F32)
    ts_["nstat"] = sb("c_nstat", [32, 32], F32)
    ts_["ntr"] = sb("c_ntr", [32, 32], F32)
    lbuf = sb("c_lbuf", [128, 16], F32)
    linv = sb("c_linv", [128, 2, 16], F32)
    esC1 = ExitStack()
    sb1 = lambda name, shape, dt: esC1.enter_context(nc.sbuf_tensor(name, shape, dt))
    wstg = sb1("c_wstg", [128, D], BF16)
    WqTp = sb1("c_wqtp", [128, 8, 128], BF16)
    WkTp = sb1("c_wktp", [128, 8, 128], BF16)
    WvTp = sb1("c_wvtp", [128, 8, 128], BF16)
    QTp = sb1("c_qtp", [128, TQ], BF16)
    KTp = sb1("c_ktp", [128, TK], BF16)
    Vp = sb1("c_vp", [128, NTK, 128], BF16)
    ts_["qraw"] = [sb1("c_qr0", [128, 512], BF16), sb1("c_qr1", [128, 512], BF16)]
    ts_["rope_scr"] = [sb1("c_rs0", [128, 512], F32), sb1("c_rs1", [128, 512], F32)]
    ts_["sqscr"] = [sb1("c_sc0", [128, 512], BF16), sb1("c_sc1", [128, 512], BF16)]
    P_sb = [sb1("c_p0", [128, 384], BF16), sb1("c_p1", [128, 384], BF16)]
    PT_sb = [sb1("c_pt0", [128, 3, 128], BF16), sb1("c_pt1", [128, 3, 128], BF16)]

    ts_["tr_ps"] = ps("c_trps", [128, 8, 128], BF16)
    ts_["pr_ps"] = [ps("c_pr0", [128, 512], F32), ps("c_pr1", [128, 512], F32)]
    ts_["rt_ps"] = ps("c_rtps", [128, 512], F32)
    s_ps = [ps("c_sps0", [128, 512], F32), ps("c_sps1", [128, 512], F32)]
    y_ps = [ps("c_yps0", [128, 512], F32), ps("c_yps1", [128, 512], F32)]
    ts_["nb_ps"] = s_ps[0]

    ld_pm = p.sy(lambda s: s.dma_start(perm[:], permin[:, :]), phase_dep)
    ld_cs = p.sy(lambda s: s.dma_start(cosT[:], cosin[:, :]), phase_dep)
    ld_sn = p.sy(lambda s: s.dma_start(sinT[:], sinin[:, :]), phase_dep)
    ld_mk = p.sy(lambda s: s.dma_start(masks[:, :, :], maskin[:, :, :]), phase_dep)
    cdep = [ld_pm, ld_cs, ld_sn]

    evac_ops, _ = emit_ln(p, ts_, NTK, x_sb, xlnT, phase_dep)
    xln_dep = evac_ops[-1]

    state = {"tr_reader": xln_dep, "s_reader": [None, None], "y_reader": [None, None],
             "psb_reader": [None, None], "pt_reader": None, "ysb_reader": None,
             "wstg_reader": None}
    ts_["_nb_ps_reader"] = None

    def wtr_pair(w_dram, WTp, pair, dep0):
        ld = p.g(lambda g: g.dma_start(wstg[:, :], w_dram[pair * 128:(pair + 1) * 128, :]),
                 [state["wstg_reader"]] + dep0)
        trs = []
        for dc in range(8):
            trs.append(p.t(lambda t_, dc=dc: t_.transpose(
                ts_["tr_ps"][:, dc, :], wstg[:, dc * 128:(dc + 1) * 128], ident[:]),
                [ld] + ([state["tr_reader"]] if dc == 0 else [])))
        state["wstg_reader"] = trs[-1]
        ev = p.v(lambda v: v.tensor_copy(WTp[:, :, :], ts_["tr_ps"][:, :, :]), [trs[-1]])
        state["tr_reader"] = ev
        return ev

    def proj_pair_fm(WTp, OUT, Tn, off, wt_dep, rope_on, bank_war):
        slices = [(s, min(512, Tn - s)) for s in range(0, Tn, 512)]
        pr_ps = ts_["pr_ps"]; rt_ps = ts_["rt_ps"]
        qraw = ts_["qraw"]; rsc = ts_["rope_scr"]
        br = [bank_war, bank_war]
        qrr = [None, None]; rtr = [None]; rscr = [None, None]
        k = 0
        out_last = None
        for (s0, sl) in slices:
            pb = k % 2
            k += 1
            mm = None
            for dc in range(8):
                mm = p.t(lambda t_, s0=s0, sl=sl, dc=dc, pb=pb: t_.matmul(
                    pr_ps[pb][:, 0:sl], WTp[:, dc, :], xlnT[:, dc, off + s0:off + s0 + sl],
                    start=(dc == 0), stop=(dc == 7)),
                    ([wt_dep, xln_dep, br[pb]] if dc == 0 else []))
            qr = p.v(lambda v, pb=pb, sl=sl: v.tensor_copy(qraw[pb][:, 0:sl], pr_ps[pb][:, 0:sl]),
                     [mm, qrr[pb]])
            br[pb] = qr
            rt = p.t(lambda t_, pb=pb, sl=sl: t_.matmul(
                rt_ps[:, 0:sl], perm[:], qraw[pb][:, 0:sl], start=True, stop=True),
                [qr, rtr[0]] + (cdep if k == 1 else []))
            t1 = p.v(lambda v, pb=pb, s0=s0, sl=sl: v.tensor_tensor(
                rsc[pb][:, 0:sl], qraw[pb][:, 0:sl], cosT[:, off + s0:off + s0 + sl], ALU.mult),
                [qr, rscr[pb]])
            qrr[pb] = t1
            t2 = p.v(lambda v, pb=pb, s0=s0, sl=sl: v.tensor_tensor(
                rt_ps[:, 0:sl], rt_ps[:, 0:sl], sinT[:, off + s0:off + s0 + sl], ALU.mult), [rt])
            ev = p.v(lambda v, s0=s0, sl=sl, pb=pb: v.tensor_tensor(
                OUT[:, s0:s0 + sl], rsc[pb][:, 0:sl], rt_ps[:, 0:sl], ALU.add), [t2, t1])
            rtr[0] = ev
            rscr[pb] = ev
            out_last = ev
        return out_last

    def normsq_pair(XT, Tn, pair, col, dep):
        slices = [(s, min(512, Tn - s)) for s in range(0, Tn, 512)]
        sqscr = ts_["sqscr"]; nb_ps = ts_["nb_ps"]; nstat = ts_["nstat"]
        run = None
        for i, (s0, sl) in enumerate(slices):
            pb = i % 2
            sq = p.s(lambda s, s0=s0, sl=sl, pb=pb, XT=XT: s.activation(
                sqscr[pb][:, 0:sl], XT[:, s0:s0 + sl], AF.Square), [dep, ts_.get("_sq_r%d" % pb)])
            mm = p.t(lambda t_, pb=pb, sl=sl, pair=pair: t_.matmul(
                nb_ps[0:16, 0:sl], indm[:, pair, :], sqscr[pb][:, 0:sl],
                start=True, stop=True), [sq, ts_["_nb_ps_reader"]])
            ts_["_sq_r%d" % pb] = mm
            rm = p.v(lambda v, col=col, sl=sl: v.reduce_max(
                nstat[0:16, col + 2:col + 3], nb_ps[0:16, 0:sl], axis=AX.X), [mm])
            ts_["_nb_ps_reader"] = rm
            if run is None:
                run = p.v(lambda v, col=col: v.tensor_copy(
                    nstat[0:16, col:col + 1], nstat[0:16, col + 2:col + 3]), [rm])
            else:
                run = p.v(lambda v, col=col: v.tensor_tensor(
                    nstat[0:16, col:col + 1], nstat[0:16, col:col + 1],
                    nstat[0:16, col + 2:col + 3], ALU.max), [rm, run])
        return run

    ysb_last = {}
    for pair in range(8):
        wq_ev = wtr_pair(wq, WqTp, pair, phase_dep)
        wk_ev = wtr_pair(wk, WkTp, pair, [])
        wv_ev = wtr_pair(wv, WvTp, pair, [])
        q_last = proj_pair_fm(WqTp, QTp, TQ, 256, wq_ev, True,
                              state.get("qtp_reader"))
        k_last = proj_pair_fm(WkTp, KTp, TK, 0, wk_ev, True, q_last)
        v_last = None
        vbr = [k_last, k_last]
        for tt in range(NTK):
            pb = tt % 2
            mm = None
            for dc in range(8):
                mm = p.t(lambda t_, tt=tt, dc=dc, pb=pb: t_.matmul(
                    ts_["pr_ps"][pb][:, 0:128], xlnT[:, dc, tt * 128:(tt + 1) * 128],
                    WvTp[:, dc, :], start=(dc == 0), stop=(dc == 7)),
                    ([wv_ev, xln_dep, vbr[pb]] if dc == 0 else []))
            ev = p.v(lambda v, tt=tt, pb=pb: v.tensor_copy(Vp[:, tt, :], ts_["pr_ps"][pb][:, 0:128]),
                     [mm, state.get("vp_reader_%d" % pb)])
            vbr[pb] = ev
            v_last = ev
        rq = normsq_pair(QTp, TQ, pair, 0, q_last)
        rk = normsq_pair(KTp, TK, pair, 1, k_last)
        nb_ev = emit_normbound_finish(p, ts_, ones1, rq, rk, 0.125)
        sr = state["s_reader"]
        if sr[0] is None:
            sr[0] = nb_ev
            sr[1] = nb_ev
        last_se = None
        first_t_dep = [nb_ev, ld_mk, v_last]
        items = [(t, hh) for t in range(16) for hh in range(2)]

        def stage_a(i):
            t, hh = items[i]
            mi = min(t, 2)
            h = 2 * pair + hh
            par = i % 2
            smm = p.t(lambda t_, t=t, hh=hh, par=par: t_.matmul(
                s_ps[par][:, 0:384], QTp[hh * 64:(hh + 1) * 64, t * 128:(t + 1) * 128],
                KTp[hh * 64:(hh + 1) * 64, t * 128:t * 128 + 384],
                start=True, stop=True),
                [sr[par]] + (first_t_dep if i == 0 else []))
            madd = p.v(lambda v, par=par, mi=mi: v.tensor_tensor(
                s_ps[par][:, 0:384], s_ps[par][:, 0:384], masks[:, mi, :], ALU.add),
                [smm, ld_mk])
            expo = p.s(lambda s, h=h, par=par: s.activation(
                P_sb[par][:], s_ps[par][:, 0:384], AF.Exp, scale=0.125,
                bias=ts_["negmh"][:, h:h + 1], accum_out=lbuf[:, h:h + 1]),
                [madd, nb_ev, state["psb_reader"][par], state.get("li_g%d" % par)])
            sr[par] = expo
            li = p.v(lambda v, h=h, t=t: v.reciprocal(linv[:, t % 2, h:h + 1], lbuf[:, h:h + 1]), [expo])
            state["li_g%d" % par] = li
            return (expo, li)

        def stage_b(i, expo):
            par = i % 2
            trs = []
            for jc in range(3):
                trs.append(p.t(lambda t_, par=par, jc=jc: t_.transpose(
                    ts_["tr_ps"][:, jc, :], P_sb[par][:, jc * 128:(jc + 1) * 128], ident[:]),
                    [expo] + ([state["tr_reader"]] if jc == 0 else [])))
            pte = p.v(lambda v, par=par: v.tensor_copy(PT_sb[par][:, :, :], ts_["tr_ps"][:, 0:3, :]),
                      [trs[-1]])
            state["psb_reader"][par] = trs[-1]
            state["tr_reader"] = pte
            return pte

        def stage_c(i, pte, li):
            t, hh = items[i]
            h = 2 * pair + hh
            par = i % 2
            pvs = None
            for jc in range(3):
                pvs = p.t(lambda t_, t=t, hh=hh, par=par, jc=jc: t_.matmul(
                    y_ps[par][:, 0:64], PT_sb[par][:, jc, :],
                    Vp[:, t + jc, hh * 64:(hh + 1) * 64],
                    start=(jc == 0), stop=(jc == 2)),
                    [pte, state["y_reader"][par]] if jc == 0 else [])
            se = p.v(lambda v, t=t, h=h, par=par: v.tensor_scalar(
                y_sb[:, t, h * 64:(h + 1) * 64], y_ps[par][:, 0:64],
                linv[:, t % 2, h:h + 1], None, ALU.mult),
                [pvs, li])
            state["y_reader"][par] = se
            return se

        hist = {}
        for i in range(len(items)):
            hist[i] = {}
            hist[i]["a"] = stage_a(i)
            if i - 1 in hist:
                hist[i - 1]["b"] = stage_b(i - 1, hist[i - 1]["a"][0])
            if i - 2 in hist:
                last_se = stage_c(i - 2, hist[i - 2]["b"], hist[i - 2]["a"][1])
        for j in (len(items) - 2, len(items) - 1):
            if "b" not in hist[j]:
                hist[j]["b"] = stage_b(j, hist[j]["a"][0])
            last_se = stage_c(j, hist[j]["b"], hist[j]["a"][1])
        state["qtp_reader"] = last_se
        state["vp_reader_0"] = last_se
        state["vp_reader_1"] = last_se
    y_done = last_se
    esC1.close()
    wstage4 = sb("c_wst4", [128, 4, D], BF16)
    WoT = sb("c_woT", [128, 8, D], BF16)
    yT_sb = sb("c_yT", [128, 8, 128], BF16)
    ts_["wstage"] = wstage4
    _, woT = emit_w_transpose(p, ts_, wo[:, :], WoT, [y_done], state["tr_reader"])
    wo_ps = ts_["pr_ps"]
    wo_reader = [y_done, y_done]
    ytr_reader = woT
    yT_reader = None
    last_ad = None
    for t in range(16):
        ytrs = []
        for fc in range(8):
            ytrs.append(p.t(lambda t_, t=t, fc=fc: t_.transpose(
                ts_["tr_ps"][:, fc, :], y_sb[:, t, fc * 128:(fc + 1) * 128], ident[:]),
                [y_done] + ([ytr_reader] if fc == 0 else [])))
        yte = p.v(lambda v: v.tensor_copy(yT_sb[:, :, :], ts_["tr_ps"][:, :, :]), [ytrs[-1], yT_reader])
        ytr_reader = yte
        adds = []
        wmm = None
        for dcol in range(2):
            for fc in range(8):
                wmm = p.t(lambda t_, t=t, dcol=dcol, fc=fc: t_.matmul(
                    wo_ps[dcol][:, :], yT_sb[:, fc, :], WoT[:, fc, dcol * 512:(dcol + 1) * 512],
                    start=(fc == 0), stop=(fc == 7)),
                    [yte, wo_reader[dcol]] if fc == 0 else [])
            ad = p.v(lambda v, t=t, dcol=dcol: v.tensor_tensor(
                x_sb[:, t + 2, dcol * 512:(dcol + 1) * 512], wo_ps[dcol][:, :],
                x_sb[:, t + 2, dcol * 512:(dcol + 1) * 512], ALU.add), [wmm])
            wo_reader[dcol] = ad
            adds.append(ad)
        yT_reader = wmm
        last_ad = adds[-1]
    esC.close()
    return last_ad


def _emit_attn1_phase(p, nc, x_own, wqkv, ident, identf, indm, ones1,
                      kv_in, kv_g, KTN, phase_dep):
    TQ = 2048
    esF = ExitStack()
    sb = lambda name, shape, dt: esF.enter_context(nc.sbuf_tensor(name, shape, dt))
    ps = lambda name, shape, dt: esF.enter_context(nc.psum_tensor(name, shape, dt))
    QT = sb("g_QT", [128, 8, TQ], BF16)
    ts_ = {"ident": ident}
    ts_["negmh"] = sb("g_negmh", [128, 16], F32)
    ts_["nstat"] = sb("g_nstat", [32, 32], F32)
    ts_["ntr"] = sb("g_ntr", [32, 32], F32)
    s2_ps = [ps("g_s2ps0", [128, 1024], F32), ps("g_s2ps1", [128, 1024], F32)]
    y_ps = [ps("g_yps0", [128, 512], F32), ps("g_yps1", [128, 512], F32)]
    tr_ps = ps("g_trps", [128, 8, 128], BF16)
    ts_["tr_ps"] = tr_ps
    ts_["pr_ps"] = [y_ps[0], y_ps[1]]
    ts_["rt_ps"] = s2_ps[0]
    ts_["nb_ps"] = s2_ps[0]
    ltr_ps = s2_ps[1]

    esF1 = ExitStack()
    sb1 = lambda name, shape, dt: esF1.enter_context(nc.sbuf_tensor(name, shape, dt))
    xlnT = sb1("g_xlnT", [128, 8, TQ], BF16)
    ts_["stats"] = sb1("g_stats", [128, 16, 8], F32)
    ts_["sq"] = sb1("g_sq", [128, D], F32)
    ts_["xln_scr"] = [sb1("g_xl0", [128, D], BF16), sb1("g_xl1", [128, D], BF16)]
    ts_["wstage"] = sb1("g_wstg", [128, 4, D], BF16)
    WTa = sb1("g_WTa", [128, 8, D], BF16)
    KTc = sb1("g_KTc", [128, TQ], BF16)
    ts_["sqscr"] = [sb1("g_sc0", [128, 512], BF16), sb1("g_sc1", [128, 512], BF16)]
    v_stage = [sb1("g_vst0", [128, 16, 65], BF16), sb1("g_vst1", [128, 16, 65], BF16)]

    ms0 = p.v(lambda v: v.memset(v_stage[0][:, :, 64:65], 1.0), phase_dep)
    ms1 = p.v(lambda v: v.memset(v_stage[1][:, :, 64:65], 1.0), phase_dep)

    evq = emit_ln(p, ts_, 16, x_own, xlnT, phase_dep)[0]
    xln_dep = evq[-1]
    ts_["_nb_sq_reader"] = [None, None]
    ts_["_nb_ps_reader"] = None

    _, wkT = emit_w_transpose(p, ts_, wqkv[D:2 * D, :], WTa, phase_dep, xln_dep)
    kt_inT = kv_in[0:1, 0:KTN].rearrange("a (qc p j) -> (a qc) p j", p=128, j=TQ)
    rk = None
    kbr = [phase_dep[0], phase_dep[0]]
    kst_ops = []
    sqscr = ts_["sqscr"]; nb_ps = ts_["nb_ps"]; nstat = ts_["nstat"]
    ktc_reader = None
    for qc in range(8):
        evs = []
        for i, s0 in enumerate(range(0, TQ, 512)):
            pb = (qc * 4 + i) % 2
            mm = None
            for dc in range(8):
                mm = p.t(lambda t_, qc=qc, s0=s0, dc=dc, pb=pb: t_.matmul(
                    ts_["pr_ps"][pb][:, :], WTa[:, dc, qc * 128:(qc + 1) * 128],
                    xlnT[:, dc, s0:s0 + 512], start=(dc == 0), stop=(dc == 7)),
                    ([wkT, xln_dep, kbr[pb]] if dc == 0 else []))
            ev = p.v(lambda v, s0=s0, pb=pb: v.tensor_copy(KTc[:, s0:s0 + 512],
                                                           ts_["pr_ps"][pb][:, :]),
                     [mm, ktc_reader])
            kbr[pb] = ev
            evs.append(ev)
        stk = p.sy(lambda s, qc=qc: s.dma_start(kt_inT[qc], KTc[:, :]), [evs[-1]])
        kst_ops.append(stk)
        for i, s0 in enumerate(range(0, TQ, 512)):
            pb = i % 2
            sq = p.s(lambda s, s0=s0, pb=pb: s.activation(
                sqscr[pb][:], KTc[:, s0:s0 + 512], AF.Square),
                [evs[-1], ts_["_nb_sq_reader"][pb]])
            mm2 = p.t(lambda t_, qc=qc, pb=pb: t_.matmul(
                nb_ps[0:16, 0:512], indm[:, qc, :], sqscr[pb][:],
                start=True, stop=True), [sq, ts_["_nb_ps_reader"]])
            ts_["_nb_sq_reader"][pb] = mm2
            rm = p.v(lambda v: v.reduce_max(nstat[0:16, 3:4], nb_ps[0:16, 0:512], axis=AX.X), [mm2])
            ts_["_nb_ps_reader"] = rm
            if rk is None:
                rk = p.v(lambda v: v.tensor_copy(nstat[0:16, 1:2], nstat[0:16, 3:4]), [rm])
            else:
                rk = p.v(lambda v: v.tensor_tensor(nstat[0:16, 1:2], nstat[0:16, 1:2],
                                                   nstat[0:16, 3:4], ALU.max), [rm, rk])
        ktc_reader = ts_["_nb_sq_reader"][1]

    _, wvT = emit_w_transpose(p, ts_, wqkv[2 * D:3 * D, :], WTa, [], kbr[0])
    v_inT = kv_in[0:1, KTN:].rearrange("a (tt p c) -> (a tt) p c", p=128, c=1040)
    vst_reader = [ms0, ms1]
    vbr = [kbr[0], kbr[1]]
    vst_ops = []
    k = 0
    for tt in range(16):
        vb = tt % 2
        evs = []
        for fcol in range(2):
            pb = k % 2
            k += 1
            mm = None
            for dc in range(8):
                mm = p.t(lambda t_, tt=tt, fcol=fcol, dc=dc, pb=pb: t_.matmul(
                    ts_["pr_ps"][pb][:, :], xlnT[:, dc, tt * 128:(tt + 1) * 128],
                    WTa[:, dc, fcol * 512:(fcol + 1) * 512],
                    start=(dc == 0), stop=(dc == 7)),
                    ([wvT, xln_dep, vbr[pb]] if dc == 0 else []))
            ev = p.v(lambda v, pb=pb, vb=vb, fcol=fcol: v.tensor_copy(
                v_stage[vb][:, fcol * 8:(fcol + 1) * 8, 0:64],
                ts_["pr_ps"][pb][:, :].rearrange("p (h c) -> p h c", c=64)),
                [mm, vst_reader[vb]])
            vbr[pb] = ev
            evs.append(ev)
        stv = p.sy(lambda s, tt=tt, vb=vb: s.dma_start(
            v_inT[tt], v_stage[vb].rearrange("p h c -> p (h c)")), evs)
        vst_reader[vb] = stv
        vst_ops.append(stv)

    _, wqT = emit_w_transpose(p, ts_, wqkv[0:D, :], WTa, [], vbr[0])
    groups = [[0, 1, 2, 3], [4, 5, 6, 7]]
    cc1 = p.add("gpsimd", lambda g: g.collective_compute(
        "AllGather", ALU.bypass, replica_groups=groups,
        ins=[kv_in.ap().opt()], outs=[kv_g.ap().opt()]),
        [kst_ops[-1], vst_ops[-1]])
    q_ops = emit_proj_fm(p, ts_, WTa, xlnT, QT, TQ, wqT, xln_dep, None, vbr[1])
    rq = emit_normsq_half(p, ts_, QT, TQ, indm, 0, [q_ops[-1]], None)
    nb_ev = emit_normbound_finish(p, ts_, ones1, rq, rk, 0.125)

    esF1.close()
    kt_sb = [sb("g_kt0", [128, 8192], BF16), sb("g_kt1", [128, 8192], BF16)]
    vp_sb0 = sb("g_vp0", [128, 64, 130], BF16)
    vp_sb = [vp_sb0, vp_sb0]
    yT_all = sb("g_yT", [128, 8, TQ], BF16)
    pT_sb = [sb("g_pt0", [128, 1024], BF16), sb("g_pt1", [128, 1024], BF16)]
    lrow = sb("g_lrow", [1, 512], F32)
    ltok = sb("g_ltok", [128, 16, 16], F32)
    linv_all = ltok

    negmh = ts_["negmh"]
    kv_depK = [nb_ev, cc1]
    kt_ld = {}
    kt_reader = [None, None]
    vp_reader = [None, None]

    def ktg_view(r, pair):
        return kv_g[r:r + 1, 0:KTN].rearrange("a (f j) -> (a f) j", j=TQ)[
            pair * 128:(pair + 1) * 128, :]

    def vg_view(r, pair):
        return kv_g[r:r + 1, KTN:].rearrange("a (j c) -> (a j) c", c=1040)[
            :, 2 * pair * 65:(2 * pair + 2) * 65].rearrange("(jt p) c -> p jt c", p=128)

    def load_pair(pair):
        b = pair % 2
        ls = []
        for r in range(4):
            ls.append(p.g(lambda g_, pair=pair, b=b, r=r: g_.dma_start(
                kt_sb[b][:, r * 2048:(r + 1) * 2048], ktg_view(r, pair)),
                kv_depK + [kt_reader[b]] if r == 0 else []))
        for r in range(4):
            ls.append(p.g(lambda g_, pair=pair, b=b, r=r: g_.dma_start(
                vp_sb[b][:, r * 16:(r + 1) * 16, :], vg_view(r, pair)),
                [cc1, vp_reader[b]] if r == 0 else []))
        kt_ld[pair] = ls

    load_pair(0)

    s_reader = [nb_ev, nb_ev]
    psb_reader = [None, None]
    y_reader = [None, None]
    ltr_reader = nb_ev
    lrow_reader = None
    first_smm_dep = [q_ops[-1]]
    pair_last = {}
    for pair in range(8):
        b = pair % 2
        dWk = kt_ld[pair][0:4]
        dWv = kt_ld[pair][4:8]
        pv_last = [None, None]
        for qh in range(4):
            q0 = qh * 512
            pend = None
            for k in range(64):
                par = k % 2
                smm = None
                for hh in range(2):
                    smm = p.t(lambda t_, b=b, jt=k, hh=hh, q0=q0, par=par, pair=pair: t_.matmul(
                        s2_ps[par][:, hh * 512:(hh + 1) * 512],
                        kt_sb[b][hh * 64:(hh + 1) * 64, jt * 128:(jt + 1) * 128],
                        QT[hh * 64:(hh + 1) * 64, pair, q0:q0 + 512],
                        start=True, stop=True),
                        ((dWk + [s_reader[par]] + first_smm_dep) if hh == 0 else []))
                expos = []
                for hh in range(2):
                    h_ = 2 * pair + hh
                    expos.append(p.s(lambda s, h_=h_, par=par, hh=hh: s.activation(
                        pT_sb[par][:, hh * 512:(hh + 1) * 512],
                        s2_ps[par][:, hh * 512:(hh + 1) * 512], AF.Exp, scale=0.125,
                        bias=negmh[:, h_:h_ + 1]),
                        [smm, nb_ev, psb_reader[par]] if hh == 0 else [smm]))
                s_reader[par] = expos[-1]
                first_smm_dep = []
                if pend is not None:
                    e1, k0, par0 = pend
                    pv = None
                    for hh in range(2):
                        pv = p.t(lambda t_, b=b, jt0=k0, hh=hh, par0=par0: t_.matmul(
                            y_ps[hh][0:65, :], vp_sb[b][:, jt0, hh * 65:(hh + 1) * 65],
                            pT_sb[par0][:, hh * 512:(hh + 1) * 512],
                            start=(jt0 == 0), stop=(jt0 == 63)),
                            ([e1, y_reader[hh]] + dWv if k0 == 0 else ([e1] if hh == 0 else [])))
                        pv_last[hh] = pv
                    psb_reader[par0] = pv
                pend = (expos[-1], k, par)
            e1, k0, par0 = pend
            for hh in range(2):
                pv = p.t(lambda t_, b=b, jt0=k0, hh=hh, par0=par0: t_.matmul(
                    y_ps[hh][0:65, :], vp_sb[b][:, jt0, hh * 65:(hh + 1) * 65],
                    pT_sb[par0][:, hh * 512:(hh + 1) * 512],
                    start=(jt0 == 0), stop=(jt0 == 63)),
                    [e1] if hh == 0 else [])
                pv_last[hh] = pv
                psb_reader[par0] = pv
            for hh in range(2):
                h = 2 * pair + hh
                ye = p.v(lambda v, pair=pair, hh=hh, q0=q0: v.tensor_copy(
                    yT_all[hh * 64:(hh + 1) * 64, pair, q0:q0 + 512],
                    y_ps[hh][0:64, :]), [pv_last[1]])
                le = p.v(lambda v, hh=hh: v.tensor_copy(
                    lrow[0:1, :], y_ps[hh][64:65, :]), [pv_last[1], lrow_reader])
                trs = []
                for i in range(4):
                    trs.append(p.t(lambda t_, i=i: t_.transpose(
                        ltr_ps[0:128, i:i + 1], lrow[0:1, i * 128:(i + 1) * 128],
                        identf[0:1, 0:1]), [le] + ([ltr_reader] if i == 0 else [])))
                t0 = q0 // 128
                lte = p.v(lambda v, t0=t0, h=h: v.tensor_copy(
                    ltok[:, t0:t0 + 4, h:h + 1].rearrange("p a b -> p (a b)"),
                    ltr_ps[0:128, 0:4]), [trs[-1]])
                ltr_reader = lte
                lrow_reader = trs[-1]
                y_reader[hh] = le
                s_reader[1] = lte
        if pair + 1 < 8:
            kt_reader[(pair + 1) % 2] = pair_last.get(pair - 1)
            vp_reader[(pair + 1) % 2] = pv_last[1]
            load_pair(pair + 1)
        pair_last[pair] = pv_last[1]

    rc = p.v(lambda v: v.reciprocal(ltok[:, :, :].rearrange("p a b -> p (a b)"),
                                    ltok[:, :, :].rearrange("p a b -> p (a b)")), [ltr_reader])

    ytr_reader = None
    last_ad = None
    for t in range(16):
        for pc in range(8):
            tr = p.t(lambda t_, t=t, pc=pc: t_.transpose(
                tr_ps[:, pc, :], yT_all[:, pc, t * 128:(t + 1) * 128], ident[:]),
                [rc if (t == 0 and pc == 0) else None, ytr_reader])
            for hh in range(2):
                h = 2 * pc + hh
                ad = p.v(lambda v, t=t, pc=pc, hh=hh, h=h: v.scalar_tensor_tensor(
                    x_own[:, t, h * 64:(h + 1) * 64], tr_ps[:, pc, hh * 64:(hh + 1) * 64],
                    linv_all[:, t, h:h + 1], x_own[:, t, h * 64:(h + 1) * 64],
                    ALU.mult, ALU.add), [tr, rc])
                ytr_reader = ad
                last_ad = ad
    esF.close()
    return last_ad


def build_fused(phases="BCDEFG"):
    TQ, TK, NTK = 2048, 2304, 18
    nc = bass.Bass("TRN2", target_bir_lowering=False)
    xin = nc.dram_tensor("xin", [TK, D], F32, kind="ExternalInput")
    wq = nc.dram_tensor("wq", [D, D], F32, kind="ExternalInput")
    wk = nc.dram_tensor("wk", [D, D], F32, kind="ExternalInput")
    wv = nc.dram_tensor("wv", [D, D], F32, kind="ExternalInput")
    wo = nc.dram_tensor("wo", [D, D], F32, kind="ExternalInput")
    rw0 = nc.dram_tensor("rw0", [D, 8], F32, kind="ExternalInput")
    up7 = nc.dram_tensor("up7", [7, D, 4096], F32, kind="ExternalInput")
    gate7 = nc.dram_tensor("gate7", [7, D, 4096], F32, kind="ExternalInput")
    down7 = nc.dram_tensor("down7", [7, 4096, D], F32, kind="ExternalInput")
    sup = nc.dram_tensor("sup", [D, 4096], F32, kind="ExternalInput")
    sgate = nc.dram_tensor("sgate", [D, 4096], F32, kind="ExternalInput")
    sdown = nc.dram_tensor("sdown", [4096, D], F32, kind="ExternalInput")
    wqkv = nc.dram_tensor("wqkv", [3 * D, D], F32, kind="ExternalInput")
    rw1 = nc.dram_tensor("rw1", [D, 8], F32, kind="ExternalInput")
    w1 = nc.dram_tensor("w1", [8, D, 2048], F32, kind="ExternalInput")
    w2 = nc.dram_tensor("w2", [8, D, 2048], F32, kind="ExternalInput")
    w3 = nc.dram_tensor("w3", [8, 2048, D], F32, kind="ExternalInput")
    identin = nc.dram_tensor("identin", [128, 128], BF16, kind="ExternalInput")
    identfin = nc.dram_tensor("identfin", [128, 128], F32, kind="ExternalInput")
    permin = nc.dram_tensor("permin", [128, 128], BF16, kind="ExternalInput")
    cosin = nc.dram_tensor("cosin", [128, TK], BF16, kind="ExternalInput")
    sinin = nc.dram_tensor("sinin", [128, TK], BF16, kind="ExternalInput")
    maskin = nc.dram_tensor("maskin", [128, 3, 384], F32, kind="ExternalInput")
    indmin = nc.dram_tensor("indmin", [128, 8, 16], BF16, kind="ExternalInput")
    onesin = nc.dram_tensor("onesin", [1, 128], F32, kind="ExternalInput")
    xout = nc.dram_tensor("xout", [TQ, D], F32, kind="ExternalOutput")
    KTN = D * TQ
    VN = TQ * 16 * 65
    kv_in = nc.dram_tensor("kv_in", [1, KTN + VN], BF16, kind="Internal")
    kv_g = nc.dram_tensor("kv_g", [4, KTN + VN], BF16, kind="Internal")

    es = ExitStack()
    sb = lambda name, shape, dt: es.enter_context(nc.sbuf_tensor(name, shape, dt))
    x_sb = sb("x", [128, NTK, D], F32)
    ident = sb("ident", [128, 128], BF16)
    identf = sb("identf", [128, 128], F32)
    indm = sb("indm", [128, 8, 16], BF16)
    ones1 = sb("ones1", [1, 128], F32)
    x_own = x_sb[:, 2:18, :]

    with nc.Block() as block:
        p = Plan(nc, es)
        ld_id = p.sy(lambda s: s.dma_start(ident[:], identin[:, :]))
        ld_if = p.sy(lambda s: s.dma_start(identf[:], identfin[:, :]))
        ld_im = p.sy(lambda s: s.dma_start(indm[:, :, :], indmin[:, :, :]))
        ld_on = p.sy(lambda s: s.dma_start(ones1[:], onesin[:, :]))
        ld_x = p.sy(lambda s: s.dma_start(x_sb[:, :, :], xin.rearrange("(t p) d -> p t d", p=128)))

        def wsrc0():
            def f(e, h8):
                hs = slice(h8 * 512, (h8 + 1) * 512)
                if e < 7:
                    return (up7[e, :, hs].rearrange("(c p) h -> p c h", p=128),
                            gate7[e, :, hs].rearrange("(c p) h -> p c h", p=128),
                            down7[e, hs, :].rearrange("(c p) d -> p c d", p=128))
                return (sup[:, hs].rearrange("(c p) h -> p c h", p=128),
                        sgate[:, hs].rearrange("(c p) h -> p c h", p=128),
                        sdown[hs, :].rearrange("(c p) d -> p c d", p=128))
            return f

        def wsrc1():
            def f(e, h8):
                hs = slice(h8 * 512, (h8 + 1) * 512)
                return (w2[e, :, hs].rearrange("(c p) h -> p c h", p=128),
                        w1[e, :, hs].rearrange("(c p) h -> p c h", p=128),
                        w3[e, hs, :].rearrange("(c p) d -> p c d", p=128))
            return f

        scope_mk = ExitStack
        end = ld_x
        if "B" in phases:
            end = _emit_ffn_phase(p, nc, scope_mk, x_sb, NTK, TK, 0, wsrc0, rw0,
                                  ident, [end, ld_id])
        if "C" in phases:
            end = _emit_attn0_phase(p, nc, None, x_sb, wq, wk, wv, wo,
                                    (permin, cosin, sinin, maskin), ident, identf,
                                    indm, ones1, [end, ld_im, ld_on, ld_id])
        if "D" in phases:
            end = _emit_ffn_phase(p, nc, scope_mk, x_own, 16, TQ, 0, wsrc0, rw0,
                                  ident, [end, ld_id])
        if "E" in phases:
            end = _emit_ffn_phase(p, nc, scope_mk, x_own, 16, TQ, 1, wsrc1, rw1,
                                  ident, [end, ld_id])
        if "F" in phases:
            end = _emit_attn1_phase(p, nc, x_own, wqkv, ident, identf, indm, ones1,
                                    kv_in, kv_g, KTN, [end, ld_if, ld_id, ld_im, ld_on])
        if "G" in phases:
            end = _emit_ffn_phase(p, nc, scope_mk, x_own, 16, TQ, 1, wsrc1, rw1,
                                  ident, [end, ld_id])
        st = p.sy(lambda s: s.dma_start(xout.rearrange("(t p) d -> p t d", p=128),
                                        x_own), [end])
        p.sy(lambda s: s.nop(), [st], dma=False)
        p.finalize_and_emit(block)
    es.close()
    return nc


def kernel_fused(L0_attn_x, L0_attn_Wq, L0_attn_Wk, L0_attn_Wv, L0_attn_Wo,
                 L0_ffn_router_w, L0_ffn_up_w, L0_ffn_gate_w, L0_ffn_down_w,
                 L0_ffn_shared_up_w, L0_ffn_shared_gate_w, L0_ffn_shared_down_w,
                 L1_attn_W_qkv, L1_ffn_Wg, L1_ffn_W1, L1_ffn_W2, L1_ffn_W3):
    x = np.asarray(L0_attn_x)
    nc = _get_nc("fused", build_fused)
    in_maps = []
    for c in range(NCORES):
        b, q = c // 4, c % 4
        own = x[b, q * 2048:(q + 1) * 2048]
        if q == 0:
            halo = np.zeros((256, D), dtype=np.float32)
        else:
            halo = x[b, q * 2048 - 256:q * 2048]
        xin = np.ascontiguousarray(np.concatenate([halo, own], 0))
        pos = np.maximum(q * 2048 - 256 + np.arange(2304), 0)
        cosT, sinT = _rope_tables(pos)
        in_maps.append(dict(
            xin=xin, wq=np.asarray(L0_attn_Wq), wk=np.asarray(L0_attn_Wk),
            wv=np.asarray(L0_attn_Wv), wo=np.asarray(L0_attn_Wo),
            rw0=np.asarray(L0_ffn_router_w), up7=np.asarray(L0_ffn_up_w),
            gate7=np.asarray(L0_ffn_gate_w), down7=np.asarray(L0_ffn_down_w),
            sup=np.asarray(L0_ffn_shared_up_w), sgate=np.asarray(L0_ffn_shared_gate_w),
            sdown=np.asarray(L0_ffn_shared_down_w), wqkv=np.asarray(L1_attn_W_qkv),
            rw1=np.asarray(L1_ffn_Wg), w1=np.asarray(L1_ffn_W1),
            w2=np.asarray(L1_ffn_W2), w3=np.asarray(L1_ffn_W3),
            identin=_IDENT, identfin=_IDENTF, permin=_PERM, cosin=cosT, sinin=sinT,
            maskin=_l0_masks(q == 0), indmin=_INDM, onesin=_ONES1))
    res = run_bass_kernel_spmd(nc, in_maps, core_ids=list(range(NCORES)))
    out = np.stack([res.results[c]["xout"] for c in range(NCORES)])
    return out.reshape(2, 8192, D)
